# version 07
# baseline (speedup 1.0000x reference)
import math
import numpy as np
import concourse.bass as bass
import concourse.mybir as mybir
from concourse.bass_utils import run_bass_kernel_spmd

F32 = mybir.dt.float32
BF16 = mybir.dt.bfloat16
I32 = mybir.dt.int32
ALU = mybir.AluOpType
AF = mybir.ActivationFunctionType
AX = mybir.AxisListType

D = 1024
T = 2048
NH = 16
HD = 64
NKC = 8
NTT = 4
NLC = 16
VW = 66
QSCALE = 1.0 / math.sqrt(math.sqrt(HD))

ENGS = ("sync", "scalar", "vector", "tensor", "gpsimd")
ESEM = {"sync": (None, "ld"), "scalar": ("s", "sd"), "vector": ("v", None),
        "tensor": ("t", None), "gpsimd": ("g", "gd")}


class Buf:
    __slots__ = ("w", "rs")

    def __init__(self):
        self.w = None
        self.rs = {}

    def war(self):
        out = list(self.rs.items())
        if self.w is not None:
            out.append(self.w)
        return out


class KB:

    def __init__(self):
        self.code = {e: [] for e in ENGS}
        self.cnt = {"t": 0, "v": 0, "s": 0, "g": 0, "ld": 0, "sd": 0, "gd": 0}
        self.step = {"t": 1, "v": 1, "s": 1, "g": 1, "ld": 16, "sd": 16, "gd": 16}

    def op(self, eng, fn, reads=(), writes=(), dma=False, waits=()):
        sem = ESEM[eng][1 if dma else 0]
        wl = []
        for b in reads:
            if b.w is not None:
                wl.append(b.w)
        for b in writes:
            wl.extend(b.war())
        wl.extend(waits)
        self.cnt[sem] += self.step[sem]
        tok = (sem, self.cnt[sem])
        self.code[eng].append((tuple(wl), fn, sem, self.step[sem]))
        for b in writes:
            b.w = tok
            b.rs = {}
        for b in reads:
            s, vv = tok
            if b.rs.get(s, 0) < vv:
                b.rs[s] = vv
        return tok

    def realize(self, block, sems):
        for eng in ENGS:
            code = self.code[eng]

            def emit(e, code=code):
                waited = {}
                for wl, fn, sem, stp in code:
                    for (s, v) in wl:
                        if waited.get(s, 0) < v:
                            e.wait_ge(sems[s], v)
                            waited[s] = v
                    fn(e).then_inc(sems[sem], stp)
            getattr(block, eng)(emit)


class Arena:

    def __init__(self, t):
        self.t = t
        self.tenants = []

    def take(self, start, n, parts=1):
        end = start + n
        pend = {}
        for (s0, e0, bl) in self.tenants:
            if s0 < end and start < e0:
                for b in bl:
                    for (sm, vv) in b.war():
                        if pend.get(sm, 0) < vv:
                            pend[sm] = vv
        bufs = []
        for _ in range(parts):
            b = Buf()
            b.rs = dict(pend)
            bufs.append(b)
        self.tenants.append((start, end, bufs))
        if parts == 1:
            return start, bufs[0]
        return start, bufs


def build_program(debug=False):
    nc = bass.Bass("TRN2", target_bir_lowering=False)

    xT = nc.dram_tensor("xT", [D, T], F32, kind="ExternalInput").ap()
    wqaT = nc.dram_tensor("wqaT", [D, D], F32, kind="ExternalInput").ap()
    wkaT = nc.dram_tensor("wkaT", [D, D], F32, kind="ExternalInput").ap()
    wvaT = nc.dram_tensor("wvaT", [D, D], F32, kind="ExternalInput").ap()
    bqa = nc.dram_tensor("bqa", [128, 8], F32, kind="ExternalInput").ap()
    bka = nc.dram_tensor("bka", [128, 8], F32, kind="ExternalInput").ap()
    bvaf = nc.dram_tensor("bvaf", [1, D], F32, kind="ExternalInput").ap()
    wffn = nc.dram_tensor("wffn", [D, D], F32, kind="ExternalInput").ap()
    gffn = nc.dram_tensor("gffn", [128, 8], F32, kind="ExternalInput").ap()
    bffn = nc.dram_tensor("bffn", [128, 8], F32, kind="ExternalInput").ap()
    wqbT = nc.dram_tensor("wqbT", [D, D], F32, kind="ExternalInput").ap()
    wkbT = nc.dram_tensor("wkbT", [D, D], F32, kind="ExternalInput").ap()
    wvbT = nc.dram_tensor("wvbT", [D, D], F32, kind="ExternalInput").ap()
    wobT = nc.dram_tensor("wobT", [D, D], F32, kind="ExternalInput").ap()
    flagT = nc.dram_tensor("flagT", [128, 1], F32, kind="ExternalInput").ap()
    outT = nc.dram_tensor("outT", [D, T], F32, kind="ExternalOutput").ap()

    dQ = nc.dram_tensor("dQ", [D, T], F32)
    dK = nc.dram_tensor("dK", [D, T], F32)
    dV = nc.dram_tensor("dV", [T, NH * VW], F32)
    dQb = nc.dram_tensor("dQb", [D, T], F32)
    dKb = nc.dram_tensor("dKb", [D, T], F32)
    dVb = nc.dram_tensor("dVb", [T, NH * VW], F32)
    dWq = nc.dram_tensor("dWq", [D, D], BF16)
    dRow = nc.dram_tensor("dRow", [2, T], F32)
    dRow2 = nc.dram_tensor("dRow2", [2, T], F32)
    dD1 = nc.dram_tensor("dD1", [NH, T], F32)
    dDb = nc.dram_tensor("dDb", [NH, T], F32)
    dTmp = nc.dram_tensor("dTmp", [64, 32], F32)
    cwi = nc.dram_tensor("cwi", [8, 1], F32)
    cwo = nc.dram_tensor("cwo", [8, 1], F32)
    cli = nc.dram_tensor("cli", [2 * D, 64], F32)
    clo = nc.dram_tensor("clo", [4 * D, 64], F32)
    c2i = nc.dram_tensor("c2i", [1, 1], F32)
    c2o = nc.dram_tensor("c2o", [1, 1], F32)
    c3i = nc.dram_tensor("c3i", [NH * 128, VW], F32)
    c3o = nc.dram_tensor("c3o", [NH * 128, VW], F32)
    cgi = nc.dram_tensor("cgi", [1, 1], F32)
    cgo = nc.dram_tensor("cgo", [1, 1], F32)
    cg2i = nc.dram_tensor("cg2i", [1, 1], F32)
    cg2o = nc.dram_tensor("cg2o", [1, 1], F32)
    BND = D * 128 + 128 * NH * VW
    cbi = nc.dram_tensor("cbi", [1, BND], F32)
    cbo = nc.dram_tensor("cbo", [2, BND], F32)
    if debug:
        dbgQ = nc.dram_tensor("dbgQ", [D, T], F32, kind="ExternalOutput").ap()
        dbgK = nc.dram_tensor("dbgK", [D, T], F32, kind="ExternalOutput").ap()
        dbgV = nc.dram_tensor("dbgV", [T, NH * VW], F32, kind="ExternalOutput").ap()
        dbgLM = nc.dram_tensor("dbgLM", [2, 1024], F32, kind="ExternalOutput").ap()
        dbgP2 = nc.dram_tensor("dbgP2", [128, NH * 128], F32, kind="ExternalOutput").ap()
        dbgVI = nc.dram_tensor("dbgVI", [128, NH * 128], F32, kind="ExternalOutput").ap()
        dbgG0 = nc.dram_tensor("dbgG0", [128, NH * VW], F32, kind="ExternalOutput").ap()
        dbgGG = nc.dram_tensor("dbgGG", [128, NH * 64], F32, kind="ExternalOutput").ap()
        dbgD1 = nc.dram_tensor("dbgD1", [NH, T], F32, kind="ExternalOutput").ap()
        dbgX1 = nc.dram_tensor("dbgX1", [D, T], F32, kind="ExternalOutput").ap()
        dbgX2 = nc.dram_tensor("dbgX2", [D, T], F32, kind="ExternalOutput").ap()
        dbgX3 = nc.dram_tensor("dbgX3", [D, T], F32, kind="ExternalOutput").ap()
        dbgSC = nc.dram_tensor("dbgSC", [128, 16], F32, kind="ExternalOutput").ap()
        dbgQb = nc.dram_tensor("dbgQb", [D, T], F32, kind="ExternalOutput").ap()
        dbgKb = nc.dram_tensor("dbgKb", [D, T], F32, kind="ExternalOutput").ap()
        dbgVb = nc.dram_tensor("dbgVb", [T, NH * VW], F32, kind="ExternalOutput").ap()
        dbgDb = nc.dram_tensor("dbgDb", [NH, T], F32, kind="ExternalOutput").ap()
        dbgBND = nc.dram_tensor("dbgBND", [2, BND], F32, kind="ExternalOutput").ap()
        dbgG0p = nc.dram_tensor("dbgG0p", [128, NH * VW], F32, kind="ExternalOutput").ap()
        dbgV1 = nc.dram_tensor("dbgV1", [T, NH * VW], F32, kind="ExternalOutput").ap()
        dbgV2 = nc.dram_tensor("dbgV2", [T, NH * VW], F32, kind="ExternalOutput").ap()
        dbgIT = nc.dram_tensor("dbgIT", [6, 128, NH * 128], F32, kind="ExternalOutput").ap()
        dbgYT = nc.dram_tensor("dbgYT", [6, 128, NH * 128], F32, kind="ExternalOutput").ap()
        dbgE3 = nc.dram_tensor("dbgE3", [128, NH * 128], F32, kind="ExternalOutput").ap()


    A = lambda name, shp, dt=F32: nc.alloc_sbuf_tensor(name, shp, dt)
    X = [A(f"X{i}", [128, T]) for i in range(NKC)]
    TMP = A("TMP", [128, 512])
    BCM = A("BCM", [128, 512])
    BCR = A("BCR", [128, 512])
    IDN = A("IDN", [128, 128])
    IOI = A("IOI", [128, 128], I32)
    IOF = A("IOF", [128, 128])
    IOP = A("IOP", [128, 1], I32)
    IOPF = A("IOPF", [128, 1])
    MSKC = A("MSKC", [128, 128])
    MSKP = A("MSKP", [128, 128])
    MSKB = A("MSKB", [128, 128])
    FLG = A("FLG", [128, 1])
    SC1 = A("SC1", [128, 16])
    BB = A("BB", [128, 16])
    SS = A("SS", [128, 64])
    SS2 = A("SS2", [64, 128])
    ST = A("ST", [64, 128])
    SCR = A("SCR", [1, 128])
    BQA = A("BQA", [128, 8])
    BKA = A("BKA", [128, 8])
    GFF = A("GFF", [128, 8])
    BFF = A("BFF", [128, 8])
    ARN = A("ARN", [128, 33800])
    ar = Arena(ARN)

    ps = [nc.alloc_psum_tensor(f"ps{i}", [128, 512], F32) for i in range(8)]
    bps = [Buf() for _ in range(8)]

    kb = KB()
    op = kb.op

    bX = [Buf() for _ in range(NKC)]
    bTMP, bBCM, bBCR = Buf(), Buf(), Buf()
    bIDN, bIOI, bIOF, bIOP, bIOPF = Buf(), Buf(), Buf(), Buf(), Buf()
    bMSKC, bMSKP, bMSKB, bFLG = Buf(), Buf(), Buf(), Buf()
    bSC1, bBB, bSS, bSS2, bST, bSCR = Buf(), Buf(), Buf(), Buf(), Buf(), Buf()
    bBQA, bBKA, bGFF, bBFF = Buf(), Buf(), Buf(), Buf()
    bdQ, bdK, bdV, bdQb, bdKb, bdVb, bdWq = (Buf(), Buf(), Buf(), Buf(), Buf(),
                                             Buf(), Buf())
    bdRow, bdRow2, bdD1, bdDb, bdTmp = Buf(), Buf(), Buf(), Buf(), Buf()
    bcw, bcl, bc2, bc3, bcg, bcg2, bcb = (Buf(), Buf(), Buf(), Buf(), Buf(),
                                          Buf(), Buf())

    def AR(off, n):
        return ARN[:, off:off + n]

    op("sync", lambda e: e.dma_start(out=FLG[:, :], in_=flagT), writes=[bFLG], dma=True)
    op("gpsimd", lambda e: e.dma_start(out=cwi.ap(), in_=flagT[0:8, 0:1]),
       reads=[bFLG], writes=[bcw], dma=True)
    op("gpsimd", lambda e: e.collective_compute(
        "AllReduce", ALU.max, replica_groups=[list(range(8))],
        ins=[cwi.ap().opt()], outs=[cwo.ap().opt()]), reads=[bcw], writes=[bcw])
    for i in range(NKC):
        op("sync", lambda e, i=i: e.dma_start(out=X[i][:, :], in_=xT[128 * i:128 * (i + 1), :]),
           writes=[bX[i]], dma=True)
    op("sync", lambda e: e.dma_start(out=BQA[:, :], in_=bqa), writes=[bBQA], dma=True)
    op("sync", lambda e: e.dma_start(out=BKA[:, :], in_=bka), writes=[bBKA], dma=True)
    op("sync", lambda e: e.dma_start(out=GFF[:, :], in_=gffn), writes=[bGFF], dma=True)
    op("sync", lambda e: e.dma_start(out=BFF[:, :], in_=bffn), writes=[bBFF], dma=True)
    op("gpsimd", lambda e: e.iota(IOI[:, :], [[1, 128]], base=0, channel_multiplier=0),
       writes=[bIOI])
    op("gpsimd", lambda e: e.iota(IOP[:, :], [[0, 1]], base=0, channel_multiplier=1,
                                  allow_small_or_imprecise_dtypes=True), writes=[bIOP])
    op("vector", lambda e: e.tensor_copy(IOF[:, :], IOI[:, :]), reads=[bIOI], writes=[bIOF])
    op("vector", lambda e: e.tensor_copy(IOPF[:, :], IOP[:, :]), reads=[bIOP], writes=[bIOPF])
    op("vector", lambda e: e.tensor_scalar(IDN[:, :], IOF[:, :], IOPF[:, 0:1], None,
                                           ALU.is_equal), reads=[bIOF, bIOPF], writes=[bIDN])
    op("vector", lambda e: e.tensor_scalar(MSKC[:, :], IOF[:, :], IOPF[:, 0:1], None,
                                           ALU.is_ge), reads=[bIOF, bIOPF], writes=[bMSKC])
    op("vector", lambda e: e.tensor_scalar(MSKP[:, :], IOF[:, :], IOPF[:, 0:1], None,
                                           ALU.is_lt), reads=[bIOF, bIOPF], writes=[bMSKP])
    op("vector", lambda e: e.tensor_scalar(MSKB[:, :], MSKP[:, :], FLG[:, 0:1], None,
                                           ALU.mult), reads=[bMSKP, bFLG], writes=[bMSKB])

    def wsplit_load(oW, bW, hbm, tmpo, btmp):
        for k in range(NKC):
            wv = AR(oW + 1024 * k, 1024).bitcast(BF16)
            tb = btmp[k % 2]
            op("gpsimd", lambda e, k=k, wv=wv, hbm=hbm: e.dma_start(
                out=wv[:, 0:1024], in_=hbm[128 * k:128 * (k + 1), :]),
               writes=[bW[k]], dma=True)
            op("sync", lambda e, k=k, tmpo=tmpo: e.dma_start(
                out=AR(tmpo + 1024 * (k % 2), 1024), in_=hbm[128 * k:128 * (k + 1), :]),
               reads=[], writes=[tb], dma=True)
            op("vector", lambda e, k=k, wv=wv, tmpo=tmpo: e.scalar_tensor_tensor(
                wv[:, 1024:2048], AR(tmpo + 1024 * (k % 2), 1024), 1.0, wv[:, 0:1024],
                ALU.mult, ALU.subtract), reads=[tb], writes=[bW[k]])

    def mm3(eng_ps, bp, lh, ll, rh, rl, first, last, rbufs):
        for i2, (a_, b_) in enumerate(((lh, rh), (lh, rl), (ll, rh))):
            op("tensor", lambda e, a_=a_, b_=b_, i2=i2: e.matmul(
                eng_ps, a_, b_, start=(first and i2 == 0), stop=(last and i2 == 2)),
               reads=rbufs, writes=[bp])


    def ln_stats(tt, src, src_bufs, rowd, browd, kind):
        for i in range(NKC):
            op("scalar", lambda e, i=i: e.activation(TMP[:, :], src(i), AF.Square),
               reads=[src_bufs[i]], writes=[bTMP])
            o0 = 0 if i == 0 else 32
            op("vector", lambda e, i=i, o0=o0: e.tensor_reduce(
                SS[:, o0:o0 + 16], src(i).rearrange("p (b x) -> p b x", x=32),
                AX.X, ALU.add, apply_transpose=True), reads=[src_bufs[i]], writes=[bSS])
            op("vector", lambda e, o0=o0: e.tensor_reduce(
                SS[:, o0 + 16:o0 + 32], TMP[:, :].rearrange("p (b x) -> p b x", x=32),
                AX.X, ALU.add, apply_transpose=True), reads=[bTMP], writes=[bSS])
            if i > 0:
                op("vector", lambda e: e.tensor_tensor(SS[:, 0:32], SS[:, 0:32],
                                                       SS[:, 32:64], ALU.add),
                   reads=[bSS], writes=[bSS])
        op("tensor", lambda e: e.transpose(ps[7][0:16, 0:128], SS[:, 0:16], IDN[:, :]),
           reads=[bSS, bIDN], writes=[bps[7]])
        op("tensor", lambda e: e.transpose(ps[7][0:16, 128:256], SS[:, 16:32],
                                           IDN[:, :]), reads=[bSS, bIDN],
           writes=[bps[7]])
        op("vector", lambda e: e.tensor_copy(SS2[0:16, 0:32], ps[7][0:16, 0:32]),
           reads=[bps[7]], writes=[bSS2])
        op("vector", lambda e: e.tensor_copy(SS2[0:16, 32:64], ps[7][0:16, 128:160]),
           reads=[bps[7]], writes=[bSS2])
        for g in range(1, 4):
            op("vector", lambda e, g=g: e.tensor_tensor(
                SS2[0:16, 0:32], SS2[0:16, 0:32], ps[7][0:16, 32 * g:32 * (g + 1)],
                ALU.add), reads=[bps[7]], writes=[bSS2])
            op("vector", lambda e, g=g: e.tensor_tensor(
                SS2[0:16, 32:64], SS2[0:16, 32:64],
                ps[7][0:16, 128 + 32 * g:128 + 32 * (g + 1)],
                ALU.add), reads=[bps[7]], writes=[bSS2])
        op("vector", lambda e: e.tensor_scalar(ST[0:16, 0:32], SS2[0:16, 0:32],
                                               1.0 / D, None, ALU.mult),
           reads=[bSS2], writes=[bST])
        op("scalar", lambda e: e.activation(ST[0:16, 64:96], ST[0:16, 0:32], AF.Square),
           reads=[bST], writes=[bST])
        op("vector", lambda e: e.scalar_tensor_tensor(
            ST[0:16, 96:128], SS2[0:16, 32:64], 1.0 / D, ST[0:16, 64:96],
            ALU.mult, ALU.subtract), reads=[bSS2, bST], writes=[bST])
        if kind == "plain":
            op("vector", lambda e: e.tensor_scalar(ST[0:16, 64:96], ST[0:16, 96:128],
                                                   1.0, 1e-5, ALU.mult, ALU.add),
               reads=[bST], writes=[bST])
        else:
            op("vector", lambda e: e.tensor_scalar(ST[0:16, 64:96], ST[0:16, 96:128],
                                                   1.001, 1e-8, ALU.mult, ALU.add),
               reads=[bST], writes=[bST])
        op("scalar", lambda e: e.activation(ST[0:16, 96:128], ST[0:16, 64:96], AF.Sqrt),
           reads=[bST], writes=[bST])
        op("vector", lambda e: e.reciprocal(ST[0:16, 64:96], ST[0:16, 96:128]),
           reads=[bST], writes=[bST])
        rb = rowd.ap()
        op("gpsimd", lambda e: e.dma_start(
            out=rb[0:1, 512 * tt:512 * (tt + 1)].rearrange("o (b i) -> (o b) i", b=16),
            in_=ST[0:16, 0:32]), reads=[bST], writes=[browd], dma=True)
        op("gpsimd", lambda e: e.dma_start(
            out=rb[1:2, 512 * tt:512 * (tt + 1)].rearrange("o (b i) -> (o b) i", b=16),
            in_=ST[0:16, 64:96]), reads=[bST], writes=[browd], dma=True)
        op("gpsimd", lambda e: e.dma_start(
            out=BCM[:, :], in_=rb[0:1, 512 * tt:512 * (tt + 1)].broadcast_to([128, 512])),
           reads=[browd], writes=[bBCM], dma=True)
        op("gpsimd", lambda e: e.dma_start(
            out=BCR[:, :], in_=rb[1:2, 512 * tt:512 * (tt + 1)].broadcast_to([128, 512])),
           reads=[browd], writes=[bBCR], dma=True)

    def xn_chunk(dst, bdst, i, tt, affine):
        op("vector", lambda e, i=i: e.tensor_tensor(
            TMP[:, :], X[i][:, 512 * tt:512 * (tt + 1)], BCM[:, :], ALU.subtract),
           reads=[bX[i], bBCM], writes=[bTMP])
        if not affine:
            op("vector", lambda e: e.tensor_tensor(dst, TMP[:, :], BCR[:, :], ALU.mult),
               reads=[bTMP, bBCR], writes=[bdst])
        else:
            op("vector", lambda e: e.tensor_tensor(TMP[:, :], TMP[:, :], BCR[:, :],
                                                   ALU.mult), reads=[bTMP, bBCR],
               writes=[bTMP])
            op("vector", lambda e, i=i: e.tensor_scalar(
                dst, TMP[:, :], GFF[:, i:i + 1], BFF[:, i:i + 1], ALU.mult, ALU.add),
               reads=[bTMP, bGFF, bBFF], writes=[bdst])

    def xn_chunk_split(oslot, bdst, i, tt):
        xv = AR(oslot, 512).bitcast(BF16)
        op("vector", lambda e, i=i: e.tensor_tensor(
            TMP[:, :], X[i][:, 512 * tt:512 * (tt + 1)], BCM[:, :], ALU.subtract),
           reads=[bX[i], bBCM], writes=[bTMP])
        op("vector", lambda e: e.tensor_tensor(TMP[:, :], TMP[:, :], BCR[:, :],
                                               ALU.mult), reads=[bTMP, bBCR],
           writes=[bTMP])
        op("scalar", lambda e, xv=xv: e.activation(xv[:, 0:512], TMP[:, :], AF.Identity,
                                                   bias=0.0, scale=1.0),
           reads=[bTMP], writes=[bdst])
        op("vector", lambda e, xv=xv: e.scalar_tensor_tensor(
            xv[:, 512:1024], TMP[:, :], 1.0, xv[:, 0:512], ALU.mult, ALU.subtract),
           reads=[bTMP], writes=[bdst])

    def partition_max_to_scalar(colap, bcol, dsti, bdsti):
        op("tensor", lambda e: e.transpose(ps[7][0:1, 0:128], colap, IDN[:, :]),
           reads=[bcol, bIDN], writes=[bps[7]])
        op("vector", lambda e: e.reduce_max(SCR[0:1, 0:1], ps[7][0:1, 0:128], AX.X), reads=[bps[7]], writes=[bSCR])
        op("gpsimd", lambda e: e.dma_start(out=dsti.ap(), in_=SCR[0:1, 0:1]),
           reads=[bSCR], writes=[bdsti], dma=True)

    oWQ, bWQ = ar.take(0, 8192, NKC)
    oWK, bWK = ar.take(8192, 8192, NKC)
    oWV, bWV = ar.take(16384, 8192, NKC)
    oXN, bXN = ar.take(24576, 4096, NKC)
    oEV, bEV = ar.take(28672, 1024, 2)
    oVT, bVT = ar.take(29696, 1056)
    oQLM, bQLM = ar.take(30752, 512)
    oKLM, bKLM = ar.take(31264, 512)
    oBVB, bBVB = ar.take(31776, 1024)

    wsplit_load(oWQ, bWQ, wqaT, oXN, (bXN[0], bXN[2]))
    wsplit_load(oWK, bWK, wkaT, oXN, (bXN[0], bXN[2]))
    wsplit_load(oWV, bWV, wvaT, oXN, (bXN[0], bXN[2]))
    op("gpsimd", lambda e: e.dma_start(out=AR(oBVB, 1024),
                                       in_=bvaf.broadcast_to([128, D])),
       writes=[bBVB], dma=True)
    op("vector", lambda e: e.memset(AR(oVT, 1056), 1.0), writes=[bVT])

    for tt in range(NTT):
        ln_stats(tt, lambda i, tt=tt: X[i][:, 512 * tt:512 * (tt + 1)], bX,
                 dRow, bdRow, "plain")
        for i in range(NKC):
            xn_chunk_split(oXN + 512 * i, bXN[i], i, tt)
        for mi, (oW, bW, BI, bBI, dspill, bdspill, oLM, bLM) in enumerate((
                (oWQ, bWQ, BQA, bBQA, dQ, bdQ, oQLM, bQLM),
                (oWK, bWK, BKA, bBKA, dK, bdK, oKLM, bKLM))):
            for jc in range(NKC):
                p, bp = (0, bps[0]) if jc % 2 == 0 else (1, bps[1])
                ev, bev = (oEV + 512 * (jc % 2)), bEV[jc % 2]
                for kc in range(NKC):
                    wv = AR(oW + 1024 * kc, 1024).bitcast(BF16)
                    xv = AR(oXN + 512 * kc, 512).bitcast(BF16)
                    mm3(ps[p][:, :], bp,
                        wv[:, 128 * jc:128 * (jc + 1)],
                        wv[:, 1024 + 128 * jc:1024 + 128 * (jc + 1)],
                        xv[:, 0:512], xv[:, 512:1024],
                        kc == 0, kc == NKC - 1, [bW[kc], bXN[kc]])
                if tt == 0 and jc == 0:
                    op("vector", lambda e, BI=BI, mi=mi: e.tensor_scalar(
                        BB[:, 8 * mi:8 * mi + 8], BI[:, :], QSCALE, None, ALU.mult),
                       reads=[bBI], writes=[bBB])
                op("scalar", lambda e, p=p, ev=ev, jc=jc, mi=mi: e.activation(
                    AR(ev, 512), ps[p][:, :], AF.Identity,
                    bias=BB[:, 8 * mi + jc:8 * mi + jc + 1], scale=QSCALE),
                   reads=[bp, bBB], writes=[bev])
                op("vector", lambda e, ev=ev, jc=jc, tt=tt, oLM=oLM: e.tensor_reduce(
                    AR(oLM + 64 * jc + 16 * tt, 16),
                    AR(ev, 512).rearrange("p (s g) -> p s g", g=32), AX.X, ALU.add),
                   reads=[bev], writes=[bLM])
                op("gpsimd", lambda e, ev=ev, jc=jc, tt=tt, dspill=dspill: e.dma_start(
                    out=dspill.ap()[128 * jc:128 * (jc + 1), 512 * tt:512 * (tt + 1)],
                    in_=AR(ev, 512)), reads=[bev], writes=[bdspill], dma=True)
        for lc4 in range(4):
            for vt2 in range(2):
                p, bp = (2, bps[2]) if vt2 == 0 else (3, bps[3])
                for kc in range(NKC):
                    wv = AR(oWV + 1024 * kc, 1024).bitcast(BF16)
                    xv = AR(oXN + 512 * kc, 512).bitcast(BF16)
                    mm3(ps[p][:, :], bp,
                        xv[:, 128 * lc4:128 * (lc4 + 1)],
                        xv[:, 512 + 128 * lc4:512 + 128 * (lc4 + 1)],
                        wv[:, 512 * vt2:512 * (vt2 + 1)],
                        wv[:, 1024 + 512 * vt2:1024 + 512 * (vt2 + 1)],
                        kc == 0, kc == NKC - 1, [bXN[kc], bWV[kc]])
                op("vector", lambda e, p=p, vt2=vt2: e.tensor_tensor(
                    AR(oVT + VW * 8 * vt2, VW * 8).rearrange("p (h w) -> p h w", w=VW)[:, :, 0:64],
                    ps[p][:, :].rearrange("p (h d) -> p h d", d=64),
                    AR(oBVB + 512 * vt2, 512).rearrange("p (h d) -> p h d", d=64),
                    ALU.add), reads=[bp, bBVB], writes=[bVT])
            op("gpsimd", lambda e, tt=tt, lc4=lc4: e.dma_start(
                out=dV.ap()[512 * tt + 128 * lc4:512 * tt + 128 * (lc4 + 1), :],
                in_=AR(oVT, 1056)), reads=[bVT], writes=[bdV], dma=True)

    if debug:
        op("gpsimd", lambda e: e.dma_start(out=dbgV1, in_=dV.ap()), reads=[bdV],
           writes=[Buf()], dma=True)

    for oLM, bLM in ((oQLM, bQLM), (oKLM, bKLM)):
        op("vector", lambda e, oLM=oLM: e.tensor_scalar(AR(oLM, 512), AR(oLM, 512),
                                                        1.0 / 32, None, ALU.mult),
           reads=[bLM], writes=[bLM])
    for i in range(NKC):
        op("gpsimd", lambda e, i=i: e.dma_start(
            out=cli.ap()[128 * i:128 * (i + 1), :], in_=AR(oQLM + 64 * i, 64)),
           reads=[bQLM], writes=[bcl], dma=True)
        op("gpsimd", lambda e, i=i: e.dma_start(
            out=cli.ap()[D + 128 * i:D + 128 * (i + 1), :], in_=AR(oKLM + 64 * i, 64)),
           reads=[bKLM], writes=[bcl], dma=True)
    op("gpsimd", lambda e: e.collective_compute(
        "AllGather", ALU.bypass, replica_groups=[[0, 1], [2, 3], [4, 5], [6, 7]],
        ins=[cli.ap().opt()], outs=[clo.ap().opt()]),
       reads=[bcl], writes=[bcl], waits=[bcw.w])

    oQF, bQF = ar.take(0, 1024)
    oKF, bKF = ar.take(1024, 1024)
    for i in range(NKC):
        op("gpsimd", lambda e, i=i: e.dma_start(
            out=AR(oQF + 128 * i, 64), in_=clo.ap()[128 * i:128 * (i + 1), :]),
           reads=[bcl], writes=[bQF], dma=True)
        op("gpsimd", lambda e, i=i: e.dma_start(
            out=AR(oQF + 128 * i + 64, 64),
            in_=clo.ap()[2 * D + 128 * i:2 * D + 128 * (i + 1), :]),
           reads=[bcl], writes=[bQF], dma=True)
        op("gpsimd", lambda e, i=i: e.dma_start(
            out=AR(oKF + 128 * i, 64), in_=clo.ap()[D + 128 * i:D + 128 * (i + 1), :]),
           reads=[bcl], writes=[bKF], dma=True)
        op("gpsimd", lambda e, i=i: e.dma_start(
            out=AR(oKF + 128 * i + 64, 64),
            in_=clo.ap()[3 * D + 128 * i:3 * D + 128 * (i + 1), :]),
           reads=[bcl], writes=[bKF], dma=True)

    if debug:
        op("gpsimd", lambda e: e.dma_start(out=dbgLM[0:1, :].rearrange("o (a b) -> (o a) b", b=128), in_=AR(oQF, 128)[0:8, :]), reads=[bQF], writes=[Buf()], dma=True)
        op("gpsimd", lambda e: e.dma_start(out=dbgLM[1:2, :].rearrange("o (a b) -> (o a) b", b=128), in_=AR(oKF, 128)[0:8, :]), reads=[bKF], writes=[Buf()], dma=True)

    if debug:
        op("gpsimd", lambda e: e.dma_start(out=dbgV2, in_=dV.ap()), reads=[bdV],
           writes=[Buf()], dma=True)

    def lmsl(o, h):
        po = 64 * (h % 2)
        return AR(o + 128 * (h // 2), 128)[po:po + 64, :]

    oP2, bP2 = ar.take(2048, 2048)
    oP2T, bP2T = ar.take(4096, 2048)
    oVI, bVI = ar.take(6144, 2048)
    oYI, bYI = ar.take(8192, 2048)
    oT0, bT0 = ar.take(10240, 2048)
    oT0T, bT0T = ar.take(12288, 2048)
    oT1, bT1 = ar.take(14336, 2048)
    oI4, bI4 = ar.take(16384, 512)
    for q in range(4):
        op("vector", lambda e, q=q: e.tensor_copy(AR(oI4 + 128 * q, 128), IDN[:, :]),
           reads=[bIDN], writes=[bI4])

    for h4 in range(4):
        p, bp = (0, bps[0]) if h4 % 2 == 0 else (1, bps[1])
        for hh in range(4):
            h = 4 * h4 + hh
            op("tensor", lambda e, h=h, hh=hh, p=p: e.matmul(
                ps[p][:, 128 * hh:128 * (hh + 1)], lmsl(oQF, h), lmsl(oKF, h),
                start=True, stop=True), reads=[bQF, bKF], writes=[bp])
        for hh in range(4):
            h = 4 * h4 + hh
            op("scalar", lambda e, p=p, hh=hh, h=h: e.activation(
                AR(oP2 + 128 * h, 128), ps[p][:, 128 * hh:128 * (hh + 1)], AF.Exp),
               reads=[bp], writes=[bP2])
        for hh in range(4):
            h = 4 * h4 + hh
            op("vector", lambda e, h=h: e.tensor_reduce(
                BB[:, 8:9], AR(oP2 + 128 * h, 128), AX.X, ALU.add),
               reads=[bP2], writes=[bBB])
            op("vector", lambda e: e.reciprocal(BB[:, 9:10], BB[:, 8:9]),
               reads=[bBB], writes=[bBB])
            op("vector", lambda e, h=h: e.tensor_scalar(
                AR(oP2 + 128 * h, 128), AR(oP2 + 128 * h, 128), BB[:, 9:10], None,
                ALU.mult), reads=[bBB], writes=[bP2])
    op("vector", lambda e: e.tensor_reduce(
        SS[:, 0:64], AR(oP2, 2048).rearrange("p (b x) -> p b x", x=32), AX.X, ALU.add,
        apply_transpose=True), reads=[bP2], writes=[bSS])
    op("tensor", lambda e: e.transpose(ps[7][0:64, 0:128], SS[:, 0:64], IDN[:, :]),
       reads=[bSS, bIDN], writes=[bps[7]])
    op("vector", lambda e: e.tensor_copy(SS2[0:64, 0:32], ps[7][0:64, 0:32]),
       reads=[bps[7]], writes=[bSS2])
    for g in range(1, 4):
        op("vector", lambda e, g=g: e.tensor_tensor(
            SS2[0:64, 0:32], SS2[0:64, 0:32], ps[7][0:64, 32 * g:32 * (g + 1)],
            ALU.add), reads=[bps[7]], writes=[bSS2])
    op("vector", lambda e: e.reduce_max(SS2[0:64, 64:65], SS2[0:64, 0:32], AX.X), reads=[bSS2], writes=[bSS2])
    op("tensor", lambda e: e.transpose(ps[7][0:1, 128:192], SS2[0:64, 64:65],
                                       IDN[0:64, 0:64]), reads=[bSS2, bIDN],
       writes=[bps[7]])
    op("vector", lambda e: e.reduce_max(SCR[0:1, 0:1], ps[7][0:1, 128:192], AX.X), reads=[bps[7]], writes=[bSCR])
    op("gpsimd", lambda e: e.dma_start(out=c2i.ap(), in_=SCR[0:1, 0:1]),
       reads=[bSCR], writes=[bc2], dma=True)
    op("gpsimd", lambda e: e.collective_compute(
        "AllReduce", ALU.max, replica_groups=[list(range(8))],
        ins=[c2i.ap().opt()], outs=[c2o.ap().opt()]),
       reads=[bc2], writes=[bc2])
    op("gpsimd", lambda e: e.dma_start(out=SC1[:, 0:1], in_=c2o.ap().broadcast_to([128, 1])),
       reads=[bc2], writes=[bSC1], dma=True)

    for h in range(NH):
        sl = h % 4
        op("tensor", lambda e, h=h, sl=sl: e.transpose(
            ps[7][:, 128 * sl:128 * (sl + 1)], AR(oP2 + 128 * h, 128), IDN[:, :]),
           reads=[bP2, bIDN], writes=[bps[7]])
        op("vector", lambda e, h=h, sl=sl: e.tensor_copy(
            AR(oP2T + 128 * h, 128), ps[7][:, 128 * sl:128 * (sl + 1)]),
           reads=[bps[7]], writes=[bP2T])
    op("vector", lambda e: e.reciprocal(SC1[:, 15:16], SC1[:, 0:1]),
       reads=[bSC1], writes=[bSC1])
    op("vector", lambda e: e.tensor_scalar(AR(oVI, 2048), AR(oP2T, 2048), SC1[:, 15:16],
                                           None, ALU.mult),
       reads=[bP2T, bSC1], writes=[bVI])
    op("vector", lambda e: e.tensor_scalar(AR(oYI, 2048), AR(oP2, 2048), SC1[:, 15:16],
                                           None, ALU.mult),
       reads=[bP2, bSC1], writes=[bYI])

    if debug:
        op("gpsimd", lambda e: e.dma_start(out=dbgP2, in_=AR(oP2, 2048)), reads=[bP2], writes=[Buf()], dma=True)

    def inv_iter():
        for h in range(NH):
            p, bp = (4, bps[4]) if (h // 4) % 2 == 0 else (5, bps[5])
            sl = h % 4
            op("tensor", lambda e, h=h, p=p, sl=sl: e.matmul(
                ps[p][:, 128 * sl:128 * (sl + 1)], AR(oP2T + 128 * h, 128),
                AR(oVI + 128 * h, 128), start=True, stop=True),
               reads=[bP2T, bVI], writes=[bp])
            if sl == 3:
                op("vector", lambda e, h=h, p=p: e.tensor_copy(
                    AR(oT0 + 128 * (h - 3), 512), ps[p][:, :]), reads=[bp], writes=[bT0])
        for h in range(NH):
            sl = h % 4
            op("tensor", lambda e, h=h, sl=sl: e.transpose(
                ps[7][:, 128 * sl:128 * (sl + 1)], AR(oT0 + 128 * h, 128), IDN[:, :]),
               reads=[bT0, bIDN], writes=[bps[7]])
            op("vector", lambda e, h=h, sl=sl: e.tensor_copy(
                AR(oT0T + 128 * h, 128), ps[7][:, 128 * sl:128 * (sl + 1)]),
               reads=[bps[7]], writes=[bT0T])
        for q in range(4):
            op("vector", lambda e, q=q: e.scalar_tensor_tensor(
                AR(oT1 + 512 * q, 512), AR(oI4, 512), 7.0, AR(oT0 + 512 * q, 512),
                ALU.mult, ALU.subtract), reads=[bI4, bT0], writes=[bT1])
        for h in range(NH):
            p, bp = (4, bps[4]) if (h // 4) % 2 == 0 else (5, bps[5])
            sl = h % 4
            op("tensor", lambda e, h=h, p=p, sl=sl: e.matmul(
                ps[p][:, 128 * sl:128 * (sl + 1)], AR(oT0T + 128 * h, 128),
                AR(oT1 + 128 * h, 128), start=True, stop=True),
               reads=[bT0T, bT1], writes=[bp])
            if sl == 3:
                op("vector", lambda e, h=h, p=p: e.scalar_tensor_tensor(
                    AR(oT1 + 128 * (h - 3), 512), AR(oI4, 512), 15.0, ps[p][:, :],
                    ALU.mult, ALU.subtract), reads=[bp, bI4], writes=[bT1])
        for h in range(NH):
            p, bp = (4, bps[4]) if (h // 4) % 2 == 0 else (5, bps[5])
            sl = h % 4
            op("tensor", lambda e, h=h, p=p, sl=sl: e.matmul(
                ps[p][:, 128 * sl:128 * (sl + 1)], AR(oT0T + 128 * h, 128),
                AR(oT1 + 128 * h, 128), start=True, stop=True),
               reads=[bT0T, bT1], writes=[bp])
            if sl == 3:
                op("vector", lambda e, h=h, p=p: e.scalar_tensor_tensor(
                    AR(oT1 + 128 * (h - 3), 512), AR(oI4, 512), 13.0, ps[p][:, :],
                    ALU.mult, ALU.subtract), reads=[bp, bI4], writes=[bT1])
        for h in range(NH):
            p, bp = (4, bps[4]) if (h // 4) % 2 == 0 else (5, bps[5])
            p2_, bp2 = (6, bps[6]) if (h // 4) % 2 == 0 else (6, bps[6])
            sl = h % 4
            op("tensor", lambda e, h=h, p=p, sl=sl: e.matmul(
                ps[p][:, 128 * sl:128 * (sl + 1)], AR(oYI + 128 * h, 128),
                AR(oT1 + 128 * h, 128), start=True, stop=True),
               reads=[bYI, bT1], writes=[bp])
            op("tensor", lambda e, h=h, p2_=p2_, sl=sl: e.matmul(
                ps[p2_][:, 128 * sl:128 * (sl + 1)], AR(oT1 + 128 * h, 128),
                AR(oYI + 128 * h, 128), start=True, stop=True),
               reads=[bYI, bT1], writes=[bp2])
            if sl == 3:
                op("vector", lambda e, h=h, p=p: e.tensor_scalar(
                    AR(oVI + 128 * (h - 3), 512), ps[p][:, :], 0.25, None, ALU.mult),
                   reads=[bp], writes=[bVI])
                op("vector", lambda e, h=h, p2_=p2_: e.tensor_scalar(
                    AR(oYI + 128 * (h - 3), 512), ps[p2_][:, :], 0.25, None, ALU.mult),
                   reads=[bp2], writes=[bYI])

    for it_ in range(6):
        inv_iter()
        if debug:
            op("gpsimd", lambda e, it_=it_: e.dma_start(out=dbgIT[it_], in_=AR(oVI, 2048)),
               reads=[bVI], writes=[Buf()], dma=True)
            op("gpsimd", lambda e, it_=it_: e.dma_start(out=dbgYT[it_], in_=AR(oYI, 2048)),
               reads=[bYI], writes=[Buf()], dma=True)

    if debug:
        op("gpsimd", lambda e: e.dma_start(out=dbgVI, in_=AR(oVI, 2048)), reads=[bVI], writes=[Buf()], dma=True)

    oE3, bE3 = ar.take(16896, 2048)
    oKC, bKC = ar.take(18944, 2048, 2)
    oVC, bVC = ar.take(20992, 2112, 2)
    oG0, bG0 = ar.take(23104, 1056)
    oGG, bGG = ar.take(24160, 1024)

    for lc in range(NLC):
        okc, bkc = oKC + 1024 * (lc % 2), bKC[lc % 2]
        ovc, bvc = oVC + 1056 * (lc % 2), bVC[lc % 2]
        for i in range(NKC):
            op("sync", lambda e, i=i, lc=lc, okc=okc: e.dma_start(
                out=AR(okc + 128 * i, 128),
                in_=dK.ap()[128 * i:128 * (i + 1), 128 * lc:128 * (lc + 1)]),
               reads=[bdK], writes=[bkc], dma=True)
        op("sync", lambda e, lc=lc, ovc=ovc: e.dma_start(
            out=AR(ovc, 1056), in_=dV.ap()[128 * lc:128 * (lc + 1), :]),
           reads=[bdV], writes=[bvc], dma=True)
        for h4 in range(4):
            p, bp = (2, bps[2]) if h4 % 2 == 0 else (3, bps[3])
            for hh in range(4):
                h = 4 * h4 + hh
                po = 64 * (h % 2)
                op("tensor", lambda e, h=h, hh=hh, po=po, p=p, okc=okc: e.matmul(
                    ps[p][:, 128 * hh:128 * (hh + 1)],
                    AR(okc + 128 * (h // 2), 128)[po:po + 64, :], lmsl(oQF, h),
                    start=True, stop=True), reads=[bkc, bQF], writes=[bp])
            op("scalar", lambda e, p=p, h4=h4: e.activation(
                AR(oE3 + 512 * h4, 512), ps[p][:, :], AF.Exp),
               reads=[bp], writes=[bE3])
        pb0 = 4 if lc % 2 == 0 else 0
        pmap = (pb0, pb0 + 1, 7 if lc % 2 else 6)
        for h in range(NH):
            p = pmap[0] if h < 7 else (pmap[1] if h < 14 else pmap[2])
            slot = h if h < 7 else (h - 7 if h < 14 else h - 14)
            op("tensor", lambda e, h=h, p=p, slot=slot, ovc=ovc: e.matmul(
                ps[p][:, VW * slot:VW * (slot + 1)], AR(oE3 + 128 * h, 128),
                AR(ovc + VW * h, VW), start=True, stop=True),
               reads=[bE3, bvc], writes=[bps[p]])
        for seg, p in ((0, pmap[0]), (1, pmap[1]), (2, pmap[2])):
            o0 = 462 * seg
            n = 462 if seg < 2 else 132
            if lc == 0:
                op("vector", lambda e, o0=o0, n=n, p=p: e.tensor_copy(
                    AR(oG0 + o0, n), ps[p][:, 0:n]), reads=[bps[p]], writes=[bG0])
            else:
                op("vector", lambda e, o0=o0, n=n, p=p: e.tensor_tensor(
                    AR(oG0 + o0, n), AR(oG0 + o0, n), ps[p][:, 0:n], ALU.add),
                   reads=[bps[p]], writes=[bG0])

    if debug:
        op("gpsimd", lambda e: e.dma_start(out=dbgG0p, in_=AR(oG0, 1056)),
           reads=[bG0], writes=[Buf()], dma=True)
        op("gpsimd", lambda e: e.dma_start(out=dbgE3, in_=AR(oE3, 2048)),
           reads=[bE3], writes=[Buf()], dma=True)
    for h in range(NH):
        op("gpsimd", lambda e, h=h: e.dma_start(
            out=c3i.ap()[128 * h:128 * (h + 1), :], in_=AR(oG0 + VW * h, VW)),
           reads=[bG0], writes=[bc3], dma=True)
    op("gpsimd", lambda e: e.collective_compute(
        "AllReduce", ALU.add, replica_groups=[[0, 1], [2, 3], [4, 5], [6, 7]],
        ins=[c3i.ap().opt()], outs=[c3o.ap().opt()]),
       reads=[bc3], writes=[bc3])
    for h in range(NH):
        op("gpsimd", lambda e, h=h: e.dma_start(
            out=AR(oG0 + VW * h, VW), in_=c3o.ap()[128 * h:128 * (h + 1), :]),
           reads=[bc3], writes=[bG0], dma=True)
    for h in range(NH):
        op("vector", lambda e, h=h: e.reciprocal(BB[:, 10:11], AR(oG0 + VW * h + 64, 1)),
           reads=[bG0], writes=[bBB])
        op("vector", lambda e, h=h: e.tensor_scalar(
            AR(oG0 + VW * h, 64), AR(oG0 + VW * h, 64), BB[:, 10:11], None, ALU.mult),
           reads=[bBB], writes=[bG0])
    for h in range(NH):
        p, bp = (4, bps[4]) if (h // 8) % 2 == 0 else (5, bps[5])
        sl = h % 8
        op("tensor", lambda e, h=h, p=p, sl=sl: e.matmul(
            ps[p][:, 64 * sl:64 * (sl + 1)], AR(oYI + 128 * h, 128),
            AR(oG0 + VW * h, 64), start=True, stop=True),
           reads=[bYI, bG0], writes=[bp])
        if sl == 7:
            op("vector", lambda e, h=h, p=p: e.tensor_copy(
                AR(oGG + 64 * (h - 7), 512), ps[p][:, :]), reads=[bp], writes=[bGG])

    if debug:
        op("gpsimd", lambda e: e.dma_start(out=dbgG0, in_=AR(oG0, 1056)), reads=[bG0], writes=[Buf()], dma=True)
        op("gpsimd", lambda e: e.dma_start(out=dbgGG, in_=AR(oGG, 1024)), reads=[bGG], writes=[Buf()], dma=True)

    oXU, bXU = ar.take(2048, 16384, NKC)
    oQC, bQC = ar.take(18944, 2048, 2)
    oE1, bE1 = ar.take(20992, 2048)
    oDB, bDB = ar.take(25184, 2048)
    oDS, bDS = ar.take(27232, 2048)
    oDR, bDR = ar.take(29280, 2048)

    for lc in range(NLC):
        oqc, bqc = oQC + 1024 * (lc % 2), bQC[lc % 2]
        for i in range(NKC):
            op("sync", lambda e, i=i, lc=lc, oqc=oqc: e.dma_start(
                out=AR(oqc + 128 * i, 128),
                in_=dQ.ap()[128 * i:128 * (i + 1), 128 * lc:128 * (lc + 1)]),
               reads=[bdQ], writes=[bqc], dma=True)
        for h4 in range(4):
            p, bp = (2, bps[2]) if h4 % 2 == 0 else (3, bps[3])
            for hh in range(4):
                h = 4 * h4 + hh
                po = 64 * (h % 2)
                op("tensor", lambda e, h=h, hh=hh, po=po, p=p, oqc=oqc: e.matmul(
                    ps[p][:, 128 * hh:128 * (hh + 1)], lmsl(oKF, h),
                    AR(oqc + 128 * (h // 2), 128)[po:po + 64, :],
                    start=True, stop=True), reads=[bqc, bKF], writes=[bp])
            op("scalar", lambda e, p=p, h4=h4: e.activation(
                AR(oE1 + 512 * h4, 512), ps[p][:, :], AF.Exp),
               reads=[bp], writes=[bE1])
        op("vector", lambda e: e.tensor_reduce(
            SS[:, 0:64], AR(oE1, 2048).rearrange("p (b x) -> p b x", x=32), AX.X,
            ALU.add, apply_transpose=True), reads=[bE1], writes=[bSS])
        op("tensor", lambda e: e.transpose(ps[7][0:64, 0:128], SS[:, 0:64], IDN[:, :]),
           reads=[bSS, bIDN], writes=[bps[7]])
        op("vector", lambda e: e.tensor_copy(ST[0:64, 0:32], ps[7][0:64, 0:32]),
           reads=[bps[7]], writes=[bST])
        for g in range(1, 4):
            op("vector", lambda e, g=g: e.tensor_tensor(
                ST[0:64, 0:32], ST[0:64, 0:32], ps[7][0:64, 32 * g:32 * (g + 1)],
                ALU.add), reads=[bps[7]], writes=[bST])
        op("gpsimd", lambda e: e.dma_start(out=dTmp.ap(), in_=ST[0:64, 0:32]),
           reads=[bST], writes=[bdTmp], dma=True)
        op("gpsimd", lambda e, lc=lc: e.dma_start(
            out=dD1.ap().rearrange("h (l b4 i) -> h l b4 i", b4=4, i=32)[:, lc, :, :],
            in_=dTmp.ap().rearrange("(h b4) i -> h b4 i", b4=4)),
           reads=[bdTmp], writes=[bdD1], dma=True)
        for jc in range(NKC):
            p, bp = (0, bps[0]) if jc % 2 == 0 else (1, bps[1])
            for par in range(2):
                h = 2 * jc + par
                op("tensor", lambda e, h=h, p=p, par=par: e.matmul(
                    ps[p][64 * par:64 * (par + 1), 0:128], AR(oGG + 64 * h, 64),
                    AR(oE1 + 128 * h, 128), start=True, stop=True),
                   reads=[bGG, bE1], writes=[bp])
            op("vector", lambda e, jc=jc, lc=lc, p=p: e.tensor_copy(
                AR(oXU + 2048 * jc + 128 * lc, 128), ps[p][:, 0:128]),
               reads=[bp], writes=[bXU[jc]])

    op("sync", lambda e: e.dma_start(out=AR(oDS, 2048)[0:16, :], in_=dD1.ap()),
       reads=[bdD1], writes=[bDS], dma=True)
    op("vector", lambda e: e.reciprocal(AR(oDR, 2048)[0:16, :], AR(oDS, 2048)[0:16, :]),
       reads=[bDS], writes=[bDR])
    op("gpsimd", lambda e: e.dma_start(out=dD1.ap(), in_=AR(oDR, 2048)[0:16, :]),
       reads=[bDR], writes=[bdD1], dma=True)
    for jc in range(NKC):
        op("gpsimd", lambda e, jc=jc: e.dma_start(
            out=AR(oDB, 2048)[0:64, :],
            in_=dD1.ap()[2 * jc:2 * jc + 1, :].broadcast_to([64, T])),
           reads=[bdD1], writes=[bDB], dma=True)
        op("gpsimd", lambda e, jc=jc: e.dma_start(
            out=AR(oDB, 2048)[64:128, :],
            in_=dD1.ap()[2 * jc + 1:2 * jc + 2, :].broadcast_to([64, T])),
           reads=[bdD1], writes=[bDB], dma=True)
        op("vector", lambda e, jc=jc: e.tensor_tensor(
            AR(oXU + 2048 * jc, 2048), AR(oXU + 2048 * jc, 2048), AR(oDB, 2048),
            ALU.mult), reads=[bDB], writes=[bXU[jc]])
        op("vector", lambda e, jc=jc: e.tensor_tensor(
            X[jc][:, :], X[jc][:, :], AR(oXU + 2048 * jc, 2048), ALU.add),
           reads=[bXU[jc]], writes=[bX[jc]])

    if debug:
        op("gpsimd", lambda e: e.dma_start(out=dbgD1, in_=dD1.ap()), reads=[bdD1], writes=[Buf()], dma=True)
        op("gpsimd", lambda e: e.dma_start(out=dbgQ, in_=dQ.ap()), reads=[bdQ], writes=[Buf()], dma=True)
        op("gpsimd", lambda e: e.dma_start(out=dbgK, in_=dK.ap()), reads=[bdK], writes=[Buf()], dma=True)
        op("gpsimd", lambda e: e.dma_start(out=dbgV, in_=dV.ap()), reads=[bdV], writes=[Buf()], dma=True)
        for jc in range(NKC):
            op("gpsimd", lambda e, jc=jc: e.dma_start(out=dbgX1[128 * jc:128 * (jc + 1), :], in_=X[jc][:, :]), reads=[bX[jc]], writes=[Buf()], dma=True)

    def ffn_sublayer2(first, rowd, browd, cgi_, cgo_, bcg_):
        oWF, bWF = ar.take(0, 8192, NKC)
        oQZ, bQZ = ar.take(8192, 8192, NKC)
        oXF, bXF = ar.take(16384, 16384, NKC)
        if first:
            for i in range(NKC):
                op("sync", lambda e, i=i: e.dma_start(
                    out=AR(oWF + 1024 * i, 1024), in_=wffn[128 * i:128 * (i + 1), :]),
                   writes=[bWF[i]], dma=True)
            for i in range(NKC):
                if i == 0:
                    op("vector", lambda e, i=i: e.tensor_reduce(
                        BB[:, 11:12], AR(oWF + 1024 * i, 1024), AX.X, ALU.add,
                        apply_absolute_value=True), reads=[bWF[i]], writes=[bBB])
                else:
                    op("vector", lambda e, i=i: e.tensor_reduce(
                        BB[:, 12:13], AR(oWF + 1024 * i, 1024), AX.X, ALU.add,
                        apply_absolute_value=True), reads=[bWF[i]], writes=[bBB])
                    op("vector", lambda e: e.tensor_tensor(
                        BB[:, 11:12], BB[:, 11:12], BB[:, 12:13], ALU.add),
                       reads=[bBB], writes=[bBB])
            op("tensor", lambda e: e.transpose(ps[7][0:1, 0:128], BB[:, 11:12],
                                               IDN[:, :]), reads=[bBB, bIDN],
               writes=[bps[7]])
            op("vector", lambda e: e.tensor_reduce(SCR[0:1, 1:2], ps[7][0:1, 0:128],
                                                   AX.X, ALU.add),
               reads=[bps[7]], writes=[bSCR])
            op("vector", lambda e: e.tensor_scalar(SCR[0:1, 2:3], SCR[0:1, 1:2],
                                                   1.0 / (D * D), None, ALU.mult),
               reads=[bSCR], writes=[bSCR])
            op("gpsimd", lambda e: e.dma_start(out=dTmp.ap()[0:1, 0:1],
                                               in_=SCR[0:1, 2:3]),
               reads=[bSCR], writes=[bdTmp], dma=True)
            op("sync", lambda e: e.dma_start(
                out=SC1[:, 1:2], in_=dTmp.ap()[0:1, 0:1].broadcast_to([128, 1])),
               reads=[bdTmp], writes=[bSC1], dma=True)
            op("vector", lambda e: e.tensor_scalar(SC1[:, 2:3], SC1[:, 1:2], 1.0, 1e-5,
                                                   ALU.mult, ALU.add),
               reads=[bSC1], writes=[bSC1])
            op("vector", lambda e: e.tensor_scalar(SC1[:, 3:4], SC1[:, 2:3], 0.5, None,
                                                   ALU.mult), reads=[bSC1], writes=[bSC1])
            op("vector", lambda e: e.tensor_scalar(SC1[:, 4:5], SC1[:, 3:4], -1.0, None,
                                                   ALU.mult), reads=[bSC1], writes=[bSC1])
            for i in range(NKC):
                op("vector", lambda e, i=i: e.tensor_scalar(
                    AR(oQZ + 1024 * i, 1024), AR(oWF + 1024 * i, 1024), SC1[:, 3:4],
                    None, ALU.is_gt), reads=[bWF[i], bSC1], writes=[bQZ[i]])
                op("vector", lambda e, i=i: e.tensor_scalar(
                    AR(oWF + 1024 * i, 1024), AR(oWF + 1024 * i, 1024), SC1[:, 4:5],
                    None, ALU.is_lt), reads=[bSC1], writes=[bWF[i]])
                qv = AR(oQZ + 1024 * i, 1024).bitcast(BF16)
                op("vector", lambda e, i=i, qv=qv: e.tensor_tensor(
                    qv[:, 0:1024], AR(oQZ + 1024 * i, 1024),
                    AR(oWF + 1024 * i, 1024), ALU.subtract),
                   reads=[bWF[i]], writes=[bQZ[i]])
                op("gpsimd", lambda e, i=i, qv=qv: e.dma_start(
                    out=dWq.ap()[128 * i:128 * (i + 1), :], in_=qv[:, 0:1024]),
                   reads=[bQZ[i]], writes=[bdWq], dma=True)
        else:
            for i in range(NKC):
                qv = AR(oQZ + 1024 * i, 1024).bitcast(BF16)
                op("sync", lambda e, i=i, qv=qv: e.dma_start(
                    out=qv[:, 0:1024], in_=dWq.ap()[128 * i:128 * (i + 1), :]),
                   reads=[bdWq], writes=[bQZ[i]], dma=True)
        op("vector", lambda e: e.memset(BB[:, 13:14], 0.0), writes=[bBB])
        for tt in range(NTT):
            ln_stats(tt, lambda i, tt=tt: X[i][:, 512 * tt:512 * (tt + 1)], bX,
                     rowd, browd, "ffn")
            for i in range(NKC):
                xn_chunk(AR(oXF + 2048 * i + 512 * tt, 512), bXF[i], i, tt, True)
                op("vector", lambda e, i=i, tt=tt: e.reduce_max(
                    BB[:, 14:15], AR(oXF + 2048 * i + 512 * tt, 512), AX.X,
                    apply_absolute_value=True), reads=[bXF[i]], writes=[bBB])
                op("vector", lambda e: e.tensor_tensor(BB[:, 13:14], BB[:, 13:14],
                                                       BB[:, 14:15], ALU.max),
                   reads=[bBB], writes=[bBB])
        partition_max_to_scalar(BB[:, 13:14], bBB, cgi_, bcg_)
        op("gpsimd", lambda e: e.collective_compute(
            "AllReduce", ALU.max, replica_groups=[list(range(8))],
            ins=[cgi_.ap().opt()], outs=[cgo_.ap().opt()]),
           reads=[bcg_], writes=[bcg_])
        op("gpsimd", lambda e: e.dma_start(out=SC1[:, 5:6],
                                           in_=cgo_.ap().broadcast_to([128, 1])),
           reads=[bcg_], writes=[bSC1], dma=True)
        op("vector", lambda e: e.reciprocal(SC1[:, 6:7], SC1[:, 5:6]),
           reads=[bSC1], writes=[bSC1])
        op("vector", lambda e: e.tensor_scalar(SC1[:, 6:7], SC1[:, 6:7], 128.0, None,
                                               ALU.mult), reads=[bSC1], writes=[bSC1])
        op("vector", lambda e: e.tensor_scalar(SC1[:, 7:8], SC1[:, 5:6], SC1[:, 1:2],
                                               None, ALU.mult), reads=[bSC1], writes=[bSC1])
        op("vector", lambda e: e.tensor_scalar(SC1[:, 7:8], SC1[:, 7:8], 1.0 / 128.0,
                                               None, ALU.mult), reads=[bSC1], writes=[bSC1])
        CL = 128.0 - 1e-5
        for tt in range(NTT):
            for i in range(NKC):
                xv = AR(oXF + 2048 * i + 512 * tt, 512).bitcast(BF16)
                op("vector", lambda e, i=i, tt=tt: e.tensor_scalar(
                    TMP[:, :], AR(oXF + 2048 * i + 512 * tt, 512), SC1[:, 6:7], None,
                    ALU.mult), reads=[bSC1, bXF[i]], writes=[bTMP])
                op("vector", lambda e: e.tensor_scalar(
                    TMP[:, :], TMP[:, :], CL, -CL, ALU.min, ALU.max),
                   reads=[], writes=[bTMP])
                op("scalar", lambda e, xv=xv: e.activation(
                    xv[:, 0:512], TMP[:, :], AF.Identity, bias=0.0, scale=1.0),
                   reads=[bTMP], writes=[bXF[i]])
                op("vector", lambda e, xv=xv: e.scalar_tensor_tensor(
                    xv[:, 512:1024], TMP[:, :], 1.0, xv[:, 0:512], ALU.mult,
                    ALU.subtract), reads=[bTMP], writes=[bXF[i]])
            for jc in range(NKC):
                p, bp = (0, bps[0]) if jc % 2 == 0 else (1, bps[1])
                for kc in range(NKC):
                    qv = AR(oQZ + 1024 * kc, 1024).bitcast(BF16)
                    xv = AR(oXF + 2048 * kc + 512 * tt, 512).bitcast(BF16)
                    for i2 in range(2):
                        op("tensor", lambda e, kc=kc, jc=jc, p=p, qv=qv, xv=xv,
                           i2=i2: e.matmul(
                            ps[p][:, :], qv[:, 128 * jc:128 * (jc + 1)],
                            xv[:, 512 * i2:512 * (i2 + 1)],
                            start=(kc == 0 and i2 == 0),
                            stop=(kc == NKC - 1 and i2 == 1)),
                           reads=[bQZ[kc], bXF[kc]], writes=[bp])
                op("vector", lambda e, jc=jc, tt=tt, p=p: e.scalar_tensor_tensor(
                    X[jc][:, 512 * tt:512 * (tt + 1)], ps[p][:, :], SC1[:, 7:8],
                    X[jc][:, 512 * tt:512 * (tt + 1)], ALU.mult, ALU.add),
                   reads=[bp, bSC1], writes=[bX[jc]])

    ffn_sublayer2(True, dRow2, bdRow2, cgi, cgo, bcg)

    if debug:
        for jc in range(NKC):
            op("gpsimd", lambda e, jc=jc: e.dma_start(out=dbgX2[128 * jc:128 * (jc + 1), :], in_=X[jc][:, :]), reads=[bX[jc]], writes=[Buf()], dma=True)
        op("gpsimd", lambda e: e.dma_start(out=dbgSC, in_=SC1[:, :]), reads=[bSC1], writes=[Buf()], dma=True)

    oWQb, bWQb = ar.take(0, 8192, NKC)
    oWKb, bWKb = ar.take(8192, 8192, NKC)
    oWVb, bWVb = ar.take(16384, 8192, NKC)
    oXN3, bXN3 = ar.take(24576, 4096, NKC)
    oEV3, bEV3 = ar.take(28672, 1024, 2)
    oVT3, bVT3 = ar.take(29696, 1056)
    wsplit_load(oWQb, bWQb, wqbT, oXN3, (bXN3[0], bXN3[2]))
    wsplit_load(oWKb, bWKb, wkbT, oXN3, (bXN3[0], bXN3[2]))
    wsplit_load(oWVb, bWVb, wvbT, oXN3, (bXN3[0], bXN3[2]))
    op("vector", lambda e: e.memset(AR(oVT3, 1056), 1.0), writes=[bVT3])
    for tt in range(NTT):
        ln_stats(tt, lambda i, tt=tt: X[i][:, 512 * tt:512 * (tt + 1)], bX,
                 dRow, bdRow, "plain")
        for i in range(NKC):
            xn_chunk_split(oXN3 + 512 * i, bXN3[i], i, tt)
        for (oW, bW, dspill, bdspill) in ((oWQb, bWQb, dQb, bdQb),
                                          (oWKb, bWKb, dKb, bdKb)):
            for jc in range(NKC):
                p, bp = (0, bps[0]) if jc % 2 == 0 else (1, bps[1])
                ev, bev = (oEV3 + 512 * (jc % 2)), bEV3[jc % 2]
                for kc in range(NKC):
                    wv = AR(oW + 1024 * kc, 1024).bitcast(BF16)
                    xv = AR(oXN3 + 512 * kc, 512).bitcast(BF16)
                    mm3(ps[p][:, :], bp,
                        wv[:, 128 * jc:128 * (jc + 1)],
                        wv[:, 1024 + 128 * jc:1024 + 128 * (jc + 1)],
                        xv[:, 0:512], xv[:, 512:1024],
                        kc == 0, kc == NKC - 1, [bW[kc], bXN3[kc]])
                op("vector", lambda e, p=p, ev=ev: e.tensor_copy(AR(ev, 512), ps[p][:, :]),
                   reads=[bp], writes=[bev])
                op("gpsimd", lambda e, ev=ev, jc=jc, tt=tt, dspill=dspill: e.dma_start(
                    out=dspill.ap()[128 * jc:128 * (jc + 1), 512 * tt:512 * (tt + 1)],
                    in_=AR(ev, 512)), reads=[bev], writes=[bdspill], dma=True)
        for lc4 in range(4):
            for vt2 in range(2):
                p, bp = (2, bps[2]) if vt2 == 0 else (3, bps[3])
                for kc in range(NKC):
                    wv = AR(oWVb + 1024 * kc, 1024).bitcast(BF16)
                    xv = AR(oXN3 + 512 * kc, 512).bitcast(BF16)
                    mm3(ps[p][:, :], bp,
                        xv[:, 128 * lc4:128 * (lc4 + 1)],
                        xv[:, 512 + 128 * lc4:512 + 128 * (lc4 + 1)],
                        wv[:, 512 * vt2:512 * (vt2 + 1)],
                        wv[:, 1024 + 512 * vt2:1024 + 512 * (vt2 + 1)],
                        kc == 0, kc == NKC - 1, [bXN3[kc], bWVb[kc]])
                op("vector", lambda e, p=p, vt2=vt2: e.tensor_copy(
                    AR(oVT3 + VW * 8 * vt2, VW * 8).rearrange(
                        "p (h w) -> p h w", w=VW)[:, :, 0:64],
                    ps[p][:, :].rearrange("p (h d) -> p h d", d=64)),
                   reads=[bp], writes=[bVT3])
            op("gpsimd", lambda e, tt=tt, lc4=lc4: e.dma_start(
                out=dVb.ap()[512 * tt + 128 * lc4:512 * tt + 128 * (lc4 + 1), :],
                in_=AR(oVT3, 1056)), reads=[bVT3], writes=[bdVb], dma=True)

    op("gpsimd", lambda e: e.dma_start(
        out=cbi.ap()[0:1, 0:D * 128].rearrange("o (r c) -> (o r) c", c=128),
        in_=dKb.ap()[:, T - 128:T]), reads=[bdKb], writes=[bcb], dma=True)
    op("gpsimd", lambda e: e.dma_start(
        out=cbi.ap()[0:1, D * 128:BND].rearrange("o (r c) -> (o r) c", c=NH * VW),
        in_=dVb.ap()[T - 128:T, :]), reads=[bdVb], writes=[bcb], dma=True)
    op("gpsimd", lambda e: e.collective_compute(
        "AllGather", ALU.bypass, replica_groups=[[0, 1], [2, 3], [4, 5], [6, 7]],
        ins=[cbi.ap().opt()], outs=[cbo.ap().opt()]),
       reads=[bcb], writes=[bcb])

    oWO, bWO = ar.take(0, 8192, NKC)
    oQW, bQW = ar.take(8192, 4096, 2)
    oKCb, bKCb = ar.take(12288, 2048, 2)
    oVCb, bVCb = ar.take(14336, 2112, 2)
    oEC, bEC = ar.take(16448, 512)
    oEP, bEP = ar.take(16960, 2048, 4)
    oOU, bOU = ar.take(19008, 4096, NKC)
    oDSb, bDSb = ar.take(23104, 512)
    oDRb, bDRb = ar.take(23616, 512)
    oDBb, bDBb = ar.take(24128, 512)
    oKBD, bKBD = ar.take(24640, 1024)
    oVBD, bVBD = ar.take(25664, 1056)
    oM4, bM4 = ar.take(26720, 1536)
    oDT, bDT = ar.take(28256, 256, 4)

    for q in range(4):
        op("vector", lambda e, q=q: e.tensor_copy(AR(oM4 + 128 * q, 128), MSKC[:, :]),
           reads=[bMSKC], writes=[bM4])
        op("vector", lambda e, q=q: e.tensor_copy(AR(oM4 + 512 + 128 * q, 128),
                                                  MSKP[:, :]),
           reads=[bMSKP], writes=[bM4])
        op("vector", lambda e, q=q: e.tensor_copy(AR(oM4 + 1024 + 128 * q, 128),
                                                  MSKB[:, :]),
           reads=[bMSKB], writes=[bM4])
    wsplit_load(oWO, bWO, wobT, oQW, (bQW[0], bQW[0]))
    for i in range(NKC):
        op("gpsimd", lambda e, i=i: e.dma_start(
            out=AR(oKBD + 128 * i, 128),
            in_=cbo.ap()[0:1, 128 * 128 * i:128 * 128 * (i + 1)].rearrange(
                "o (r c) -> (o r) c", c=128)), reads=[bcb], writes=[bKBD], dma=True)
    op("gpsimd", lambda e: e.dma_start(
        out=AR(oVBD, 1056),
        in_=cbo.ap()[0:1, D * 128:BND].rearrange("o (r c) -> (o r) c", c=NH * VW)),
       reads=[bcb], writes=[bVBD], dma=True)

    def b2_iter2(kc):
        last = (kc == NLC - 1)
        first = (kc == -1)
        if first:
            okk, bkk = oKBD, bKBD
            ovv, bvv = oVBD, bVBD
        else:
            okk, bkk = oKCb + 1024 * (kc % 2), bKCb[kc % 2]
            ovv, bvv = oVCb + 1056 * (kc % 2), bVCb[kc % 2]
            for i in range(NKC):
                op("sync", lambda e, i=i, kc=kc, okk=okk: e.dma_start(
                    out=AR(okk + 128 * i, 128),
                    in_=dKb.ap()[128 * i:128 * (i + 1), 128 * kc:128 * (kc + 1)]),
                   reads=[bdKb], writes=[bkk], dma=True)
            op("sync", lambda e, kc=kc, ovv=ovv: e.dma_start(
                out=AR(ovv, 1056), in_=dVb.ap()[128 * kc:128 * (kc + 1), :]),
               reads=[bdVb], writes=[bvv], dma=True)
        oqw, bqw = oQW + 2048 * ((kc + 1) % 2), bQW[(kc + 1) % 2]
        q0 = max(kc, 0) * 128
        qn = 128 if (first or last) else 256
        qoff = 128 if first else 0
        for i in range(NKC):
            op("sync", lambda e, i=i, oqw=oqw, q0=q0, qn=qn, qoff=qoff: e.dma_start(
                out=AR(oqw + 256 * i + qoff, qn),
                in_=dQb.ap()[128 * i:128 * (i + 1), q0:q0 + qn]),
               reads=[bdQb], writes=[bqw], dma=True)
        for w in range(4):
            for hh in range(4):
                h = 4 * w + hh
                p = 0 if hh < 2 else 1
                so = 256 * (hh % 2)
                po = 64 * (h % 2)
                op("tensor", lambda e, h=h, p=p, so=so, po=po, okk=okk, oqw=oqw,
                   qoff=qoff, qn=qn: e.matmul(
                    ps[p][:, so + qoff:so + qoff + qn],
                    AR(okk + 128 * (h // 2), 128)[po:po + 64, :],
                    AR(oqw + 256 * (h // 2) + qoff, qn)[po:po + 64, :],
                    start=True, stop=True), reads=[bkk, bqw], writes=[bps[p]])
            for part in range(2):
                if (part == 0 and first) or (part == 1 and last):
                    continue
                oE = oEC if part == 0 else oEP + 512 * w
                bE = bEC if part == 0 else bEP[w]
                for hh in range(4):
                    p = 0 if hh < 2 else 1
                    so = 256 * (hh % 2) + 128 * part
                    op("scalar", lambda e, p=p, so=so, hh=hh, oE=oE: e.activation(
                        AR(oE + 128 * hh, 128), ps[p][:, so:so + 128], AF.Exp,
                        scale=0.125), reads=[bps[p]], writes=[bE])
                mk = oM4 if part == 0 else (oM4 + 1024 if first else oM4 + 512)
                op("vector", lambda e, oE=oE, mk=mk: e.tensor_tensor(
                    AR(oE, 512), AR(oE, 512), AR(mk, 512), ALU.mult),
                   reads=[bM4], writes=[bE])
            if not first:
                pb = 3 + (w // 2)
                co = 256 * (w % 2)
                for hh in range(4):
                    h = 4 * w + hh
                    op("tensor", lambda e, h=h, hh=hh, pb=pb, co=co, ovv=ovv: e.matmul(
                        ps[pb][64 * (h % 2):64 * (h % 2) + 64,
                               co + 128 * (hh // 2):co + 128 * (hh // 2) + 128],
                        AR(ovv + VW * h, 64), AR(oEC + 128 * hh, 128),
                        start=True, stop=True), reads=[bvv, bEC], writes=[bps[pb]])
                for pr in range(2):
                    jc = 2 * w + pr
                    op("vector", lambda e, jc=jc, pb=pb, co=co, pr=pr, kc=kc: e.tensor_tensor(
                        AR(oOU + 512 * jc + 128 * (kc % 4), 128),
                        AR(oOU + 512 * jc + 128 * (kc % 4), 128),
                        ps[pb][:, co + 128 * pr:co + 128 * (pr + 1)], ALU.add),
                       reads=[bps[pb]], writes=[bOU[jc]])
                op("vector", lambda e: e.tensor_reduce(
                    AR(oDT, 16), AR(oEC, 512).rearrange("p (b x) -> p b x", x=32),
                    AX.X, ALU.add, apply_transpose=True), reads=[bEC], writes=[bDT[0]])
                op("vector", lambda e, w=w: e.tensor_tensor(
                    AR(oDT, 16), AR(oDT, 16), AR(oDT + 64 + 16 * w, 16), ALU.add),
                   reads=[bDT[1]], writes=[bDT[0]])
                op("tensor", lambda e: e.transpose(ps[2][0:16, 0:128], AR(oDT, 16),
                                                   IDN[:, :]),
                   reads=[bDT[0], bIDN], writes=[bps[2]])
                op("vector", lambda e: e.tensor_copy(SS2[0:16, 0:32],
                                                     ps[2][0:16, 0:32]),
                   reads=[bps[2]], writes=[bSS2])
                for g in range(1, 4):
                    op("vector", lambda e, g=g: e.tensor_tensor(
                        SS2[0:16, 0:32], SS2[0:16, 0:32],
                        ps[2][0:16, 32 * g:32 * (g + 1)], ALU.add),
                       reads=[bps[2]], writes=[bSS2])
                op("gpsimd", lambda e: e.dma_start(out=dTmp.ap()[0:16, :],
                                                   in_=SS2[0:16, 0:32]),
                   reads=[bSS2], writes=[bdTmp], dma=True)
                op("gpsimd", lambda e, w=w, kc=kc: e.dma_start(
                    out=dDb.ap().rearrange("h (l b4 i) -> h l b4 i", b4=4, i=32)[
                        4 * w:4 * (w + 1), kc, :, :],
                    in_=dTmp.ap()[0:16, :].rearrange("(h b4) i -> h b4 i", b4=4)),
                   reads=[bdTmp], writes=[bdDb], dma=True)
        if (not first) and kc % 4 == 3:
            tt = kc // 4
            op("sync", lambda e, tt=tt: e.dma_start(
                out=AR(oDSb, 512)[0:16, :], in_=dDb.ap()[:, 512 * tt:512 * (tt + 1)]),
               reads=[bdDb], writes=[bDSb], dma=True)
            op("vector", lambda e: e.reciprocal(AR(oDRb, 512)[0:16, :],
                                                AR(oDSb, 512)[0:16, :]),
               reads=[bDSb], writes=[bDRb])
            op("gpsimd", lambda e, tt=tt: e.dma_start(
                out=dDb.ap()[:, 512 * tt:512 * (tt + 1)], in_=AR(oDRb, 512)[0:16, :]),
               reads=[bDRb], writes=[bdDb], dma=True)
            for jc in range(NKC):
                op("gpsimd", lambda e, jc=jc, tt=tt: e.dma_start(
                    out=AR(oDBb, 512)[0:64, :],
                    in_=dDb.ap()[2 * jc:2 * jc + 1, 512 * tt:512 * (tt + 1)].broadcast_to([64, 512])),
                   reads=[bdDb], writes=[bDBb], dma=True)
                op("gpsimd", lambda e, jc=jc, tt=tt: e.dma_start(
                    out=AR(oDBb, 512)[64:128, :],
                    in_=dDb.ap()[2 * jc + 1:2 * jc + 2, 512 * tt:512 * (tt + 1)].broadcast_to([64, 512])),
                   reads=[bdDb], writes=[bDBb], dma=True)
                op("vector", lambda e, jc=jc: e.tensor_tensor(
                    TMP[:, :], AR(oOU + 512 * jc, 512), AR(oDBb, 512),
                    ALU.mult), reads=[bDBb, bOU[jc]], writes=[bTMP])
                ov = AR(oOU + 512 * jc, 512).bitcast(BF16)
                op("scalar", lambda e, ov=ov: e.activation(
                    ov[:, 0:512], TMP[:, :], AF.Identity, bias=0.0, scale=1.0),
                   reads=[bTMP], writes=[bOU[jc]])
                op("vector", lambda e, ov=ov: e.scalar_tensor_tensor(
                    ov[:, 512:1024], TMP[:, :], 1.0, ov[:, 0:512], ALU.mult,
                    ALU.subtract), reads=[bTMP], writes=[bOU[jc]])
            for jc in range(NKC):
                p = 0 if jc % 2 == 0 else 1
                for kk in range(NKC):
                    wv = AR(oWO + 1024 * kk, 1024).bitcast(BF16)
                    ov = AR(oOU + 512 * kk, 512).bitcast(BF16)
                    mm3(ps[p][:, :], bps[p],
                        wv[:, 128 * jc:128 * (jc + 1)],
                        wv[:, 1024 + 128 * jc:1024 + 128 * (jc + 1)],
                        ov[:, 0:512], ov[:, 512:1024],
                        kk == 0, kk == NKC - 1, [bWO[kk], bOU[kk]])
                op("vector", lambda e, jc=jc, tt=tt, p=p: e.tensor_tensor(
                    X[jc][:, 512 * tt:512 * (tt + 1)], X[jc][:, 512 * tt:512 * (tt + 1)],
                    ps[p][:, :], ALU.add), reads=[bps[p]], writes=[bX[jc]])
        if not last:
            for w in range(4):
                pb = 5 + (w // 2)
                co = 256 * (w % 2)
                for hh in range(4):
                    h = 4 * w + hh
                    op("tensor", lambda e, h=h, hh=hh, pb=pb, co=co, ovv=ovv, w=w: e.matmul(
                        ps[pb][64 * (h % 2):64 * (h % 2) + 64,
                               co + 128 * (hh // 2):co + 128 * (hh // 2) + 128],
                        AR(ovv + VW * h, 64), AR(oEP + 512 * w + 128 * hh, 128),
                        start=True, stop=True), reads=[bvv, bEP[w]], writes=[bps[pb]])
                for pr in range(2):
                    jc = 2 * w + pr
                    op("vector", lambda e, jc=jc, pb=pb, co=co, pr=pr, kc=kc: e.tensor_copy(
                        AR(oOU + 512 * jc + 128 * ((kc + 1) % 4), 128),
                        ps[pb][:, co + 128 * pr:co + 128 * (pr + 1)]),
                       reads=[bps[pb]], writes=[bOU[jc]])
                op("vector", lambda e, w=w: e.tensor_reduce(
                    AR(oDT + 64 + 16 * w, 16),
                    AR(oEP + 512 * w, 512).rearrange("p (b x) -> p b x", x=32),
                    AX.X, ALU.add, apply_transpose=True),
                   reads=[bEP[w]], writes=[bDT[1]])

    for kc in range(-1, NLC):
        b2_iter2(kc)

    if debug:
        for jc in range(NKC):
            op("gpsimd", lambda e, jc=jc: e.dma_start(out=dbgX3[128 * jc:128 * (jc + 1), :], in_=X[jc][:, :]), reads=[bX[jc]], writes=[Buf()], dma=True)
        op("gpsimd", lambda e: e.dma_start(out=dbgQb, in_=dQb.ap()), reads=[bdQb], writes=[Buf()], dma=True)
        op("gpsimd", lambda e: e.dma_start(out=dbgKb, in_=dKb.ap()), reads=[bdKb], writes=[Buf()], dma=True)
        op("gpsimd", lambda e: e.dma_start(out=dbgVb, in_=dVb.ap()), reads=[bdVb], writes=[Buf()], dma=True)
        op("gpsimd", lambda e: e.dma_start(out=dbgDb, in_=dDb.ap()), reads=[bdDb], writes=[Buf()], dma=True)
        op("gpsimd", lambda e: e.dma_start(out=dbgBND, in_=cbo.ap()), reads=[bcb], writes=[Buf()], dma=True)

    ffn_sublayer2(False, dRow2, bdRow2, cg2i, cg2o, bcg2)

    otoks = []
    for jc in range(NKC):
        tok = op("sync", lambda e, jc=jc: e.dma_start(
            out=outT[128 * jc:128 * (jc + 1), :], in_=X[jc][:, :]),
            reads=[bX[jc]], dma=True)
        otoks.append(tok)
    op("vector", lambda e: e.memset(SCR[0:1, 64:65], 0.0), writes=[bSCR], waits=otoks)

    with (nc.semaphore("t") as st_, nc.semaphore("v") as sv, nc.semaphore("s") as ss_,
          nc.semaphore("g") as sg, nc.semaphore("ld") as sld, nc.semaphore("sd") as ssd,
          nc.semaphore("gd") as sgd, nc.Block() as block):
        sems = {"t": st_, "v": sv, "s": ss_, "g": sg, "ld": sld, "sd": ssd, "gd": sgd}
        kb.realize(block, sems)
    return nc


_NC_CACHE = None


def make_inmaps(**inputs):
    x = np.asarray(inputs["L0_attn_a_x"], dtype=np.float32)

    def t(w):
        return np.ascontiguousarray(np.asarray(w, dtype=np.float32).T)

    def pp(v):
        return np.ascontiguousarray(np.asarray(v, dtype=np.float32).reshape(8, 128).T)

    shared = {
        "wqaT": t(inputs["L0_attn_a_Wq"]), "wkaT": t(inputs["L0_attn_a_Wk"]),
        "wvaT": t(inputs["L0_attn_a_Wv"]),
        "bqa": pp(inputs["L0_attn_a_bq"]), "bka": pp(inputs["L0_attn_a_bk"]),
        "bvaf": np.ascontiguousarray(np.asarray(inputs["L0_attn_a_bv"], np.float32).reshape(1, D)),
        "wffn": np.ascontiguousarray(np.asarray(inputs["L0_ffn_weight"], np.float32)),
        "gffn": pp(inputs["L0_ffn_ln_gamma"]), "bffn": pp(inputs["L0_ffn_ln_beta"]),
        "wqbT": t(inputs["L0_attn_b_Wq"]), "wkbT": t(inputs["L0_attn_b_Wk"]),
        "wvbT": t(inputs["L0_attn_b_Wv"]), "wobT": t(inputs["L0_attn_b_Wo"]),
    }
    in_maps = []
    for c in range(8):
        b, hf = c // 2, c % 2
        m = dict(shared)
        m["xT"] = np.ascontiguousarray(x[b, 2048 * hf:2048 * (hf + 1), :].T)
        m["flagT"] = np.full((128, 1), float(hf), np.float32)
        in_maps.append(m)
    return in_maps


def kernel(L0_attn_a_x, L0_attn_a_Wq, L0_attn_a_bq, L0_attn_a_Wk, L0_attn_a_bk,
           L0_attn_a_Wv, L0_attn_a_bv, L0_attn_b_Wq, L0_attn_b_Wk, L0_attn_b_Wv,
           L0_attn_b_Wo, L0_ffn_weight, L0_ffn_ln_gamma, L0_ffn_ln_beta):
    global _NC_CACHE
    if _NC_CACHE is None:
        _NC_CACHE = build_program()
    nc = _NC_CACHE

    x = np.asarray(L0_attn_a_x, dtype=np.float32)

    def t(w):
        return np.ascontiguousarray(np.asarray(w, dtype=np.float32).T)

    def pp(v):
        return np.ascontiguousarray(np.asarray(v, dtype=np.float32).reshape(8, 128).T)

    shared = {
        "wqaT": t(L0_attn_a_Wq), "wkaT": t(L0_attn_a_Wk), "wvaT": t(L0_attn_a_Wv),
        "bqa": pp(L0_attn_a_bq), "bka": pp(L0_attn_a_bk),
        "bvaf": np.ascontiguousarray(np.asarray(L0_attn_a_bv, np.float32).reshape(1, D)),
        "wffn": np.ascontiguousarray(np.asarray(L0_ffn_weight, np.float32)),
        "gffn": pp(L0_ffn_ln_gamma), "bffn": pp(L0_ffn_ln_beta),
        "wqbT": t(L0_attn_b_Wq), "wkbT": t(L0_attn_b_Wk), "wvbT": t(L0_attn_b_Wv),
        "wobT": t(L0_attn_b_Wo),
    }
    in_maps = []
    for c in range(8):
        b, hf = c // 2, c % 2
        m = dict(shared)
        m["xT"] = np.ascontiguousarray(x[b, 2048 * hf:2048 * (hf + 1), :].T)
        m["flagT"] = np.full((128, 1), float(hf), np.float32)
        in_maps.append(m)
    res = run_bass_kernel_spmd(nc, in_maps, core_ids=list(range(8)))
    out = np.empty((4, 4096, 1024), np.float32)
    for c in range(8):
        b, hf = c // 2, c % 2
        out[b, 2048 * hf:2048 * (hf + 1), :] = res.results[c]["outT"].T
    return out


# version 08
# speedup vs baseline: 1.4147x; 1.4147x over previous
import math
import numpy as np
import concourse.bass as bass
import concourse.mybir as mybir
from concourse.bass_utils import run_bass_kernel_spmd

F32 = mybir.dt.float32
BF16 = mybir.dt.bfloat16
I32 = mybir.dt.int32
ALU = mybir.AluOpType
AF = mybir.ActivationFunctionType
AX = mybir.AxisListType

D = 1024
T = 2048
NH = 16
HD = 64
NKC = 8
NTT = 4
NLC = 16
VW = 66
QSCALE = 1.0 / math.sqrt(math.sqrt(HD))

ENGS = ("sync", "scalar", "vector", "tensor", "gpsimd")
ESEM = {"sync": (None, "ld"), "scalar": ("s", "sd"), "vector": ("v", None),
        "tensor": ("t", None), "gpsimd": ("g", "gd")}


class Buf:
    __slots__ = ("w", "rs")

    def __init__(self):
        self.w = None
        self.rs = {}

    def war(self):
        out = list(self.rs.items())
        if self.w is not None:
            out.append(self.w)
        return out


class KB:

    def __init__(self):
        self.code = {e: [] for e in ENGS}
        self.cnt = {"t": 0, "v": 0, "s": 0, "g": 0, "ld": 0, "sd": 0, "gd": 0}
        self.step = {"t": 1, "v": 1, "s": 1, "g": 1, "ld": 16, "sd": 16, "gd": 16}

    def op(self, eng, fn, reads=(), writes=(), dma=False, waits=()):
        sem = ESEM[eng][1 if dma else 0]
        wl = []
        for b in reads:
            if b.w is not None:
                wl.append(b.w)
        for b in writes:
            wl.extend(b.war())
        wl.extend(waits)
        self.cnt[sem] += self.step[sem]
        tok = (sem, self.cnt[sem])
        self.code[eng].append((tuple(wl), fn, sem, self.step[sem]))
        for b in writes:
            b.w = tok
            b.rs = {}
        for b in reads:
            s, vv = tok
            if b.rs.get(s, 0) < vv:
                b.rs[s] = vv
        return tok

    def opchain(self, eng, fns, reads=(), writes=(), waits=()):
        sem = ESEM[eng][0]
        wl = []
        for b in reads:
            if b.w is not None:
                wl.append(b.w)
        for b in writes:
            wl.extend(b.war())
        wl.extend(waits)
        self.cnt[sem] += self.step[sem]
        tok = (sem, self.cnt[sem])
        for i, f in enumerate(fns):
            self.code[eng].append((tuple(wl) if i == 0 else (), f,
                                   sem if i == len(fns) - 1 else None,
                                   self.step[sem]))
        for b in writes:
            b.w = tok
            b.rs = {}
        for b in reads:
            s, vv = tok
            if b.rs.get(s, 0) < vv:
                b.rs[s] = vv
        return tok

    def realize(self, block, sems):
        for eng in ENGS:
            code = self.code[eng]

            def emit(e, code=code):
                waited = {}
                for wl, fn, sem, stp in code:
                    for (s, v) in wl:
                        if waited.get(s, 0) < v:
                            e.wait_ge(sems[s], v)
                            waited[s] = v
                    r = fn(e)
                    if sem is not None:
                        r.then_inc(sems[sem], stp)
            getattr(block, eng)(emit)


class Arena:

    def __init__(self, t):
        self.t = t
        self.tenants = []

    def take(self, start, n, parts=1):
        end = start + n
        pend = {}
        for (s0, e0, bl) in self.tenants:
            if s0 < end and start < e0:
                for b in bl:
                    for (sm, vv) in b.war():
                        if pend.get(sm, 0) < vv:
                            pend[sm] = vv
        bufs = []
        for _ in range(parts):
            b = Buf()
            b.rs = dict(pend)
            bufs.append(b)
        self.tenants.append((start, end, bufs))
        if parts == 1:
            return start, bufs[0]
        return start, bufs


def build_program(debug=False):
    nc = bass.Bass("TRN2", target_bir_lowering=False)

    xT = nc.dram_tensor("xT", [D, T], F32, kind="ExternalInput").ap()
    wqaT = nc.dram_tensor("wqaT", [D, D], F32, kind="ExternalInput").ap()
    wkaT = nc.dram_tensor("wkaT", [D, D], F32, kind="ExternalInput").ap()
    wvaT = nc.dram_tensor("wvaT", [D, D], F32, kind="ExternalInput").ap()
    bqa = nc.dram_tensor("bqa", [128, 8], F32, kind="ExternalInput").ap()
    bka = nc.dram_tensor("bka", [128, 8], F32, kind="ExternalInput").ap()
    bvaf = nc.dram_tensor("bvaf", [1, D], F32, kind="ExternalInput").ap()
    wffn = nc.dram_tensor("wffn", [D, D], F32, kind="ExternalInput").ap()
    gffn = nc.dram_tensor("gffn", [128, 8], F32, kind="ExternalInput").ap()
    bffn = nc.dram_tensor("bffn", [128, 8], F32, kind="ExternalInput").ap()
    wqbT = nc.dram_tensor("wqbT", [D, D], F32, kind="ExternalInput").ap()
    wkbT = nc.dram_tensor("wkbT", [D, D], F32, kind="ExternalInput").ap()
    wvbT = nc.dram_tensor("wvbT", [D, D], F32, kind="ExternalInput").ap()
    wobT = nc.dram_tensor("wobT", [D, D], F32, kind="ExternalInput").ap()
    flagT = nc.dram_tensor("flagT", [128, 1], F32, kind="ExternalInput").ap()
    outT = nc.dram_tensor("outT", [D, T], F32, kind="ExternalOutput").ap()

    dQ = nc.dram_tensor("dQ", [D, T], F32)
    dK = nc.dram_tensor("dK", [D, T], F32)
    dV = nc.dram_tensor("dV", [T, NH * VW], F32)
    dQb = nc.dram_tensor("dQb", [D, T], F32)
    dKb = nc.dram_tensor("dKb", [D, T], F32)
    dVb = nc.dram_tensor("dVb", [T, NH * VW], F32)
    dWq = nc.dram_tensor("dWq", [D, D], BF16)
    dRow = nc.dram_tensor("dRow", [2, T], F32)
    dRow2 = nc.dram_tensor("dRow2", [2, T], F32)
    dD1 = nc.dram_tensor("dD1", [NH, T], F32)
    dDb = nc.dram_tensor("dDb", [NH, T], F32)
    dTmp = nc.dram_tensor("dTmp", [64, 32], F32)
    cwi = nc.dram_tensor("cwi", [8, 1], F32)
    cwo = nc.dram_tensor("cwo", [8, 1], F32)
    cli = nc.dram_tensor("cli", [2 * D, 64], F32)
    clo = nc.dram_tensor("clo", [4 * D, 64], F32)
    c2i = nc.dram_tensor("c2i", [1, 1], F32)
    c2o = nc.dram_tensor("c2o", [1, 1], F32)
    c3i = nc.dram_tensor("c3i", [NH * 128, VW], F32)
    c3o = nc.dram_tensor("c3o", [NH * 128, VW], F32)
    cgi = nc.dram_tensor("cgi", [1, 1], F32)
    cgo = nc.dram_tensor("cgo", [1, 1], F32)
    cg2i = nc.dram_tensor("cg2i", [1, 1], F32)
    cg2o = nc.dram_tensor("cg2o", [1, 1], F32)
    BND = D * 128 + 128 * NH * VW
    cbi = nc.dram_tensor("cbi", [1, BND], F32)
    cbo = nc.dram_tensor("cbo", [2, BND], F32)
    if debug:
        dbgQ = nc.dram_tensor("dbgQ", [D, T], F32, kind="ExternalOutput").ap()
        dbgK = nc.dram_tensor("dbgK", [D, T], F32, kind="ExternalOutput").ap()
        dbgV = nc.dram_tensor("dbgV", [T, NH * VW], F32, kind="ExternalOutput").ap()
        dbgLM = nc.dram_tensor("dbgLM", [2, 1024], F32, kind="ExternalOutput").ap()
        dbgP2 = nc.dram_tensor("dbgP2", [128, NH * 128], F32, kind="ExternalOutput").ap()
        dbgVI = nc.dram_tensor("dbgVI", [128, NH * 128], F32, kind="ExternalOutput").ap()
        dbgG0 = nc.dram_tensor("dbgG0", [128, NH * VW], F32, kind="ExternalOutput").ap()
        dbgGG = nc.dram_tensor("dbgGG", [128, NH * 64], F32, kind="ExternalOutput").ap()
        dbgD1 = nc.dram_tensor("dbgD1", [NH, T], F32, kind="ExternalOutput").ap()
        dbgX1 = nc.dram_tensor("dbgX1", [D, T], F32, kind="ExternalOutput").ap()
        dbgX2 = nc.dram_tensor("dbgX2", [D, T], F32, kind="ExternalOutput").ap()
        dbgX3 = nc.dram_tensor("dbgX3", [D, T], F32, kind="ExternalOutput").ap()
        dbgSC = nc.dram_tensor("dbgSC", [128, 16], F32, kind="ExternalOutput").ap()
        dbgQb = nc.dram_tensor("dbgQb", [D, T], F32, kind="ExternalOutput").ap()
        dbgKb = nc.dram_tensor("dbgKb", [D, T], F32, kind="ExternalOutput").ap()
        dbgVb = nc.dram_tensor("dbgVb", [T, NH * VW], F32, kind="ExternalOutput").ap()
        dbgDb = nc.dram_tensor("dbgDb", [NH, T], F32, kind="ExternalOutput").ap()
        dbgBND = nc.dram_tensor("dbgBND", [2, BND], F32, kind="ExternalOutput").ap()
        dbgG0p = nc.dram_tensor("dbgG0p", [128, NH * VW], F32, kind="ExternalOutput").ap()
        dbgV1 = nc.dram_tensor("dbgV1", [T, NH * VW], F32, kind="ExternalOutput").ap()
        dbgV2 = nc.dram_tensor("dbgV2", [T, NH * VW], F32, kind="ExternalOutput").ap()
        dbgIT = nc.dram_tensor("dbgIT", [6, 128, NH * 128], F32, kind="ExternalOutput").ap()
        dbgYT = nc.dram_tensor("dbgYT", [6, 128, NH * 128], F32, kind="ExternalOutput").ap()
        dbgE3 = nc.dram_tensor("dbgE3", [128, NH * 128], F32, kind="ExternalOutput").ap()


    A = lambda name, shp, dt=F32: nc.alloc_sbuf_tensor(name, shp, dt)
    X = [A(f"X{i}", [128, T]) for i in range(NKC)]
    TMP = A("TMP", [128, 512])
    BCM = A("BCM", [128, 512])
    BCR = A("BCR", [128, 512])
    IDN = A("IDN", [128, 128])
    IOI = A("IOI", [128, 128], I32)
    IOF = A("IOF", [128, 128])
    IOP = A("IOP", [128, 1], I32)
    IOPF = A("IOPF", [128, 1])
    MSKC = A("MSKC", [128, 128])
    MSKP = A("MSKP", [128, 128])
    MSKB = A("MSKB", [128, 128])
    FLG = A("FLG", [128, 1])
    SC1 = A("SC1", [128, 16])
    BB = A("BB", [128, 16])
    SS = A("SS", [128, 64])
    SS2 = A("SS2", [64, 128])
    ST = A("ST", [64, 128])
    SCR = A("SCR", [1, 128])
    BQA = A("BQA", [128, 8])
    BKA = A("BKA", [128, 8])
    GFF = A("GFF", [128, 8])
    BFF = A("BFF", [128, 8])
    ARN = A("ARN", [128, 33800])
    ar = Arena(ARN)

    ps = [nc.alloc_psum_tensor(f"ps{i}", [128, 512], F32) for i in range(8)]
    bps = [Buf() for _ in range(8)]

    kb = KB()
    op = kb.op

    bX = [Buf() for _ in range(NKC)]
    bTMP, bBCM, bBCR = Buf(), Buf(), Buf()
    bIDN, bIOI, bIOF, bIOP, bIOPF = Buf(), Buf(), Buf(), Buf(), Buf()
    bMSKC, bMSKP, bMSKB, bFLG = Buf(), Buf(), Buf(), Buf()
    bSC1, bBB, bSS, bSS2, bST, bSCR = Buf(), Buf(), Buf(), Buf(), Buf(), Buf()
    bBQA, bBKA, bGFF, bBFF = Buf(), Buf(), Buf(), Buf()
    bdQ, bdK, bdV, bdQb, bdKb, bdVb, bdWq = (Buf(), Buf(), Buf(), Buf(), Buf(),
                                             Buf(), Buf())
    bdRow, bdRow2, bdD1, bdDb, bdTmp = Buf(), Buf(), Buf(), Buf(), Buf()
    bcw, bcl, bc2, bc3, bcg, bcg2, bcb = (Buf(), Buf(), Buf(), Buf(), Buf(),
                                          Buf(), Buf())

    def AR(off, n):
        return ARN[:, off:off + n]

    op("sync", lambda e: e.dma_start(out=FLG[:, :], in_=flagT), writes=[bFLG], dma=True)
    op("gpsimd", lambda e: e.dma_start(out=cwi.ap(), in_=flagT[0:8, 0:1]),
       reads=[bFLG], writes=[bcw], dma=True)
    op("gpsimd", lambda e: e.collective_compute(
        "AllReduce", ALU.max, replica_groups=[list(range(8))],
        ins=[cwi.ap().opt()], outs=[cwo.ap().opt()]), reads=[bcw], writes=[bcw])
    for i in range(NKC):
        op("sync", lambda e, i=i: e.dma_start(out=X[i][:, :], in_=xT[128 * i:128 * (i + 1), :]),
           writes=[bX[i]], dma=True)
    op("sync", lambda e: e.dma_start(out=BQA[:, :], in_=bqa), writes=[bBQA], dma=True)
    op("sync", lambda e: e.dma_start(out=BKA[:, :], in_=bka), writes=[bBKA], dma=True)
    op("sync", lambda e: e.dma_start(out=GFF[:, :], in_=gffn), writes=[bGFF], dma=True)
    op("sync", lambda e: e.dma_start(out=BFF[:, :], in_=bffn), writes=[bBFF], dma=True)
    op("gpsimd", lambda e: e.iota(IOI[:, :], [[1, 128]], base=0, channel_multiplier=0),
       writes=[bIOI])
    op("gpsimd", lambda e: e.iota(IOP[:, :], [[0, 1]], base=0, channel_multiplier=1,
                                  allow_small_or_imprecise_dtypes=True), writes=[bIOP])
    op("vector", lambda e: e.tensor_copy(IOF[:, :], IOI[:, :]), reads=[bIOI], writes=[bIOF])
    op("vector", lambda e: e.tensor_copy(IOPF[:, :], IOP[:, :]), reads=[bIOP], writes=[bIOPF])
    op("vector", lambda e: e.tensor_scalar(IDN[:, :], IOF[:, :], IOPF[:, 0:1], None,
                                           ALU.is_equal), reads=[bIOF, bIOPF], writes=[bIDN])
    op("vector", lambda e: e.tensor_scalar(MSKC[:, :], IOF[:, :], IOPF[:, 0:1], None,
                                           ALU.is_ge), reads=[bIOF, bIOPF], writes=[bMSKC])
    op("vector", lambda e: e.tensor_scalar(MSKP[:, :], IOF[:, :], IOPF[:, 0:1], None,
                                           ALU.is_lt), reads=[bIOF, bIOPF], writes=[bMSKP])
    op("vector", lambda e: e.tensor_scalar(MSKB[:, :], MSKP[:, :], FLG[:, 0:1], None,
                                           ALU.mult), reads=[bMSKP, bFLG], writes=[bMSKB])

    def wsplit_load(oW, bW, hbm, tmpo, btmp):
        for k in range(NKC):
            wv = AR(oW + 1024 * k, 1024).bitcast(BF16)
            tb = btmp[k % 2]
            op("gpsimd", lambda e, k=k, wv=wv, hbm=hbm: e.dma_start(
                out=wv[:, 0:1024], in_=hbm[128 * k:128 * (k + 1), :]),
               writes=[bW[k]], dma=True)
            op("sync", lambda e, k=k, tmpo=tmpo: e.dma_start(
                out=AR(tmpo + 1024 * (k % 2), 1024), in_=hbm[128 * k:128 * (k + 1), :]),
               reads=[], writes=[tb], dma=True)
            op("vector", lambda e, k=k, wv=wv, tmpo=tmpo: e.scalar_tensor_tensor(
                wv[:, 1024:2048], AR(tmpo + 1024 * (k % 2), 1024), 1.0, wv[:, 0:1024],
                ALU.mult, ALU.subtract), reads=[tb], writes=[bW[k]])

    def mm3fns(eng_ps, lh, ll, rh, rl, first, last):
        return [
            (lambda e, a_=a_, b_=b_, i2=i2: e.matmul(
                eng_ps, a_, b_, start=(first and i2 == 0),
                stop=(last and i2 == 2)))
            for i2, (a_, b_) in enumerate(((lh, rh), (lh, rl), (ll, rh)))]


    def ln_stats(tt, src, src_bufs, rowd, browd, kind):
        for i in range(NKC):
            op("scalar", lambda e, i=i: e.activation(TMP[:, :], src(i), AF.Square),
               reads=[src_bufs[i]], writes=[bTMP])
            o0 = 0 if i == 0 else 32
            op("vector", lambda e, i=i, o0=o0: e.tensor_reduce(
                SS[:, o0:o0 + 16], src(i).rearrange("p (b x) -> p b x", x=32),
                AX.X, ALU.add, apply_transpose=True), reads=[src_bufs[i]], writes=[bSS])
            op("vector", lambda e, o0=o0: e.tensor_reduce(
                SS[:, o0 + 16:o0 + 32], TMP[:, :].rearrange("p (b x) -> p b x", x=32),
                AX.X, ALU.add, apply_transpose=True), reads=[bTMP], writes=[bSS])
            if i > 0:
                op("vector", lambda e: e.tensor_tensor(SS[:, 0:32], SS[:, 0:32],
                                                       SS[:, 32:64], ALU.add),
                   reads=[bSS], writes=[bSS])
        op("tensor", lambda e: e.transpose(ps[7][0:16, 0:128], SS[:, 0:16], IDN[:, :]),
           reads=[bSS, bIDN], writes=[bps[7]])
        op("tensor", lambda e: e.transpose(ps[7][0:16, 128:256], SS[:, 16:32],
                                           IDN[:, :]), reads=[bSS, bIDN],
           writes=[bps[7]])
        op("vector", lambda e: e.tensor_copy(SS2[0:16, 0:32], ps[7][0:16, 0:32]),
           reads=[bps[7]], writes=[bSS2])
        op("vector", lambda e: e.tensor_copy(SS2[0:16, 32:64], ps[7][0:16, 128:160]),
           reads=[bps[7]], writes=[bSS2])
        for g in range(1, 4):
            op("vector", lambda e, g=g: e.tensor_tensor(
                SS2[0:16, 0:32], SS2[0:16, 0:32], ps[7][0:16, 32 * g:32 * (g + 1)],
                ALU.add), reads=[bps[7]], writes=[bSS2])
            op("vector", lambda e, g=g: e.tensor_tensor(
                SS2[0:16, 32:64], SS2[0:16, 32:64],
                ps[7][0:16, 128 + 32 * g:128 + 32 * (g + 1)],
                ALU.add), reads=[bps[7]], writes=[bSS2])
        op("vector", lambda e: e.tensor_scalar(ST[0:16, 0:32], SS2[0:16, 0:32],
                                               1.0 / D, None, ALU.mult),
           reads=[bSS2], writes=[bST])
        op("scalar", lambda e: e.activation(ST[0:16, 64:96], ST[0:16, 0:32], AF.Square),
           reads=[bST], writes=[bST])
        op("vector", lambda e: e.scalar_tensor_tensor(
            ST[0:16, 96:128], SS2[0:16, 32:64], 1.0 / D, ST[0:16, 64:96],
            ALU.mult, ALU.subtract), reads=[bSS2, bST], writes=[bST])
        if kind == "plain":
            op("vector", lambda e: e.tensor_scalar(ST[0:16, 64:96], ST[0:16, 96:128],
                                                   1.0, 1e-5, ALU.mult, ALU.add),
               reads=[bST], writes=[bST])
        else:
            op("vector", lambda e: e.tensor_scalar(ST[0:16, 64:96], ST[0:16, 96:128],
                                                   1.001, 1e-8, ALU.mult, ALU.add),
               reads=[bST], writes=[bST])
        op("scalar", lambda e: e.activation(ST[0:16, 96:128], ST[0:16, 64:96], AF.Sqrt),
           reads=[bST], writes=[bST])
        op("vector", lambda e: e.reciprocal(ST[0:16, 64:96], ST[0:16, 96:128]),
           reads=[bST], writes=[bST])
        rb = rowd.ap()
        op("gpsimd", lambda e: e.dma_start(
            out=rb[0:1, 512 * tt:512 * (tt + 1)].rearrange("o (b i) -> (o b) i", b=16),
            in_=ST[0:16, 0:32]), reads=[bST], writes=[browd], dma=True)
        op("gpsimd", lambda e: e.dma_start(
            out=rb[1:2, 512 * tt:512 * (tt + 1)].rearrange("o (b i) -> (o b) i", b=16),
            in_=ST[0:16, 64:96]), reads=[bST], writes=[browd], dma=True)
        op("gpsimd", lambda e: e.dma_start(
            out=BCM[:, :], in_=rb[0:1, 512 * tt:512 * (tt + 1)].broadcast_to([128, 512])),
           reads=[browd], writes=[bBCM], dma=True)
        op("gpsimd", lambda e: e.dma_start(
            out=BCR[:, :], in_=rb[1:2, 512 * tt:512 * (tt + 1)].broadcast_to([128, 512])),
           reads=[browd], writes=[bBCR], dma=True)

    def xn_chunk(dst, bdst, i, tt, affine):
        op("vector", lambda e, i=i: e.tensor_tensor(
            TMP[:, :], X[i][:, 512 * tt:512 * (tt + 1)], BCM[:, :], ALU.subtract),
           reads=[bX[i], bBCM], writes=[bTMP])
        if not affine:
            op("vector", lambda e: e.tensor_tensor(dst, TMP[:, :], BCR[:, :], ALU.mult),
               reads=[bTMP, bBCR], writes=[bdst])
        else:
            op("vector", lambda e: e.tensor_tensor(TMP[:, :], TMP[:, :], BCR[:, :],
                                                   ALU.mult), reads=[bTMP, bBCR],
               writes=[bTMP])
            op("vector", lambda e, i=i: e.tensor_scalar(
                dst, TMP[:, :], GFF[:, i:i + 1], BFF[:, i:i + 1], ALU.mult, ALU.add),
               reads=[bTMP, bGFF, bBFF], writes=[bdst])

    def xn_chunk_split(oslot, bdst, i, tt):
        xv = AR(oslot, 512).bitcast(BF16)
        op("vector", lambda e, i=i: e.tensor_tensor(
            TMP[:, :], X[i][:, 512 * tt:512 * (tt + 1)], BCM[:, :], ALU.subtract),
           reads=[bX[i], bBCM], writes=[bTMP])
        op("vector", lambda e: e.tensor_tensor(TMP[:, :], TMP[:, :], BCR[:, :],
                                               ALU.mult), reads=[bTMP, bBCR],
           writes=[bTMP])
        op("scalar", lambda e, xv=xv: e.activation(xv[:, 0:512], TMP[:, :], AF.Identity,
                                                   bias=0.0, scale=1.0),
           reads=[bTMP], writes=[bdst])
        op("vector", lambda e, xv=xv: e.scalar_tensor_tensor(
            xv[:, 512:1024], TMP[:, :], 1.0, xv[:, 0:512], ALU.mult, ALU.subtract),
           reads=[bTMP], writes=[bdst])

    def partition_max_to_scalar(colap, bcol, dsti, bdsti):
        op("tensor", lambda e: e.transpose(ps[7][0:1, 0:128], colap, IDN[:, :]),
           reads=[bcol, bIDN], writes=[bps[7]])
        op("vector", lambda e: e.reduce_max(SCR[0:1, 0:1], ps[7][0:1, 0:128], AX.X), reads=[bps[7]], writes=[bSCR])
        op("gpsimd", lambda e: e.dma_start(out=dsti.ap(), in_=SCR[0:1, 0:1]),
           reads=[bSCR], writes=[bdsti], dma=True)

    oWQ, bWQ = ar.take(0, 8192, NKC)
    oWK, bWK = ar.take(8192, 8192, NKC)
    oWV, bWV = ar.take(16384, 8192, NKC)
    oXN, bXN = ar.take(24576, 4096, NKC)
    oEV, bEV = ar.take(28672, 1024, 2)
    oVT, bVT = ar.take(29696, 1056)
    oQLM, bQLM = ar.take(30752, 512)
    oKLM, bKLM = ar.take(31264, 512)
    oBVB, bBVB = ar.take(31776, 1024)

    wsplit_load(oWQ, bWQ, wqaT, oXN, (bXN[0], bXN[2]))
    wsplit_load(oWK, bWK, wkaT, oXN, (bXN[0], bXN[2]))
    wsplit_load(oWV, bWV, wvaT, oXN, (bXN[0], bXN[2]))
    op("gpsimd", lambda e: e.dma_start(out=AR(oBVB, 1024),
                                       in_=bvaf.broadcast_to([128, D])),
       writes=[bBVB], dma=True)
    op("vector", lambda e: e.memset(AR(oVT, 1056), 1.0), writes=[bVT])

    for tt in range(NTT):
        ln_stats(tt, lambda i, tt=tt: X[i][:, 512 * tt:512 * (tt + 1)], bX,
                 dRow, bdRow, "plain")
        for i in range(NKC):
            xn_chunk_split(oXN + 512 * i, bXN[i], i, tt)
        for mi, (oW, bW, BI, bBI, dspill, bdspill, oLM, bLM) in enumerate((
                (oWQ, bWQ, BQA, bBQA, dQ, bdQ, oQLM, bQLM),
                (oWK, bWK, BKA, bBKA, dK, bdK, oKLM, bKLM))):
            for jc in range(NKC):
                p, bp = (0, bps[0]) if jc % 2 == 0 else (1, bps[1])
                ev, bev = (oEV + 512 * (jc % 2)), bEV[jc % 2]
                fns = []
                for kc in range(NKC):
                    wv = AR(oW + 1024 * kc, 1024).bitcast(BF16)
                    xv = AR(oXN + 512 * kc, 512).bitcast(BF16)
                    fns += mm3fns(ps[p][:, :],
                                  wv[:, 128 * jc:128 * (jc + 1)],
                                  wv[:, 1024 + 128 * jc:1024 + 128 * (jc + 1)],
                                  xv[:, 0:512], xv[:, 512:1024],
                                  kc == 0, kc == NKC - 1)
                kb.opchain("tensor", fns, reads=list(bW) + list(bXN), writes=[bp])
                if tt == 0 and jc == 0:
                    op("vector", lambda e, BI=BI, mi=mi: e.tensor_scalar(
                        BB[:, 8 * mi:8 * mi + 8], BI[:, :], QSCALE, None, ALU.mult),
                       reads=[bBI], writes=[bBB])
                op("scalar", lambda e, p=p, ev=ev, jc=jc, mi=mi: e.activation(
                    AR(ev, 512), ps[p][:, :], AF.Identity,
                    bias=BB[:, 8 * mi + jc:8 * mi + jc + 1], scale=QSCALE),
                   reads=[bp, bBB], writes=[bev])
                op("vector", lambda e, ev=ev, jc=jc, tt=tt, oLM=oLM: e.tensor_reduce(
                    AR(oLM + 64 * jc + 16 * tt, 16),
                    AR(ev, 512).rearrange("p (s g) -> p s g", g=32), AX.X, ALU.add),
                   reads=[bev], writes=[bLM])
                op("gpsimd", lambda e, ev=ev, jc=jc, tt=tt, dspill=dspill: e.dma_start(
                    out=dspill.ap()[128 * jc:128 * (jc + 1), 512 * tt:512 * (tt + 1)],
                    in_=AR(ev, 512)), reads=[bev], writes=[bdspill], dma=True)
        for lc4 in range(4):
            for vt2 in range(2):
                p, bp = (2, bps[2]) if vt2 == 0 else (3, bps[3])
                fns = []
                for kc in range(NKC):
                    wv = AR(oWV + 1024 * kc, 1024).bitcast(BF16)
                    xv = AR(oXN + 512 * kc, 512).bitcast(BF16)
                    fns += mm3fns(ps[p][:, :],
                                  xv[:, 128 * lc4:128 * (lc4 + 1)],
                                  xv[:, 512 + 128 * lc4:512 + 128 * (lc4 + 1)],
                                  wv[:, 512 * vt2:512 * (vt2 + 1)],
                                  wv[:, 1024 + 512 * vt2:1024 + 512 * (vt2 + 1)],
                                  kc == 0, kc == NKC - 1)
                kb.opchain("tensor", fns, reads=list(bXN) + list(bWV), writes=[bp])
                op("vector", lambda e, p=p, vt2=vt2: e.tensor_tensor(
                    AR(oVT + VW * 8 * vt2, VW * 8).rearrange("p (h w) -> p h w", w=VW)[:, :, 0:64],
                    ps[p][:, :].rearrange("p (h d) -> p h d", d=64),
                    AR(oBVB + 512 * vt2, 512).rearrange("p (h d) -> p h d", d=64),
                    ALU.add), reads=[bp, bBVB], writes=[bVT])
            op("gpsimd", lambda e, tt=tt, lc4=lc4: e.dma_start(
                out=dV.ap()[512 * tt + 128 * lc4:512 * tt + 128 * (lc4 + 1), :],
                in_=AR(oVT, 1056)), reads=[bVT], writes=[bdV], dma=True)

    if debug:
        op("gpsimd", lambda e: e.dma_start(out=dbgV1, in_=dV.ap()), reads=[bdV],
           writes=[Buf()], dma=True)

    for oLM, bLM in ((oQLM, bQLM), (oKLM, bKLM)):
        op("vector", lambda e, oLM=oLM: e.tensor_scalar(AR(oLM, 512), AR(oLM, 512),
                                                        1.0 / 32, None, ALU.mult),
           reads=[bLM], writes=[bLM])
    for i in range(NKC):
        op("gpsimd", lambda e, i=i: e.dma_start(
            out=cli.ap()[128 * i:128 * (i + 1), :], in_=AR(oQLM + 64 * i, 64)),
           reads=[bQLM], writes=[bcl], dma=True)
        op("gpsimd", lambda e, i=i: e.dma_start(
            out=cli.ap()[D + 128 * i:D + 128 * (i + 1), :], in_=AR(oKLM + 64 * i, 64)),
           reads=[bKLM], writes=[bcl], dma=True)
    op("gpsimd", lambda e: e.collective_compute(
        "AllGather", ALU.bypass, replica_groups=[[0, 1], [2, 3], [4, 5], [6, 7]],
        ins=[cli.ap().opt()], outs=[clo.ap().opt()]),
       reads=[bcl], writes=[bcl], waits=[bcw.w])

    oQF, bQF = ar.take(0, 1024)
    oKF, bKF = ar.take(1024, 1024)
    for i in range(NKC):
        op("gpsimd", lambda e, i=i: e.dma_start(
            out=AR(oQF + 128 * i, 64), in_=clo.ap()[128 * i:128 * (i + 1), :]),
           reads=[bcl], writes=[bQF], dma=True)
        op("gpsimd", lambda e, i=i: e.dma_start(
            out=AR(oQF + 128 * i + 64, 64),
            in_=clo.ap()[2 * D + 128 * i:2 * D + 128 * (i + 1), :]),
           reads=[bcl], writes=[bQF], dma=True)
        op("gpsimd", lambda e, i=i: e.dma_start(
            out=AR(oKF + 128 * i, 64), in_=clo.ap()[D + 128 * i:D + 128 * (i + 1), :]),
           reads=[bcl], writes=[bKF], dma=True)
        op("gpsimd", lambda e, i=i: e.dma_start(
            out=AR(oKF + 128 * i + 64, 64),
            in_=clo.ap()[3 * D + 128 * i:3 * D + 128 * (i + 1), :]),
           reads=[bcl], writes=[bKF], dma=True)

    if debug:
        op("gpsimd", lambda e: e.dma_start(out=dbgLM[0:1, :].rearrange("o (a b) -> (o a) b", b=128), in_=AR(oQF, 128)[0:8, :]), reads=[bQF], writes=[Buf()], dma=True)
        op("gpsimd", lambda e: e.dma_start(out=dbgLM[1:2, :].rearrange("o (a b) -> (o a) b", b=128), in_=AR(oKF, 128)[0:8, :]), reads=[bKF], writes=[Buf()], dma=True)

    if debug:
        op("gpsimd", lambda e: e.dma_start(out=dbgV2, in_=dV.ap()), reads=[bdV],
           writes=[Buf()], dma=True)

    def lmsl(o, h):
        po = 64 * (h % 2)
        return AR(o + 128 * (h // 2), 128)[po:po + 64, :]

    oP2, bP2 = ar.take(2048, 2048)
    oP2T, bP2T = ar.take(4096, 2048)
    oVI, bVI = ar.take(6144, 2048)
    oYI, bYI = ar.take(8192, 2048)
    oT0, bT0 = ar.take(10240, 2048)
    oT0T, bT0T = ar.take(12288, 2048)
    oT1, bT1 = ar.take(14336, 2048)
    oI4, bI4 = ar.take(16384, 512)
    for q in range(4):
        op("vector", lambda e, q=q: e.tensor_copy(AR(oI4 + 128 * q, 128), IDN[:, :]),
           reads=[bIDN], writes=[bI4])

    for h4 in range(4):
        p, bp = (0, bps[0]) if h4 % 2 == 0 else (1, bps[1])
        for hh in range(4):
            h = 4 * h4 + hh
            op("tensor", lambda e, h=h, hh=hh, p=p: e.matmul(
                ps[p][:, 128 * hh:128 * (hh + 1)], lmsl(oQF, h), lmsl(oKF, h),
                start=True, stop=True), reads=[bQF, bKF], writes=[bp])
        for hh in range(4):
            h = 4 * h4 + hh
            op("scalar", lambda e, p=p, hh=hh, h=h: e.activation(
                AR(oP2 + 128 * h, 128), ps[p][:, 128 * hh:128 * (hh + 1)], AF.Exp),
               reads=[bp], writes=[bP2])
        for hh in range(4):
            h = 4 * h4 + hh
            op("vector", lambda e, h=h: e.tensor_reduce(
                BB[:, 8:9], AR(oP2 + 128 * h, 128), AX.X, ALU.add),
               reads=[bP2], writes=[bBB])
            op("vector", lambda e: e.reciprocal(BB[:, 9:10], BB[:, 8:9]),
               reads=[bBB], writes=[bBB])
            op("vector", lambda e, h=h: e.tensor_scalar(
                AR(oP2 + 128 * h, 128), AR(oP2 + 128 * h, 128), BB[:, 9:10], None,
                ALU.mult), reads=[bBB], writes=[bP2])
    op("vector", lambda e: e.tensor_reduce(
        SS[:, 0:64], AR(oP2, 2048).rearrange("p (b x) -> p b x", x=32), AX.X, ALU.add,
        apply_transpose=True), reads=[bP2], writes=[bSS])
    op("tensor", lambda e: e.transpose(ps[7][0:64, 0:128], SS[:, 0:64], IDN[:, :]),
       reads=[bSS, bIDN], writes=[bps[7]])
    op("vector", lambda e: e.tensor_copy(SS2[0:64, 0:32], ps[7][0:64, 0:32]),
       reads=[bps[7]], writes=[bSS2])
    for g in range(1, 4):
        op("vector", lambda e, g=g: e.tensor_tensor(
            SS2[0:64, 0:32], SS2[0:64, 0:32], ps[7][0:64, 32 * g:32 * (g + 1)],
            ALU.add), reads=[bps[7]], writes=[bSS2])
    op("vector", lambda e: e.reduce_max(SS2[0:64, 64:65], SS2[0:64, 0:32], AX.X), reads=[bSS2], writes=[bSS2])
    op("tensor", lambda e: e.transpose(ps[7][0:1, 128:192], SS2[0:64, 64:65],
                                       IDN[0:64, 0:64]), reads=[bSS2, bIDN],
       writes=[bps[7]])
    op("vector", lambda e: e.reduce_max(SCR[0:1, 0:1], ps[7][0:1, 128:192], AX.X), reads=[bps[7]], writes=[bSCR])
    op("gpsimd", lambda e: e.dma_start(out=c2i.ap(), in_=SCR[0:1, 0:1]),
       reads=[bSCR], writes=[bc2], dma=True)
    op("gpsimd", lambda e: e.collective_compute(
        "AllReduce", ALU.max, replica_groups=[list(range(8))],
        ins=[c2i.ap().opt()], outs=[c2o.ap().opt()]),
       reads=[bc2], writes=[bc2])
    op("gpsimd", lambda e: e.dma_start(out=SC1[:, 0:1], in_=c2o.ap().broadcast_to([128, 1])),
       reads=[bc2], writes=[bSC1], dma=True)

    for h in range(NH):
        sl = h % 4
        op("tensor", lambda e, h=h, sl=sl: e.transpose(
            ps[7][:, 128 * sl:128 * (sl + 1)], AR(oP2 + 128 * h, 128), IDN[:, :]),
           reads=[bP2, bIDN], writes=[bps[7]])
        op("vector", lambda e, h=h, sl=sl: e.tensor_copy(
            AR(oP2T + 128 * h, 128), ps[7][:, 128 * sl:128 * (sl + 1)]),
           reads=[bps[7]], writes=[bP2T])
    op("vector", lambda e: e.reciprocal(SC1[:, 15:16], SC1[:, 0:1]),
       reads=[bSC1], writes=[bSC1])
    op("vector", lambda e: e.tensor_scalar(AR(oVI, 2048), AR(oP2T, 2048), SC1[:, 15:16],
                                           None, ALU.mult),
       reads=[bP2T, bSC1], writes=[bVI])
    op("vector", lambda e: e.tensor_scalar(AR(oYI, 2048), AR(oP2, 2048), SC1[:, 15:16],
                                           None, ALU.mult),
       reads=[bP2, bSC1], writes=[bYI])

    if debug:
        op("gpsimd", lambda e: e.dma_start(out=dbgP2, in_=AR(oP2, 2048)), reads=[bP2], writes=[Buf()], dma=True)

    def inv_iter():
        for h in range(NH):
            p, bp = (4, bps[4]) if (h // 4) % 2 == 0 else (5, bps[5])
            sl = h % 4
            op("tensor", lambda e, h=h, p=p, sl=sl: e.matmul(
                ps[p][:, 128 * sl:128 * (sl + 1)], AR(oP2T + 128 * h, 128),
                AR(oVI + 128 * h, 128), start=True, stop=True),
               reads=[bP2T, bVI], writes=[bp])
            if sl == 3:
                op("vector", lambda e, h=h, p=p: e.tensor_copy(
                    AR(oT0 + 128 * (h - 3), 512), ps[p][:, :]), reads=[bp], writes=[bT0])
        for h in range(NH):
            sl = h % 4
            op("tensor", lambda e, h=h, sl=sl: e.transpose(
                ps[7][:, 128 * sl:128 * (sl + 1)], AR(oT0 + 128 * h, 128), IDN[:, :]),
               reads=[bT0, bIDN], writes=[bps[7]])
            op("vector", lambda e, h=h, sl=sl: e.tensor_copy(
                AR(oT0T + 128 * h, 128), ps[7][:, 128 * sl:128 * (sl + 1)]),
               reads=[bps[7]], writes=[bT0T])
        for q in range(4):
            op("vector", lambda e, q=q: e.scalar_tensor_tensor(
                AR(oT1 + 512 * q, 512), AR(oI4, 512), 7.0, AR(oT0 + 512 * q, 512),
                ALU.mult, ALU.subtract), reads=[bI4, bT0], writes=[bT1])
        for h in range(NH):
            p, bp = (4, bps[4]) if (h // 4) % 2 == 0 else (5, bps[5])
            sl = h % 4
            op("tensor", lambda e, h=h, p=p, sl=sl: e.matmul(
                ps[p][:, 128 * sl:128 * (sl + 1)], AR(oT0T + 128 * h, 128),
                AR(oT1 + 128 * h, 128), start=True, stop=True),
               reads=[bT0T, bT1], writes=[bp])
            if sl == 3:
                op("vector", lambda e, h=h, p=p: e.scalar_tensor_tensor(
                    AR(oT1 + 128 * (h - 3), 512), AR(oI4, 512), 15.0, ps[p][:, :],
                    ALU.mult, ALU.subtract), reads=[bp, bI4], writes=[bT1])
        for h in range(NH):
            p, bp = (4, bps[4]) if (h // 4) % 2 == 0 else (5, bps[5])
            sl = h % 4
            op("tensor", lambda e, h=h, p=p, sl=sl: e.matmul(
                ps[p][:, 128 * sl:128 * (sl + 1)], AR(oT0T + 128 * h, 128),
                AR(oT1 + 128 * h, 128), start=True, stop=True),
               reads=[bT0T, bT1], writes=[bp])
            if sl == 3:
                op("vector", lambda e, h=h, p=p: e.scalar_tensor_tensor(
                    AR(oT1 + 128 * (h - 3), 512), AR(oI4, 512), 13.0, ps[p][:, :],
                    ALU.mult, ALU.subtract), reads=[bp, bI4], writes=[bT1])
        for h in range(NH):
            p, bp = (4, bps[4]) if (h // 4) % 2 == 0 else (5, bps[5])
            p2_, bp2 = (6, bps[6]) if (h // 4) % 2 == 0 else (6, bps[6])
            sl = h % 4
            op("tensor", lambda e, h=h, p=p, sl=sl: e.matmul(
                ps[p][:, 128 * sl:128 * (sl + 1)], AR(oYI + 128 * h, 128),
                AR(oT1 + 128 * h, 128), start=True, stop=True),
               reads=[bYI, bT1], writes=[bp])
            op("tensor", lambda e, h=h, p2_=p2_, sl=sl: e.matmul(
                ps[p2_][:, 128 * sl:128 * (sl + 1)], AR(oT1 + 128 * h, 128),
                AR(oYI + 128 * h, 128), start=True, stop=True),
               reads=[bYI, bT1], writes=[bp2])
            if sl == 3:
                op("vector", lambda e, h=h, p=p: e.tensor_scalar(
                    AR(oVI + 128 * (h - 3), 512), ps[p][:, :], 0.25, None, ALU.mult),
                   reads=[bp], writes=[bVI])
                op("vector", lambda e, h=h, p2_=p2_: e.tensor_scalar(
                    AR(oYI + 128 * (h - 3), 512), ps[p2_][:, :], 0.25, None, ALU.mult),
                   reads=[bp2], writes=[bYI])

    for it_ in range(6):
        inv_iter()
        if debug:
            op("gpsimd", lambda e, it_=it_: e.dma_start(out=dbgIT[it_], in_=AR(oVI, 2048)),
               reads=[bVI], writes=[Buf()], dma=True)
            op("gpsimd", lambda e, it_=it_: e.dma_start(out=dbgYT[it_], in_=AR(oYI, 2048)),
               reads=[bYI], writes=[Buf()], dma=True)

    if debug:
        op("gpsimd", lambda e: e.dma_start(out=dbgVI, in_=AR(oVI, 2048)), reads=[bVI], writes=[Buf()], dma=True)

    oE3, bE3 = ar.take(16896, 2048)
    oKC, bKC = ar.take(18944, 2048, 2)
    oVC, bVC = ar.take(20992, 2112, 2)
    oG0, bG0 = ar.take(23104, 1056)
    oGG, bGG = ar.take(24160, 1024)

    for lc in range(NLC):
        okc, bkc = oKC + 1024 * (lc % 2), bKC[lc % 2]
        ovc, bvc = oVC + 1056 * (lc % 2), bVC[lc % 2]
        for i in range(NKC):
            op("sync", lambda e, i=i, lc=lc, okc=okc: e.dma_start(
                out=AR(okc + 128 * i, 128),
                in_=dK.ap()[128 * i:128 * (i + 1), 128 * lc:128 * (lc + 1)]),
               reads=[bdK], writes=[bkc], dma=True)
        op("sync", lambda e, lc=lc, ovc=ovc: e.dma_start(
            out=AR(ovc, 1056), in_=dV.ap()[128 * lc:128 * (lc + 1), :]),
           reads=[bdV], writes=[bvc], dma=True)
        for h4 in range(4):
            p, bp = (2, bps[2]) if h4 % 2 == 0 else (3, bps[3])
            for hh in range(4):
                h = 4 * h4 + hh
                po = 64 * (h % 2)
                op("tensor", lambda e, h=h, hh=hh, po=po, p=p, okc=okc: e.matmul(
                    ps[p][:, 128 * hh:128 * (hh + 1)],
                    AR(okc + 128 * (h // 2), 128)[po:po + 64, :], lmsl(oQF, h),
                    start=True, stop=True), reads=[bkc, bQF], writes=[bp])
            op("scalar", lambda e, p=p, h4=h4: e.activation(
                AR(oE3 + 512 * h4, 512), ps[p][:, :], AF.Exp),
               reads=[bp], writes=[bE3])
        pb0 = 4 if lc % 2 == 0 else 0
        pmap = (pb0, pb0 + 1, 7 if lc % 2 else 6)
        for h in range(NH):
            p = pmap[0] if h < 7 else (pmap[1] if h < 14 else pmap[2])
            slot = h if h < 7 else (h - 7 if h < 14 else h - 14)
            op("tensor", lambda e, h=h, p=p, slot=slot, ovc=ovc: e.matmul(
                ps[p][:, VW * slot:VW * (slot + 1)], AR(oE3 + 128 * h, 128),
                AR(ovc + VW * h, VW), start=True, stop=True),
               reads=[bE3, bvc], writes=[bps[p]])
        for seg, p in ((0, pmap[0]), (1, pmap[1]), (2, pmap[2])):
            o0 = 462 * seg
            n = 462 if seg < 2 else 132
            if lc == 0:
                op("vector", lambda e, o0=o0, n=n, p=p: e.tensor_copy(
                    AR(oG0 + o0, n), ps[p][:, 0:n]), reads=[bps[p]], writes=[bG0])
            else:
                op("vector", lambda e, o0=o0, n=n, p=p: e.tensor_tensor(
                    AR(oG0 + o0, n), AR(oG0 + o0, n), ps[p][:, 0:n], ALU.add),
                   reads=[bps[p]], writes=[bG0])

    if debug:
        op("gpsimd", lambda e: e.dma_start(out=dbgG0p, in_=AR(oG0, 1056)),
           reads=[bG0], writes=[Buf()], dma=True)
        op("gpsimd", lambda e: e.dma_start(out=dbgE3, in_=AR(oE3, 2048)),
           reads=[bE3], writes=[Buf()], dma=True)
    for h in range(NH):
        op("gpsimd", lambda e, h=h: e.dma_start(
            out=c3i.ap()[128 * h:128 * (h + 1), :], in_=AR(oG0 + VW * h, VW)),
           reads=[bG0], writes=[bc3], dma=True)
    op("gpsimd", lambda e: e.collective_compute(
        "AllReduce", ALU.add, replica_groups=[[0, 1], [2, 3], [4, 5], [6, 7]],
        ins=[c3i.ap().opt()], outs=[c3o.ap().opt()]),
       reads=[bc3], writes=[bc3])
    for h in range(NH):
        op("gpsimd", lambda e, h=h: e.dma_start(
            out=AR(oG0 + VW * h, VW), in_=c3o.ap()[128 * h:128 * (h + 1), :]),
           reads=[bc3], writes=[bG0], dma=True)
    for h in range(NH):
        op("vector", lambda e, h=h: e.reciprocal(BB[:, 10:11], AR(oG0 + VW * h + 64, 1)),
           reads=[bG0], writes=[bBB])
        op("vector", lambda e, h=h: e.tensor_scalar(
            AR(oG0 + VW * h, 64), AR(oG0 + VW * h, 64), BB[:, 10:11], None, ALU.mult),
           reads=[bBB], writes=[bG0])
    for h in range(NH):
        p, bp = (4, bps[4]) if (h // 8) % 2 == 0 else (5, bps[5])
        sl = h % 8
        op("tensor", lambda e, h=h, p=p, sl=sl: e.matmul(
            ps[p][:, 64 * sl:64 * (sl + 1)], AR(oYI + 128 * h, 128),
            AR(oG0 + VW * h, 64), start=True, stop=True),
           reads=[bYI, bG0], writes=[bp])
        if sl == 7:
            op("vector", lambda e, h=h, p=p: e.tensor_copy(
                AR(oGG + 64 * (h - 7), 512), ps[p][:, :]), reads=[bp], writes=[bGG])

    if debug:
        op("gpsimd", lambda e: e.dma_start(out=dbgG0, in_=AR(oG0, 1056)), reads=[bG0], writes=[Buf()], dma=True)
        op("gpsimd", lambda e: e.dma_start(out=dbgGG, in_=AR(oGG, 1024)), reads=[bGG], writes=[Buf()], dma=True)

    oXU, bXU = ar.take(2048, 16384, NKC)
    oQC, bQC = ar.take(18944, 2048, 2)
    oE1, bE1 = ar.take(20992, 2048)
    oDB, bDB = ar.take(25184, 2048)
    oDS, bDS = ar.take(27232, 2048)
    oDR, bDR = ar.take(29280, 2048)

    for lc in range(NLC):
        oqc, bqc = oQC + 1024 * (lc % 2), bQC[lc % 2]
        for i in range(NKC):
            op("sync", lambda e, i=i, lc=lc, oqc=oqc: e.dma_start(
                out=AR(oqc + 128 * i, 128),
                in_=dQ.ap()[128 * i:128 * (i + 1), 128 * lc:128 * (lc + 1)]),
               reads=[bdQ], writes=[bqc], dma=True)
        for h4 in range(4):
            p, bp = (2, bps[2]) if h4 % 2 == 0 else (3, bps[3])
            for hh in range(4):
                h = 4 * h4 + hh
                po = 64 * (h % 2)
                op("tensor", lambda e, h=h, hh=hh, po=po, p=p, oqc=oqc: e.matmul(
                    ps[p][:, 128 * hh:128 * (hh + 1)], lmsl(oKF, h),
                    AR(oqc + 128 * (h // 2), 128)[po:po + 64, :],
                    start=True, stop=True), reads=[bqc, bKF], writes=[bp])
            op("scalar", lambda e, p=p, h4=h4: e.activation(
                AR(oE1 + 512 * h4, 512), ps[p][:, :], AF.Exp),
               reads=[bp], writes=[bE1])
        op("vector", lambda e: e.tensor_reduce(
            SS[:, 0:64], AR(oE1, 2048).rearrange("p (b x) -> p b x", x=32), AX.X,
            ALU.add, apply_transpose=True), reads=[bE1], writes=[bSS])
        op("tensor", lambda e: e.transpose(ps[7][0:64, 0:128], SS[:, 0:64], IDN[:, :]),
           reads=[bSS, bIDN], writes=[bps[7]])
        op("vector", lambda e: e.tensor_copy(ST[0:64, 0:32], ps[7][0:64, 0:32]),
           reads=[bps[7]], writes=[bST])
        for g in range(1, 4):
            op("vector", lambda e, g=g: e.tensor_tensor(
                ST[0:64, 0:32], ST[0:64, 0:32], ps[7][0:64, 32 * g:32 * (g + 1)],
                ALU.add), reads=[bps[7]], writes=[bST])
        op("gpsimd", lambda e: e.dma_start(out=dTmp.ap(), in_=ST[0:64, 0:32]),
           reads=[bST], writes=[bdTmp], dma=True)
        op("gpsimd", lambda e, lc=lc: e.dma_start(
            out=dD1.ap().rearrange("h (l b4 i) -> h l b4 i", b4=4, i=32)[:, lc, :, :],
            in_=dTmp.ap().rearrange("(h b4) i -> h b4 i", b4=4)),
           reads=[bdTmp], writes=[bdD1], dma=True)
        for jc in range(NKC):
            p, bp = (0, bps[0]) if jc % 2 == 0 else (1, bps[1])
            for par in range(2):
                h = 2 * jc + par
                op("tensor", lambda e, h=h, p=p, par=par: e.matmul(
                    ps[p][64 * par:64 * (par + 1), 0:128], AR(oGG + 64 * h, 64),
                    AR(oE1 + 128 * h, 128), start=True, stop=True),
                   reads=[bGG, bE1], writes=[bp])
            op("vector", lambda e, jc=jc, lc=lc, p=p: e.tensor_copy(
                AR(oXU + 2048 * jc + 128 * lc, 128), ps[p][:, 0:128]),
               reads=[bp], writes=[bXU[jc]])

    op("sync", lambda e: e.dma_start(out=AR(oDS, 2048)[0:16, :], in_=dD1.ap()),
       reads=[bdD1], writes=[bDS], dma=True)
    op("vector", lambda e: e.reciprocal(AR(oDR, 2048)[0:16, :], AR(oDS, 2048)[0:16, :]),
       reads=[bDS], writes=[bDR])
    op("gpsimd", lambda e: e.dma_start(out=dD1.ap(), in_=AR(oDR, 2048)[0:16, :]),
       reads=[bDR], writes=[bdD1], dma=True)
    for jc in range(NKC):
        op("gpsimd", lambda e, jc=jc: e.dma_start(
            out=AR(oDB, 2048)[0:64, :],
            in_=dD1.ap()[2 * jc:2 * jc + 1, :].broadcast_to([64, T])),
           reads=[bdD1], writes=[bDB], dma=True)
        op("gpsimd", lambda e, jc=jc: e.dma_start(
            out=AR(oDB, 2048)[64:128, :],
            in_=dD1.ap()[2 * jc + 1:2 * jc + 2, :].broadcast_to([64, T])),
           reads=[bdD1], writes=[bDB], dma=True)
        op("vector", lambda e, jc=jc: e.tensor_tensor(
            AR(oXU + 2048 * jc, 2048), AR(oXU + 2048 * jc, 2048), AR(oDB, 2048),
            ALU.mult), reads=[bDB], writes=[bXU[jc]])
        op("vector", lambda e, jc=jc: e.tensor_tensor(
            X[jc][:, :], X[jc][:, :], AR(oXU + 2048 * jc, 2048), ALU.add),
           reads=[bXU[jc]], writes=[bX[jc]])

    if debug:
        op("gpsimd", lambda e: e.dma_start(out=dbgD1, in_=dD1.ap()), reads=[bdD1], writes=[Buf()], dma=True)
        op("gpsimd", lambda e: e.dma_start(out=dbgQ, in_=dQ.ap()), reads=[bdQ], writes=[Buf()], dma=True)
        op("gpsimd", lambda e: e.dma_start(out=dbgK, in_=dK.ap()), reads=[bdK], writes=[Buf()], dma=True)
        op("gpsimd", lambda e: e.dma_start(out=dbgV, in_=dV.ap()), reads=[bdV], writes=[Buf()], dma=True)
        for jc in range(NKC):
            op("gpsimd", lambda e, jc=jc: e.dma_start(out=dbgX1[128 * jc:128 * (jc + 1), :], in_=X[jc][:, :]), reads=[bX[jc]], writes=[Buf()], dma=True)

    def ffn_sublayer2(first, rowd, browd, cgi_, cgo_, bcg_):
        oWF, bWF = ar.take(0, 8192, NKC)
        oQZ, bQZ = ar.take(8192, 8192, NKC)
        oXF, bXF = ar.take(16384, 16384, NKC)
        if first:
            for i in range(NKC):
                op("sync", lambda e, i=i: e.dma_start(
                    out=AR(oWF + 1024 * i, 1024), in_=wffn[128 * i:128 * (i + 1), :]),
                   writes=[bWF[i]], dma=True)
            for i in range(NKC):
                if i == 0:
                    op("vector", lambda e, i=i: e.tensor_reduce(
                        BB[:, 11:12], AR(oWF + 1024 * i, 1024), AX.X, ALU.add,
                        apply_absolute_value=True), reads=[bWF[i]], writes=[bBB])
                else:
                    op("vector", lambda e, i=i: e.tensor_reduce(
                        BB[:, 12:13], AR(oWF + 1024 * i, 1024), AX.X, ALU.add,
                        apply_absolute_value=True), reads=[bWF[i]], writes=[bBB])
                    op("vector", lambda e: e.tensor_tensor(
                        BB[:, 11:12], BB[:, 11:12], BB[:, 12:13], ALU.add),
                       reads=[bBB], writes=[bBB])
            op("tensor", lambda e: e.transpose(ps[7][0:1, 0:128], BB[:, 11:12],
                                               IDN[:, :]), reads=[bBB, bIDN],
               writes=[bps[7]])
            op("vector", lambda e: e.tensor_reduce(SCR[0:1, 1:2], ps[7][0:1, 0:128],
                                                   AX.X, ALU.add),
               reads=[bps[7]], writes=[bSCR])
            op("vector", lambda e: e.tensor_scalar(SCR[0:1, 2:3], SCR[0:1, 1:2],
                                                   1.0 / (D * D), None, ALU.mult),
               reads=[bSCR], writes=[bSCR])
            op("gpsimd", lambda e: e.dma_start(out=dTmp.ap()[0:1, 0:1],
                                               in_=SCR[0:1, 2:3]),
               reads=[bSCR], writes=[bdTmp], dma=True)
            op("sync", lambda e: e.dma_start(
                out=SC1[:, 1:2], in_=dTmp.ap()[0:1, 0:1].broadcast_to([128, 1])),
               reads=[bdTmp], writes=[bSC1], dma=True)
            op("vector", lambda e: e.tensor_scalar(SC1[:, 2:3], SC1[:, 1:2], 1.0, 1e-5,
                                                   ALU.mult, ALU.add),
               reads=[bSC1], writes=[bSC1])
            op("vector", lambda e: e.tensor_scalar(SC1[:, 3:4], SC1[:, 2:3], 0.5, None,
                                                   ALU.mult), reads=[bSC1], writes=[bSC1])
            op("vector", lambda e: e.tensor_scalar(SC1[:, 4:5], SC1[:, 3:4], -1.0, None,
                                                   ALU.mult), reads=[bSC1], writes=[bSC1])
            for i in range(NKC):
                op("vector", lambda e, i=i: e.tensor_scalar(
                    AR(oQZ + 1024 * i, 1024), AR(oWF + 1024 * i, 1024), SC1[:, 3:4],
                    None, ALU.is_gt), reads=[bWF[i], bSC1], writes=[bQZ[i]])
                op("vector", lambda e, i=i: e.tensor_scalar(
                    AR(oWF + 1024 * i, 1024), AR(oWF + 1024 * i, 1024), SC1[:, 4:5],
                    None, ALU.is_lt), reads=[bSC1], writes=[bWF[i]])
                qv = AR(oQZ + 1024 * i, 1024).bitcast(BF16)
                op("vector", lambda e, i=i, qv=qv: e.tensor_tensor(
                    qv[:, 0:1024], AR(oQZ + 1024 * i, 1024),
                    AR(oWF + 1024 * i, 1024), ALU.subtract),
                   reads=[bWF[i]], writes=[bQZ[i]])
                op("gpsimd", lambda e, i=i, qv=qv: e.dma_start(
                    out=dWq.ap()[128 * i:128 * (i + 1), :], in_=qv[:, 0:1024]),
                   reads=[bQZ[i]], writes=[bdWq], dma=True)
        else:
            for i in range(NKC):
                qv = AR(oQZ + 1024 * i, 1024).bitcast(BF16)
                op("sync", lambda e, i=i, qv=qv: e.dma_start(
                    out=qv[:, 0:1024], in_=dWq.ap()[128 * i:128 * (i + 1), :]),
                   reads=[bdWq], writes=[bQZ[i]], dma=True)
        op("vector", lambda e: e.memset(BB[:, 13:14], 0.0), writes=[bBB])
        for tt in range(NTT):
            ln_stats(tt, lambda i, tt=tt: X[i][:, 512 * tt:512 * (tt + 1)], bX,
                     rowd, browd, "ffn")
            for i in range(NKC):
                xn_chunk(AR(oXF + 2048 * i + 512 * tt, 512), bXF[i], i, tt, True)
                op("vector", lambda e, i=i, tt=tt: e.reduce_max(
                    BB[:, 14:15], AR(oXF + 2048 * i + 512 * tt, 512), AX.X,
                    apply_absolute_value=True), reads=[bXF[i]], writes=[bBB])
                op("vector", lambda e: e.tensor_tensor(BB[:, 13:14], BB[:, 13:14],
                                                       BB[:, 14:15], ALU.max),
                   reads=[bBB], writes=[bBB])
        partition_max_to_scalar(BB[:, 13:14], bBB, cgi_, bcg_)
        op("gpsimd", lambda e: e.collective_compute(
            "AllReduce", ALU.max, replica_groups=[list(range(8))],
            ins=[cgi_.ap().opt()], outs=[cgo_.ap().opt()]),
           reads=[bcg_], writes=[bcg_])
        op("gpsimd", lambda e: e.dma_start(out=SC1[:, 5:6],
                                           in_=cgo_.ap().broadcast_to([128, 1])),
           reads=[bcg_], writes=[bSC1], dma=True)
        op("vector", lambda e: e.reciprocal(SC1[:, 6:7], SC1[:, 5:6]),
           reads=[bSC1], writes=[bSC1])
        op("vector", lambda e: e.tensor_scalar(SC1[:, 6:7], SC1[:, 6:7], 128.0, None,
                                               ALU.mult), reads=[bSC1], writes=[bSC1])
        op("vector", lambda e: e.tensor_scalar(SC1[:, 7:8], SC1[:, 5:6], SC1[:, 1:2],
                                               None, ALU.mult), reads=[bSC1], writes=[bSC1])
        op("vector", lambda e: e.tensor_scalar(SC1[:, 7:8], SC1[:, 7:8], 1.0 / 128.0,
                                               None, ALU.mult), reads=[bSC1], writes=[bSC1])
        CL = 128.0 - 1e-5
        for tt in range(NTT):
            for i in range(NKC):
                xv = AR(oXF + 2048 * i + 512 * tt, 512).bitcast(BF16)
                op("vector", lambda e, i=i, tt=tt: e.tensor_scalar(
                    TMP[:, :], AR(oXF + 2048 * i + 512 * tt, 512), SC1[:, 6:7], None,
                    ALU.mult), reads=[bSC1, bXF[i]], writes=[bTMP])
                op("vector", lambda e: e.tensor_scalar(
                    TMP[:, :], TMP[:, :], CL, -CL, ALU.min, ALU.max),
                   reads=[], writes=[bTMP])
                op("scalar", lambda e, xv=xv: e.activation(
                    xv[:, 0:512], TMP[:, :], AF.Identity, bias=0.0, scale=1.0),
                   reads=[bTMP], writes=[bXF[i]])
                op("vector", lambda e, xv=xv: e.scalar_tensor_tensor(
                    xv[:, 512:1024], TMP[:, :], 1.0, xv[:, 0:512], ALU.mult,
                    ALU.subtract), reads=[bTMP], writes=[bXF[i]])
            for jc in range(NKC):
                p, bp = (0, bps[0]) if jc % 2 == 0 else (1, bps[1])
                fns = []
                for kc in range(NKC):
                    qv = AR(oQZ + 1024 * kc, 1024).bitcast(BF16)
                    xv = AR(oXF + 2048 * kc + 512 * tt, 512).bitcast(BF16)
                    for i2 in range(2):
                        fns.append(lambda e, kc=kc, jc=jc, p=p, qv=qv, xv=xv,
                                   i2=i2: e.matmul(
                            ps[p][:, :], qv[:, 128 * jc:128 * (jc + 1)],
                            xv[:, 512 * i2:512 * (i2 + 1)],
                            start=(kc == 0 and i2 == 0),
                            stop=(kc == NKC - 1 and i2 == 1)))
                kb.opchain("tensor", fns, reads=list(bQZ) + list(bXF), writes=[bp])
                op("vector", lambda e, jc=jc, tt=tt, p=p: e.scalar_tensor_tensor(
                    X[jc][:, 512 * tt:512 * (tt + 1)], ps[p][:, :], SC1[:, 7:8],
                    X[jc][:, 512 * tt:512 * (tt + 1)], ALU.mult, ALU.add),
                   reads=[bp, bSC1], writes=[bX[jc]])

    ffn_sublayer2(True, dRow2, bdRow2, cgi, cgo, bcg)

    if debug:
        for jc in range(NKC):
            op("gpsimd", lambda e, jc=jc: e.dma_start(out=dbgX2[128 * jc:128 * (jc + 1), :], in_=X[jc][:, :]), reads=[bX[jc]], writes=[Buf()], dma=True)
        op("gpsimd", lambda e: e.dma_start(out=dbgSC, in_=SC1[:, :]), reads=[bSC1], writes=[Buf()], dma=True)

    oWQb, bWQb = ar.take(0, 8192, NKC)
    oWKb, bWKb = ar.take(8192, 8192, NKC)
    oWVb, bWVb = ar.take(16384, 8192, NKC)
    oXN3, bXN3 = ar.take(24576, 4096, NKC)
    oEV3, bEV3 = ar.take(28672, 1024, 2)
    oVT3, bVT3 = ar.take(29696, 1056)
    wsplit_load(oWQb, bWQb, wqbT, oXN3, (bXN3[0], bXN3[2]))
    wsplit_load(oWKb, bWKb, wkbT, oXN3, (bXN3[0], bXN3[2]))
    wsplit_load(oWVb, bWVb, wvbT, oXN3, (bXN3[0], bXN3[2]))
    op("vector", lambda e: e.memset(AR(oVT3, 1056), 1.0), writes=[bVT3])
    for tt in range(NTT):
        ln_stats(tt, lambda i, tt=tt: X[i][:, 512 * tt:512 * (tt + 1)], bX,
                 dRow, bdRow, "plain")
        for i in range(NKC):
            xn_chunk_split(oXN3 + 512 * i, bXN3[i], i, tt)
        for (oW, bW, dspill, bdspill) in ((oWQb, bWQb, dQb, bdQb),
                                          (oWKb, bWKb, dKb, bdKb)):
            for jc in range(NKC):
                p, bp = (0, bps[0]) if jc % 2 == 0 else (1, bps[1])
                ev, bev = (oEV3 + 512 * (jc % 2)), bEV3[jc % 2]
                fns = []
                for kc in range(NKC):
                    wv = AR(oW + 1024 * kc, 1024).bitcast(BF16)
                    xv = AR(oXN3 + 512 * kc, 512).bitcast(BF16)
                    fns += mm3fns(ps[p][:, :],
                                  wv[:, 128 * jc:128 * (jc + 1)],
                                  wv[:, 1024 + 128 * jc:1024 + 128 * (jc + 1)],
                                  xv[:, 0:512], xv[:, 512:1024],
                                  kc == 0, kc == NKC - 1)
                kb.opchain("tensor", fns, reads=list(bW) + list(bXN3), writes=[bp])
                op("vector", lambda e, p=p, ev=ev: e.tensor_copy(AR(ev, 512), ps[p][:, :]),
                   reads=[bp], writes=[bev])
                op("gpsimd", lambda e, ev=ev, jc=jc, tt=tt, dspill=dspill: e.dma_start(
                    out=dspill.ap()[128 * jc:128 * (jc + 1), 512 * tt:512 * (tt + 1)],
                    in_=AR(ev, 512)), reads=[bev], writes=[bdspill], dma=True)
        for lc4 in range(4):
            for vt2 in range(2):
                p, bp = (2, bps[2]) if vt2 == 0 else (3, bps[3])
                fns = []
                for kc in range(NKC):
                    wv = AR(oWVb + 1024 * kc, 1024).bitcast(BF16)
                    xv = AR(oXN3 + 512 * kc, 512).bitcast(BF16)
                    fns += mm3fns(ps[p][:, :],
                                  xv[:, 128 * lc4:128 * (lc4 + 1)],
                                  xv[:, 512 + 128 * lc4:512 + 128 * (lc4 + 1)],
                                  wv[:, 512 * vt2:512 * (vt2 + 1)],
                                  wv[:, 1024 + 512 * vt2:1024 + 512 * (vt2 + 1)],
                                  kc == 0, kc == NKC - 1)
                kb.opchain("tensor", fns, reads=list(bXN3) + list(bWVb), writes=[bp])
                op("vector", lambda e, p=p, vt2=vt2: e.tensor_copy(
                    AR(oVT3 + VW * 8 * vt2, VW * 8).rearrange(
                        "p (h w) -> p h w", w=VW)[:, :, 0:64],
                    ps[p][:, :].rearrange("p (h d) -> p h d", d=64)),
                   reads=[bp], writes=[bVT3])
            op("gpsimd", lambda e, tt=tt, lc4=lc4: e.dma_start(
                out=dVb.ap()[512 * tt + 128 * lc4:512 * tt + 128 * (lc4 + 1), :],
                in_=AR(oVT3, 1056)), reads=[bVT3], writes=[bdVb], dma=True)

    op("gpsimd", lambda e: e.dma_start(
        out=cbi.ap()[0:1, 0:D * 128].rearrange("o (r c) -> (o r) c", c=128),
        in_=dKb.ap()[:, T - 128:T]), reads=[bdKb], writes=[bcb], dma=True)
    op("gpsimd", lambda e: e.dma_start(
        out=cbi.ap()[0:1, D * 128:BND].rearrange("o (r c) -> (o r) c", c=NH * VW),
        in_=dVb.ap()[T - 128:T, :]), reads=[bdVb], writes=[bcb], dma=True)
    op("gpsimd", lambda e: e.collective_compute(
        "AllGather", ALU.bypass, replica_groups=[[0, 1], [2, 3], [4, 5], [6, 7]],
        ins=[cbi.ap().opt()], outs=[cbo.ap().opt()]),
       reads=[bcb], writes=[bcb])

    oWO, bWO = ar.take(0, 8192, NKC)
    oQW, bQW = ar.take(8192, 4096, 2)
    oKCb, bKCb = ar.take(12288, 2048, 2)
    oVCb, bVCb = ar.take(14336, 2112, 2)
    oEC, bEC = ar.take(16448, 512)
    oEP, bEP = ar.take(16960, 2048, 4)
    oOU, bOU = ar.take(19008, 4096, NKC)
    oDSb, bDSb = ar.take(23104, 512)
    oDRb, bDRb = ar.take(23616, 512)
    oDBb, bDBb = ar.take(24128, 512)
    oKBD, bKBD = ar.take(24640, 1024)
    oVBD, bVBD = ar.take(25664, 1056)
    oM4, bM4 = ar.take(26720, 1536)
    oDT, bDT = ar.take(28256, 256, 4)

    for q in range(4):
        op("vector", lambda e, q=q: e.tensor_copy(AR(oM4 + 128 * q, 128), MSKC[:, :]),
           reads=[bMSKC], writes=[bM4])
        op("vector", lambda e, q=q: e.tensor_copy(AR(oM4 + 512 + 128 * q, 128),
                                                  MSKP[:, :]),
           reads=[bMSKP], writes=[bM4])
        op("vector", lambda e, q=q: e.tensor_copy(AR(oM4 + 1024 + 128 * q, 128),
                                                  MSKB[:, :]),
           reads=[bMSKB], writes=[bM4])
    wsplit_load(oWO, bWO, wobT, oQW, (bQW[0], bQW[0]))
    for i in range(NKC):
        op("gpsimd", lambda e, i=i: e.dma_start(
            out=AR(oKBD + 128 * i, 128),
            in_=cbo.ap()[0:1, 128 * 128 * i:128 * 128 * (i + 1)].rearrange(
                "o (r c) -> (o r) c", c=128)), reads=[bcb], writes=[bKBD], dma=True)
    op("gpsimd", lambda e: e.dma_start(
        out=AR(oVBD, 1056),
        in_=cbo.ap()[0:1, D * 128:BND].rearrange("o (r c) -> (o r) c", c=NH * VW)),
       reads=[bcb], writes=[bVBD], dma=True)

    def b2_iter2(kc):
        last = (kc == NLC - 1)
        first = (kc == -1)
        if first:
            okk, bkk = oKBD, bKBD
            ovv, bvv = oVBD, bVBD
        else:
            okk, bkk = oKCb + 1024 * (kc % 2), bKCb[kc % 2]
            ovv, bvv = oVCb + 1056 * (kc % 2), bVCb[kc % 2]
            for i in range(NKC):
                op("sync", lambda e, i=i, kc=kc, okk=okk: e.dma_start(
                    out=AR(okk + 128 * i, 128),
                    in_=dKb.ap()[128 * i:128 * (i + 1), 128 * kc:128 * (kc + 1)]),
                   reads=[bdKb], writes=[bkk], dma=True)
            op("sync", lambda e, kc=kc, ovv=ovv: e.dma_start(
                out=AR(ovv, 1056), in_=dVb.ap()[128 * kc:128 * (kc + 1), :]),
               reads=[bdVb], writes=[bvv], dma=True)
        oqw, bqw = oQW + 2048 * ((kc + 1) % 2), bQW[(kc + 1) % 2]
        q0 = max(kc, 0) * 128
        qn = 128 if (first or last) else 256
        qoff = 128 if first else 0
        for i in range(NKC):
            op("sync", lambda e, i=i, oqw=oqw, q0=q0, qn=qn, qoff=qoff: e.dma_start(
                out=AR(oqw + 256 * i + qoff, qn),
                in_=dQb.ap()[128 * i:128 * (i + 1), q0:q0 + qn]),
               reads=[bdQb], writes=[bqw], dma=True)
        for w in range(4):
            for hh in range(4):
                h = 4 * w + hh
                p = 0 if hh < 2 else 1
                so = 256 * (hh % 2)
                po = 64 * (h % 2)
                op("tensor", lambda e, h=h, p=p, so=so, po=po, okk=okk, oqw=oqw,
                   qoff=qoff, qn=qn: e.matmul(
                    ps[p][:, so + qoff:so + qoff + qn],
                    AR(okk + 128 * (h // 2), 128)[po:po + 64, :],
                    AR(oqw + 256 * (h // 2) + qoff, qn)[po:po + 64, :],
                    start=True, stop=True), reads=[bkk, bqw], writes=[bps[p]])
            for part in range(2):
                if (part == 0 and first) or (part == 1 and last):
                    continue
                oE = oEC if part == 0 else oEP + 512 * w
                bE = bEC if part == 0 else bEP[w]
                for hh in range(4):
                    p = 0 if hh < 2 else 1
                    so = 256 * (hh % 2) + 128 * part
                    op("scalar", lambda e, p=p, so=so, hh=hh, oE=oE: e.activation(
                        AR(oE + 128 * hh, 128), ps[p][:, so:so + 128], AF.Exp,
                        scale=0.125), reads=[bps[p]], writes=[bE])
                mk = oM4 if part == 0 else (oM4 + 1024 if first else oM4 + 512)
                op("vector", lambda e, oE=oE, mk=mk: e.tensor_tensor(
                    AR(oE, 512), AR(oE, 512), AR(mk, 512), ALU.mult),
                   reads=[bM4], writes=[bE])
            if not first:
                pb = 3 + (w // 2)
                co = 256 * (w % 2)
                for hh in range(4):
                    h = 4 * w + hh
                    op("tensor", lambda e, h=h, hh=hh, pb=pb, co=co, ovv=ovv: e.matmul(
                        ps[pb][64 * (h % 2):64 * (h % 2) + 64,
                               co + 128 * (hh // 2):co + 128 * (hh // 2) + 128],
                        AR(ovv + VW * h, 64), AR(oEC + 128 * hh, 128),
                        start=True, stop=True), reads=[bvv, bEC], writes=[bps[pb]])
                for pr in range(2):
                    jc = 2 * w + pr
                    op("vector", lambda e, jc=jc, pb=pb, co=co, pr=pr, kc=kc: e.tensor_tensor(
                        AR(oOU + 512 * jc + 128 * (kc % 4), 128),
                        AR(oOU + 512 * jc + 128 * (kc % 4), 128),
                        ps[pb][:, co + 128 * pr:co + 128 * (pr + 1)], ALU.add),
                       reads=[bps[pb]], writes=[bOU[jc]])
                op("vector", lambda e: e.tensor_reduce(
                    AR(oDT, 16), AR(oEC, 512).rearrange("p (b x) -> p b x", x=32),
                    AX.X, ALU.add, apply_transpose=True), reads=[bEC], writes=[bDT[0]])
                op("vector", lambda e, w=w: e.tensor_tensor(
                    AR(oDT, 16), AR(oDT, 16), AR(oDT + 64 + 16 * w, 16), ALU.add),
                   reads=[bDT[1]], writes=[bDT[0]])
                op("tensor", lambda e: e.transpose(ps[2][0:16, 0:128], AR(oDT, 16),
                                                   IDN[:, :]),
                   reads=[bDT[0], bIDN], writes=[bps[2]])
                op("vector", lambda e: e.tensor_copy(SS2[0:16, 0:32],
                                                     ps[2][0:16, 0:32]),
                   reads=[bps[2]], writes=[bSS2])
                for g in range(1, 4):
                    op("vector", lambda e, g=g: e.tensor_tensor(
                        SS2[0:16, 0:32], SS2[0:16, 0:32],
                        ps[2][0:16, 32 * g:32 * (g + 1)], ALU.add),
                       reads=[bps[2]], writes=[bSS2])
                op("gpsimd", lambda e: e.dma_start(out=dTmp.ap()[0:16, :],
                                                   in_=SS2[0:16, 0:32]),
                   reads=[bSS2], writes=[bdTmp], dma=True)
                op("gpsimd", lambda e, w=w, kc=kc: e.dma_start(
                    out=dDb.ap().rearrange("h (l b4 i) -> h l b4 i", b4=4, i=32)[
                        4 * w:4 * (w + 1), kc, :, :],
                    in_=dTmp.ap()[0:16, :].rearrange("(h b4) i -> h b4 i", b4=4)),
                   reads=[bdTmp], writes=[bdDb], dma=True)
        if (not first) and kc % 4 == 3:
            tt = kc // 4
            op("sync", lambda e, tt=tt: e.dma_start(
                out=AR(oDSb, 512)[0:16, :], in_=dDb.ap()[:, 512 * tt:512 * (tt + 1)]),
               reads=[bdDb], writes=[bDSb], dma=True)
            op("vector", lambda e: e.reciprocal(AR(oDRb, 512)[0:16, :],
                                                AR(oDSb, 512)[0:16, :]),
               reads=[bDSb], writes=[bDRb])
            op("gpsimd", lambda e, tt=tt: e.dma_start(
                out=dDb.ap()[:, 512 * tt:512 * (tt + 1)], in_=AR(oDRb, 512)[0:16, :]),
               reads=[bDRb], writes=[bdDb], dma=True)
            for jc in range(NKC):
                op("gpsimd", lambda e, jc=jc, tt=tt: e.dma_start(
                    out=AR(oDBb, 512)[0:64, :],
                    in_=dDb.ap()[2 * jc:2 * jc + 1, 512 * tt:512 * (tt + 1)].broadcast_to([64, 512])),
                   reads=[bdDb], writes=[bDBb], dma=True)
                op("gpsimd", lambda e, jc=jc, tt=tt: e.dma_start(
                    out=AR(oDBb, 512)[64:128, :],
                    in_=dDb.ap()[2 * jc + 1:2 * jc + 2, 512 * tt:512 * (tt + 1)].broadcast_to([64, 512])),
                   reads=[bdDb], writes=[bDBb], dma=True)
                op("vector", lambda e, jc=jc: e.tensor_tensor(
                    TMP[:, :], AR(oOU + 512 * jc, 512), AR(oDBb, 512),
                    ALU.mult), reads=[bDBb, bOU[jc]], writes=[bTMP])
                ov = AR(oOU + 512 * jc, 512).bitcast(BF16)
                op("scalar", lambda e, ov=ov: e.activation(
                    ov[:, 0:512], TMP[:, :], AF.Identity, bias=0.0, scale=1.0),
                   reads=[bTMP], writes=[bOU[jc]])
                op("vector", lambda e, ov=ov: e.scalar_tensor_tensor(
                    ov[:, 512:1024], TMP[:, :], 1.0, ov[:, 0:512], ALU.mult,
                    ALU.subtract), reads=[bTMP], writes=[bOU[jc]])
            for jc in range(NKC):
                p = 0 if jc % 2 == 0 else 1
                fns = []
                for kk in range(NKC):
                    wv = AR(oWO + 1024 * kk, 1024).bitcast(BF16)
                    ov = AR(oOU + 512 * kk, 512).bitcast(BF16)
                    fns += mm3fns(ps[p][:, :],
                                  wv[:, 128 * jc:128 * (jc + 1)],
                                  wv[:, 1024 + 128 * jc:1024 + 128 * (jc + 1)],
                                  ov[:, 0:512], ov[:, 512:1024],
                                  kk == 0, kk == NKC - 1)
                kb.opchain("tensor", fns, reads=list(bWO) + list(bOU), writes=[bps[p]])
                op("vector", lambda e, jc=jc, tt=tt, p=p: e.tensor_tensor(
                    X[jc][:, 512 * tt:512 * (tt + 1)], X[jc][:, 512 * tt:512 * (tt + 1)],
                    ps[p][:, :], ALU.add), reads=[bps[p]], writes=[bX[jc]])
        if not last:
            for w in range(4):
                pb = 5 + (w // 2)
                co = 256 * (w % 2)
                for hh in range(4):
                    h = 4 * w + hh
                    op("tensor", lambda e, h=h, hh=hh, pb=pb, co=co, ovv=ovv, w=w: e.matmul(
                        ps[pb][64 * (h % 2):64 * (h % 2) + 64,
                               co + 128 * (hh // 2):co + 128 * (hh // 2) + 128],
                        AR(ovv + VW * h, 64), AR(oEP + 512 * w + 128 * hh, 128),
                        start=True, stop=True), reads=[bvv, bEP[w]], writes=[bps[pb]])
                for pr in range(2):
                    jc = 2 * w + pr
                    op("vector", lambda e, jc=jc, pb=pb, co=co, pr=pr, kc=kc: e.tensor_copy(
                        AR(oOU + 512 * jc + 128 * ((kc + 1) % 4), 128),
                        ps[pb][:, co + 128 * pr:co + 128 * (pr + 1)]),
                       reads=[bps[pb]], writes=[bOU[jc]])
                op("vector", lambda e, w=w: e.tensor_reduce(
                    AR(oDT + 64 + 16 * w, 16),
                    AR(oEP + 512 * w, 512).rearrange("p (b x) -> p b x", x=32),
                    AX.X, ALU.add, apply_transpose=True),
                   reads=[bEP[w]], writes=[bDT[1]])

    for kc in range(-1, NLC):
        b2_iter2(kc)

    if debug:
        for jc in range(NKC):
            op("gpsimd", lambda e, jc=jc: e.dma_start(out=dbgX3[128 * jc:128 * (jc + 1), :], in_=X[jc][:, :]), reads=[bX[jc]], writes=[Buf()], dma=True)
        op("gpsimd", lambda e: e.dma_start(out=dbgQb, in_=dQb.ap()), reads=[bdQb], writes=[Buf()], dma=True)
        op("gpsimd", lambda e: e.dma_start(out=dbgKb, in_=dKb.ap()), reads=[bdKb], writes=[Buf()], dma=True)
        op("gpsimd", lambda e: e.dma_start(out=dbgVb, in_=dVb.ap()), reads=[bdVb], writes=[Buf()], dma=True)
        op("gpsimd", lambda e: e.dma_start(out=dbgDb, in_=dDb.ap()), reads=[bdDb], writes=[Buf()], dma=True)
        op("gpsimd", lambda e: e.dma_start(out=dbgBND, in_=cbo.ap()), reads=[bcb], writes=[Buf()], dma=True)

    ffn_sublayer2(False, dRow2, bdRow2, cg2i, cg2o, bcg2)

    otoks = []
    for jc in range(NKC):
        tok = op("sync", lambda e, jc=jc: e.dma_start(
            out=outT[128 * jc:128 * (jc + 1), :], in_=X[jc][:, :]),
            reads=[bX[jc]], dma=True)
        otoks.append(tok)
    op("vector", lambda e: e.memset(SCR[0:1, 64:65], 0.0), writes=[bSCR], waits=otoks)

    with (nc.semaphore("t") as st_, nc.semaphore("v") as sv, nc.semaphore("s") as ss_,
          nc.semaphore("g") as sg, nc.semaphore("ld") as sld, nc.semaphore("sd") as ssd,
          nc.semaphore("gd") as sgd, nc.Block() as block):
        sems = {"t": st_, "v": sv, "s": ss_, "g": sg, "ld": sld, "sd": ssd, "gd": sgd}
        kb.realize(block, sems)
    return nc


_NC_CACHE = None


def make_inmaps(**inputs):
    x = np.asarray(inputs["L0_attn_a_x"], dtype=np.float32)

    def t(w):
        return np.ascontiguousarray(np.asarray(w, dtype=np.float32).T)

    def pp(v):
        return np.ascontiguousarray(np.asarray(v, dtype=np.float32).reshape(8, 128).T)

    shared = {
        "wqaT": t(inputs["L0_attn_a_Wq"]), "wkaT": t(inputs["L0_attn_a_Wk"]),
        "wvaT": t(inputs["L0_attn_a_Wv"]),
        "bqa": pp(inputs["L0_attn_a_bq"]), "bka": pp(inputs["L0_attn_a_bk"]),
        "bvaf": np.ascontiguousarray(np.asarray(inputs["L0_attn_a_bv"], np.float32).reshape(1, D)),
        "wffn": np.ascontiguousarray(np.asarray(inputs["L0_ffn_weight"], np.float32)),
        "gffn": pp(inputs["L0_ffn_ln_gamma"]), "bffn": pp(inputs["L0_ffn_ln_beta"]),
        "wqbT": t(inputs["L0_attn_b_Wq"]), "wkbT": t(inputs["L0_attn_b_Wk"]),
        "wvbT": t(inputs["L0_attn_b_Wv"]), "wobT": t(inputs["L0_attn_b_Wo"]),
    }
    in_maps = []
    for c in range(8):
        b, hf = c // 2, c % 2
        m = dict(shared)
        m["xT"] = np.ascontiguousarray(x[b, 2048 * hf:2048 * (hf + 1), :].T)
        m["flagT"] = np.full((128, 1), float(hf), np.float32)
        in_maps.append(m)
    return in_maps


def kernel(L0_attn_a_x, L0_attn_a_Wq, L0_attn_a_bq, L0_attn_a_Wk, L0_attn_a_bk,
           L0_attn_a_Wv, L0_attn_a_bv, L0_attn_b_Wq, L0_attn_b_Wk, L0_attn_b_Wv,
           L0_attn_b_Wo, L0_ffn_weight, L0_ffn_ln_gamma, L0_ffn_ln_beta):
    global _NC_CACHE
    if _NC_CACHE is None:
        _NC_CACHE = build_program()
    nc = _NC_CACHE

    x = np.asarray(L0_attn_a_x, dtype=np.float32)

    def t(w):
        return np.ascontiguousarray(np.asarray(w, dtype=np.float32).T)

    def pp(v):
        return np.ascontiguousarray(np.asarray(v, dtype=np.float32).reshape(8, 128).T)

    shared = {
        "wqaT": t(L0_attn_a_Wq), "wkaT": t(L0_attn_a_Wk), "wvaT": t(L0_attn_a_Wv),
        "bqa": pp(L0_attn_a_bq), "bka": pp(L0_attn_a_bk),
        "bvaf": np.ascontiguousarray(np.asarray(L0_attn_a_bv, np.float32).reshape(1, D)),
        "wffn": np.ascontiguousarray(np.asarray(L0_ffn_weight, np.float32)),
        "gffn": pp(L0_ffn_ln_gamma), "bffn": pp(L0_ffn_ln_beta),
        "wqbT": t(L0_attn_b_Wq), "wkbT": t(L0_attn_b_Wk), "wvbT": t(L0_attn_b_Wv),
        "wobT": t(L0_attn_b_Wo),
    }
    in_maps = []
    for c in range(8):
        b, hf = c // 2, c % 2
        m = dict(shared)
        m["xT"] = np.ascontiguousarray(x[b, 2048 * hf:2048 * (hf + 1), :].T)
        m["flagT"] = np.full((128, 1), float(hf), np.float32)
        in_maps.append(m)
    res = run_bass_kernel_spmd(nc, in_maps, core_ids=list(range(8)))
    out = np.empty((4, 4096, 1024), np.float32)
    for c in range(8):
        b, hf = c // 2, c % 2
        out[b, 2048 * hf:2048 * (hf + 1), :] = res.results[c]["outT"].T
    return out


# version 11 (final)
# speedup vs baseline: 1.4162x; 1.0011x over previous
import math
import numpy as np
import concourse.bass as bass
import concourse.mybir as mybir
from concourse.bass_utils import run_bass_kernel_spmd

F32 = mybir.dt.float32
BF16 = mybir.dt.bfloat16
I32 = mybir.dt.int32
ALU = mybir.AluOpType
AF = mybir.ActivationFunctionType
AX = mybir.AxisListType

D = 1024
T = 2048
NH = 16
HD = 64
NKC = 8
NTT = 4
NLC = 16
VW = 66
QSCALE = 1.0 / math.sqrt(math.sqrt(HD))

ENGS = ("sync", "scalar", "vector", "tensor", "gpsimd")
ESEM = {"sync": (None, "ld"), "scalar": ("s", "sd"), "vector": ("v", None),
        "tensor": ("t", None), "gpsimd": ("g", "gd")}


class Buf:
    __slots__ = ("w", "rs")

    def __init__(self):
        self.w = None
        self.rs = {}

    def war(self):
        out = list(self.rs.items())
        if self.w is not None:
            out.append(self.w)
        return out


class KB:

    def __init__(self):
        self.code = {e: [] for e in ENGS}
        self.cnt = {"t": 0, "v": 0, "s": 0, "g": 0, "ld": 0, "sd": 0, "gd": 0}
        self.step = {"t": 1, "v": 1, "s": 1, "g": 1, "ld": 16, "sd": 16, "gd": 16}

    def op(self, eng, fn, reads=(), writes=(), dma=False, waits=()):
        sem = ESEM[eng][1 if dma else 0]
        wl = []
        for b in reads:
            if b.w is not None:
                wl.append(b.w)
        for b in writes:
            wl.extend(b.war())
        wl.extend(waits)
        self.cnt[sem] += self.step[sem]
        tok = (sem, self.cnt[sem])
        self.code[eng].append((tuple(wl), fn, sem, self.step[sem]))
        for b in writes:
            b.w = tok
            b.rs = {}
        for b in reads:
            s, vv = tok
            if b.rs.get(s, 0) < vv:
                b.rs[s] = vv
        return tok

    def opchain(self, eng, fns, reads=(), writes=(), waits=()):
        sem = ESEM[eng][0]
        wl = []
        for b in reads:
            if b.w is not None:
                wl.append(b.w)
        for b in writes:
            wl.extend(b.war())
        wl.extend(waits)
        self.cnt[sem] += self.step[sem]
        tok = (sem, self.cnt[sem])
        for i, f in enumerate(fns):
            self.code[eng].append((tuple(wl) if i == 0 else (), f,
                                   sem if i == len(fns) - 1 else None,
                                   self.step[sem]))
        for b in writes:
            b.w = tok
            b.rs = {}
        for b in reads:
            s, vv = tok
            if b.rs.get(s, 0) < vv:
                b.rs[s] = vv
        return tok

    def realize(self, block, sems):
        for eng in ENGS:
            code = self.code[eng]

            def emit(e, code=code):
                waited = {}
                for wl, fn, sem, stp in code:
                    for (s, v) in wl:
                        if waited.get(s, 0) < v:
                            e.wait_ge(sems[s], v)
                            waited[s] = v
                    r = fn(e)
                    if sem is not None:
                        r.then_inc(sems[sem], stp)
            getattr(block, eng)(emit)


class Arena:

    def __init__(self, t):
        self.t = t
        self.tenants = []

    def take(self, start, n, parts=1):
        end = start + n
        pend = {}
        for (s0, e0, bl) in self.tenants:
            if s0 < end and start < e0:
                for b in bl:
                    for (sm, vv) in b.war():
                        if pend.get(sm, 0) < vv:
                            pend[sm] = vv
        bufs = []
        for _ in range(parts):
            b = Buf()
            b.rs = dict(pend)
            bufs.append(b)
        self.tenants.append((start, end, bufs))
        if parts == 1:
            return start, bufs[0]
        return start, bufs


def build_program(debug=False):
    nc = bass.Bass("TRN2", target_bir_lowering=False)

    xT = nc.dram_tensor("xT", [D, T], F32, kind="ExternalInput").ap()
    wqaT = nc.dram_tensor("wqaT", [D, D], F32, kind="ExternalInput").ap()
    wkaT = nc.dram_tensor("wkaT", [D, D], F32, kind="ExternalInput").ap()
    wvaT = nc.dram_tensor("wvaT", [D, D], F32, kind="ExternalInput").ap()
    bqa = nc.dram_tensor("bqa", [128, 8], F32, kind="ExternalInput").ap()
    bka = nc.dram_tensor("bka", [128, 8], F32, kind="ExternalInput").ap()
    bvaf = nc.dram_tensor("bvaf", [1, D], F32, kind="ExternalInput").ap()
    wffn = nc.dram_tensor("wffn", [D, D], F32, kind="ExternalInput").ap()
    gffn = nc.dram_tensor("gffn", [128, 8], F32, kind="ExternalInput").ap()
    bffn = nc.dram_tensor("bffn", [128, 8], F32, kind="ExternalInput").ap()
    wqbT = nc.dram_tensor("wqbT", [D, D], F32, kind="ExternalInput").ap()
    wkbT = nc.dram_tensor("wkbT", [D, D], F32, kind="ExternalInput").ap()
    wvbT = nc.dram_tensor("wvbT", [D, D], F32, kind="ExternalInput").ap()
    wobT = nc.dram_tensor("wobT", [D, D], F32, kind="ExternalInput").ap()
    flagT = nc.dram_tensor("flagT", [128, 1], F32, kind="ExternalInput").ap()
    outT = nc.dram_tensor("outT", [D, T], F32, kind="ExternalOutput").ap()

    dQ = nc.dram_tensor("dQ", [D, T], F32)
    dK = nc.dram_tensor("dK", [D, T], F32)
    dV = nc.dram_tensor("dV", [T, NH * VW], F32)
    dQb = nc.dram_tensor("dQb", [D, T], F32)
    dKb = nc.dram_tensor("dKb", [D, T], F32)
    dVb = nc.dram_tensor("dVb", [T, NH * VW], F32)
    dWq = nc.dram_tensor("dWq", [D, D], BF16)
    dRow = nc.dram_tensor("dRow", [2, T], F32)
    dRow2 = nc.dram_tensor("dRow2", [2, T], F32)
    dD1 = nc.dram_tensor("dD1", [NH, T], F32)
    dDb = nc.dram_tensor("dDb", [NH, T], F32)
    dTmp = nc.dram_tensor("dTmp", [64, 32], F32)
    cwi = nc.dram_tensor("cwi", [8, 1], F32)
    cwo = nc.dram_tensor("cwo", [8, 1], F32)
    cli = nc.dram_tensor("cli", [2 * D, 64], F32)
    clo = nc.dram_tensor("clo", [4 * D, 64], F32)
    c2i = nc.dram_tensor("c2i", [1, 1], F32)
    c2o = nc.dram_tensor("c2o", [1, 1], F32)
    c3i = nc.dram_tensor("c3i", [NH * 128, VW], F32)
    c3o = nc.dram_tensor("c3o", [NH * 128, VW], F32)
    cgi = nc.dram_tensor("cgi", [1, 1], F32)
    cgo = nc.dram_tensor("cgo", [1, 1], F32)
    cg2i = nc.dram_tensor("cg2i", [1, 1], F32)
    cg2o = nc.dram_tensor("cg2o", [1, 1], F32)
    BND = D * 128 + 128 * NH * VW
    cbi = nc.dram_tensor("cbi", [1, BND], F32)
    cbo = nc.dram_tensor("cbo", [2, BND], F32)
    if debug:
        dbgQ = nc.dram_tensor("dbgQ", [D, T], F32, kind="ExternalOutput").ap()
        dbgK = nc.dram_tensor("dbgK", [D, T], F32, kind="ExternalOutput").ap()
        dbgV = nc.dram_tensor("dbgV", [T, NH * VW], F32, kind="ExternalOutput").ap()
        dbgLM = nc.dram_tensor("dbgLM", [2, 1024], F32, kind="ExternalOutput").ap()
        dbgP2 = nc.dram_tensor("dbgP2", [128, NH * 128], F32, kind="ExternalOutput").ap()
        dbgVI = nc.dram_tensor("dbgVI", [128, NH * 128], F32, kind="ExternalOutput").ap()
        dbgG0 = nc.dram_tensor("dbgG0", [128, NH * VW], F32, kind="ExternalOutput").ap()
        dbgGG = nc.dram_tensor("dbgGG", [128, NH * 64], F32, kind="ExternalOutput").ap()
        dbgD1 = nc.dram_tensor("dbgD1", [NH, T], F32, kind="ExternalOutput").ap()
        dbgX1 = nc.dram_tensor("dbgX1", [D, T], F32, kind="ExternalOutput").ap()
        dbgX2 = nc.dram_tensor("dbgX2", [D, T], F32, kind="ExternalOutput").ap()
        dbgX3 = nc.dram_tensor("dbgX3", [D, T], F32, kind="ExternalOutput").ap()
        dbgSC = nc.dram_tensor("dbgSC", [128, 16], F32, kind="ExternalOutput").ap()
        dbgQb = nc.dram_tensor("dbgQb", [D, T], F32, kind="ExternalOutput").ap()
        dbgKb = nc.dram_tensor("dbgKb", [D, T], F32, kind="ExternalOutput").ap()
        dbgVb = nc.dram_tensor("dbgVb", [T, NH * VW], F32, kind="ExternalOutput").ap()
        dbgDb = nc.dram_tensor("dbgDb", [NH, T], F32, kind="ExternalOutput").ap()
        dbgBND = nc.dram_tensor("dbgBND", [2, BND], F32, kind="ExternalOutput").ap()
        dbgG0p = nc.dram_tensor("dbgG0p", [128, NH * VW], F32, kind="ExternalOutput").ap()
        dbgV1 = nc.dram_tensor("dbgV1", [T, NH * VW], F32, kind="ExternalOutput").ap()
        dbgV2 = nc.dram_tensor("dbgV2", [T, NH * VW], F32, kind="ExternalOutput").ap()
        dbgIT = nc.dram_tensor("dbgIT", [6, 128, NH * 128], F32, kind="ExternalOutput").ap()
        dbgYT = nc.dram_tensor("dbgYT", [6, 128, NH * 128], F32, kind="ExternalOutput").ap()
        dbgE3 = nc.dram_tensor("dbgE3", [128, NH * 128], F32, kind="ExternalOutput").ap()


    A = lambda name, shp, dt=F32: nc.alloc_sbuf_tensor(name, shp, dt)
    X = [A(f"X{i}", [128, T]) for i in range(NKC)]
    TMP = A("TMP", [128, 512])
    BCM = A("BCM", [128, 512])
    BCR = A("BCR", [128, 512])
    IDN = A("IDN", [128, 128])
    IOI = A("IOI", [128, 128], I32)
    IOF = A("IOF", [128, 128])
    IOP = A("IOP", [128, 1], I32)
    IOPF = A("IOPF", [128, 1])
    MSKC = A("MSKC", [128, 128])
    MSKP = A("MSKP", [128, 128])
    MSKB = A("MSKB", [128, 128])
    FLG = A("FLG", [128, 1])
    SC1 = A("SC1", [128, 16])
    BB = A("BB", [128, 16])
    SS = A("SS", [128, 64])
    SS2 = A("SS2", [64, 128])
    ST = A("ST", [64, 128])
    SCR = A("SCR", [1, 128])
    BQA = A("BQA", [128, 8])
    BKA = A("BKA", [128, 8])
    GFF = A("GFF", [128, 8])
    BFF = A("BFF", [128, 8])
    ARN = A("ARN", [128, 33800])
    ar = Arena(ARN)

    ps = [nc.alloc_psum_tensor(f"ps{i}", [128, 512], F32) for i in range(8)]
    bps = [Buf() for _ in range(8)]

    kb = KB()
    op = kb.op

    bX = [Buf() for _ in range(NKC)]
    bTMP, bBCM, bBCR = Buf(), Buf(), Buf()
    bIDN, bIOI, bIOF, bIOP, bIOPF = Buf(), Buf(), Buf(), Buf(), Buf()
    bMSKC, bMSKP, bMSKB, bFLG = Buf(), Buf(), Buf(), Buf()
    bSC1, bBB, bSS, bSS2, bST, bSCR = Buf(), Buf(), Buf(), Buf(), Buf(), Buf()
    bBQA, bBKA, bGFF, bBFF = Buf(), Buf(), Buf(), Buf()
    bdQ, bdK, bdV, bdQb, bdKb, bdVb, bdWq = (Buf(), Buf(), Buf(), Buf(), Buf(),
                                             Buf(), Buf())
    bdRow, bdRow2, bdD1, bdDb, bdTmp = Buf(), Buf(), Buf(), Buf(), Buf()
    bcw, bcl, bc2, bc3, bcg, bcg2, bcb = (Buf(), Buf(), Buf(), Buf(), Buf(),
                                          Buf(), Buf())

    def AR(off, n):
        return ARN[:, off:off + n]

    op("sync", lambda e: e.dma_start(out=FLG[:, :], in_=flagT), writes=[bFLG], dma=True)
    op("gpsimd", lambda e: e.dma_start(out=cwi.ap(), in_=flagT[0:8, 0:1]),
       reads=[bFLG], writes=[bcw], dma=True)
    op("gpsimd", lambda e: e.collective_compute(
        "AllReduce", ALU.max, replica_groups=[list(range(8))],
        ins=[cwi.ap().opt()], outs=[cwo.ap().opt()]), reads=[bcw], writes=[bcw])
    for i in range(NKC):
        op("scalar", lambda e, i=i: e.dma_start(out=X[i][:, :], in_=xT[128 * i:128 * (i + 1), :]),
           writes=[bX[i]], dma=True)
    op("sync", lambda e: e.dma_start(out=BQA[:, :], in_=bqa), writes=[bBQA], dma=True)
    op("sync", lambda e: e.dma_start(out=BKA[:, :], in_=bka), writes=[bBKA], dma=True)
    op("sync", lambda e: e.dma_start(out=GFF[:, :], in_=gffn), writes=[bGFF], dma=True)
    op("sync", lambda e: e.dma_start(out=BFF[:, :], in_=bffn), writes=[bBFF], dma=True)
    op("gpsimd", lambda e: e.iota(IOI[:, :], [[1, 128]], base=0, channel_multiplier=0),
       writes=[bIOI])
    op("gpsimd", lambda e: e.iota(IOP[:, :], [[0, 1]], base=0, channel_multiplier=1,
                                  allow_small_or_imprecise_dtypes=True), writes=[bIOP])
    op("vector", lambda e: e.tensor_copy(IOF[:, :], IOI[:, :]), reads=[bIOI], writes=[bIOF])
    op("vector", lambda e: e.tensor_copy(IOPF[:, :], IOP[:, :]), reads=[bIOP], writes=[bIOPF])
    op("vector", lambda e: e.tensor_scalar(IDN[:, :], IOF[:, :], IOPF[:, 0:1], None,
                                           ALU.is_equal), reads=[bIOF, bIOPF], writes=[bIDN])
    op("vector", lambda e: e.tensor_scalar(MSKC[:, :], IOF[:, :], IOPF[:, 0:1], None,
                                           ALU.is_ge), reads=[bIOF, bIOPF], writes=[bMSKC])
    op("vector", lambda e: e.tensor_scalar(MSKP[:, :], IOF[:, :], IOPF[:, 0:1], None,
                                           ALU.is_lt), reads=[bIOF, bIOPF], writes=[bMSKP])
    op("vector", lambda e: e.tensor_scalar(MSKB[:, :], MSKP[:, :], FLG[:, 0:1], None,
                                           ALU.mult), reads=[bMSKP, bFLG], writes=[bMSKB])

    def wsplit_load(oW, bW, hbm, tmpo, btmp):
        for k in range(NKC):
            wv = AR(oW + 1024 * k, 1024).bitcast(BF16)
            tb = btmp[k % 2]
            op("gpsimd", lambda e, k=k, wv=wv, hbm=hbm: e.dma_start(
                out=wv[:, 0:1024], in_=hbm[128 * k:128 * (k + 1), :]),
               writes=[bW[k]], dma=True)
            op("sync", lambda e, k=k, tmpo=tmpo: e.dma_start(
                out=AR(tmpo + 1024 * (k % 2), 1024), in_=hbm[128 * k:128 * (k + 1), :]),
               reads=[], writes=[tb], dma=True)
            op("vector", lambda e, k=k, wv=wv, tmpo=tmpo: e.scalar_tensor_tensor(
                wv[:, 1024:2048], AR(tmpo + 1024 * (k % 2), 1024), 1.0, wv[:, 0:1024],
                ALU.mult, ALU.subtract), reads=[tb], writes=[bW[k]])

    def mm3fns(eng_ps, lh, ll, rh, rl, first, last):
        return [
            (lambda e, a_=a_, b_=b_, i2=i2: e.matmul(
                eng_ps, a_, b_, start=(first and i2 == 0),
                stop=(last and i2 == 2)))
            for i2, (a_, b_) in enumerate(((lh, rh), (lh, rl), (ll, rh)))]


    def ln_stats(tt, src, src_bufs, rowd, browd, kind):
        for i in range(NKC):
            op("scalar", lambda e, i=i: e.activation(TMP[:, :], src(i), AF.Square),
               reads=[src_bufs[i]], writes=[bTMP])
            o0 = 0 if i == 0 else 32
            op("vector", lambda e, i=i, o0=o0: e.tensor_reduce(
                SS[:, o0:o0 + 16], src(i).rearrange("p (b x) -> p b x", x=32),
                AX.X, ALU.add, apply_transpose=True), reads=[src_bufs[i]], writes=[bSS])
            op("vector", lambda e, o0=o0: e.tensor_reduce(
                SS[:, o0 + 16:o0 + 32], TMP[:, :].rearrange("p (b x) -> p b x", x=32),
                AX.X, ALU.add, apply_transpose=True), reads=[bTMP], writes=[bSS])
            if i > 0:
                op("vector", lambda e: e.tensor_tensor(SS[:, 0:32], SS[:, 0:32],
                                                       SS[:, 32:64], ALU.add),
                   reads=[bSS], writes=[bSS])
        op("tensor", lambda e: e.transpose(ps[7][0:16, 0:128], SS[:, 0:16], IDN[:, :]),
           reads=[bSS, bIDN], writes=[bps[7]])
        op("tensor", lambda e: e.transpose(ps[7][0:16, 128:256], SS[:, 16:32],
                                           IDN[:, :]), reads=[bSS, bIDN],
           writes=[bps[7]])
        op("vector", lambda e: e.tensor_copy(SS2[0:16, 0:32], ps[7][0:16, 0:32]),
           reads=[bps[7]], writes=[bSS2])
        op("vector", lambda e: e.tensor_copy(SS2[0:16, 32:64], ps[7][0:16, 128:160]),
           reads=[bps[7]], writes=[bSS2])
        for g in range(1, 4):
            op("vector", lambda e, g=g: e.tensor_tensor(
                SS2[0:16, 0:32], SS2[0:16, 0:32], ps[7][0:16, 32 * g:32 * (g + 1)],
                ALU.add), reads=[bps[7]], writes=[bSS2])
            op("vector", lambda e, g=g: e.tensor_tensor(
                SS2[0:16, 32:64], SS2[0:16, 32:64],
                ps[7][0:16, 128 + 32 * g:128 + 32 * (g + 1)],
                ALU.add), reads=[bps[7]], writes=[bSS2])
        op("vector", lambda e: e.tensor_scalar(ST[0:16, 0:32], SS2[0:16, 0:32],
                                               1.0 / D, None, ALU.mult),
           reads=[bSS2], writes=[bST])
        op("scalar", lambda e: e.activation(ST[0:16, 64:96], ST[0:16, 0:32], AF.Square),
           reads=[bST], writes=[bST])
        op("vector", lambda e: e.scalar_tensor_tensor(
            ST[0:16, 96:128], SS2[0:16, 32:64], 1.0 / D, ST[0:16, 64:96],
            ALU.mult, ALU.subtract), reads=[bSS2, bST], writes=[bST])
        if kind == "plain":
            op("vector", lambda e: e.tensor_scalar(ST[0:16, 64:96], ST[0:16, 96:128],
                                                   1.0, 1e-5, ALU.mult, ALU.add),
               reads=[bST], writes=[bST])
        else:
            op("vector", lambda e: e.tensor_scalar(ST[0:16, 64:96], ST[0:16, 96:128],
                                                   1.001, 1e-8, ALU.mult, ALU.add),
               reads=[bST], writes=[bST])
        op("scalar", lambda e: e.activation(ST[0:16, 96:128], ST[0:16, 64:96], AF.Sqrt),
           reads=[bST], writes=[bST])
        op("vector", lambda e: e.reciprocal(ST[0:16, 64:96], ST[0:16, 96:128]),
           reads=[bST], writes=[bST])
        rb = rowd.ap()
        op("gpsimd", lambda e: e.dma_start(
            out=rb[0:1, 512 * tt:512 * (tt + 1)].rearrange("o (b i) -> (o b) i", b=16),
            in_=ST[0:16, 0:32]), reads=[bST], writes=[browd], dma=True)
        op("gpsimd", lambda e: e.dma_start(
            out=rb[1:2, 512 * tt:512 * (tt + 1)].rearrange("o (b i) -> (o b) i", b=16),
            in_=ST[0:16, 64:96]), reads=[bST], writes=[browd], dma=True)
        op("gpsimd", lambda e: e.dma_start(
            out=BCM[:, :], in_=rb[0:1, 512 * tt:512 * (tt + 1)].broadcast_to([128, 512])),
           reads=[browd], writes=[bBCM], dma=True)
        op("gpsimd", lambda e: e.dma_start(
            out=BCR[:, :], in_=rb[1:2, 512 * tt:512 * (tt + 1)].broadcast_to([128, 512])),
           reads=[browd], writes=[bBCR], dma=True)

    def xn_chunk(dst, bdst, i, tt, affine):
        op("vector", lambda e, i=i: e.tensor_tensor(
            TMP[:, :], X[i][:, 512 * tt:512 * (tt + 1)], BCM[:, :], ALU.subtract),
           reads=[bX[i], bBCM], writes=[bTMP])
        if not affine:
            op("vector", lambda e: e.tensor_tensor(dst, TMP[:, :], BCR[:, :], ALU.mult),
               reads=[bTMP, bBCR], writes=[bdst])
        else:
            op("vector", lambda e: e.tensor_tensor(TMP[:, :], TMP[:, :], BCR[:, :],
                                                   ALU.mult), reads=[bTMP, bBCR],
               writes=[bTMP])
            op("vector", lambda e, i=i: e.tensor_scalar(
                dst, TMP[:, :], GFF[:, i:i + 1], BFF[:, i:i + 1], ALU.mult, ALU.add),
               reads=[bTMP, bGFF, bBFF], writes=[bdst])

    def xn_chunk_split(oslot, bdst, i, tt):
        xv = AR(oslot, 512).bitcast(BF16)
        op("vector", lambda e, i=i: e.tensor_tensor(
            TMP[:, :], X[i][:, 512 * tt:512 * (tt + 1)], BCM[:, :], ALU.subtract),
           reads=[bX[i], bBCM], writes=[bTMP])
        op("vector", lambda e: e.tensor_tensor(TMP[:, :], TMP[:, :], BCR[:, :],
                                               ALU.mult), reads=[bTMP, bBCR],
           writes=[bTMP])
        op("scalar", lambda e, xv=xv: e.activation(xv[:, 0:512], TMP[:, :], AF.Identity,
                                                   bias=0.0, scale=1.0),
           reads=[bTMP], writes=[bdst])
        op("vector", lambda e, xv=xv: e.scalar_tensor_tensor(
            xv[:, 512:1024], TMP[:, :], 1.0, xv[:, 0:512], ALU.mult, ALU.subtract),
           reads=[bTMP], writes=[bdst])

    def partition_max_to_scalar(colap, bcol, dsti, bdsti):
        op("tensor", lambda e: e.transpose(ps[7][0:1, 0:128], colap, IDN[:, :]),
           reads=[bcol, bIDN], writes=[bps[7]])
        op("vector", lambda e: e.reduce_max(SCR[0:1, 0:1], ps[7][0:1, 0:128], AX.X), reads=[bps[7]], writes=[bSCR])
        op("gpsimd", lambda e: e.dma_start(out=dsti.ap(), in_=SCR[0:1, 0:1]),
           reads=[bSCR], writes=[bdsti], dma=True)

    oWQ, bWQ = ar.take(0, 8192, NKC)
    oWK, bWK = ar.take(8192, 8192, NKC)
    oWV, bWV = ar.take(16384, 8192, NKC)
    oXN, bXN = ar.take(24576, 4096, NKC)
    oEV, bEV = ar.take(28672, 1024, 2)
    oVT, bVT = ar.take(29696, 1056)
    oQLM, bQLM = ar.take(30752, 512)
    oKLM, bKLM = ar.take(31264, 512)
    oBVB, bBVB = ar.take(31776, 1024)

    wsplit_load(oWQ, bWQ, wqaT, oXN, (bXN[0], bXN[2]))
    wsplit_load(oWK, bWK, wkaT, oXN, (bXN[0], bXN[2]))
    wsplit_load(oWV, bWV, wvaT, oXN, (bXN[0], bXN[2]))
    op("gpsimd", lambda e: e.dma_start(out=AR(oBVB, 1024),
                                       in_=bvaf.broadcast_to([128, D])),
       writes=[bBVB], dma=True)
    op("vector", lambda e: e.memset(AR(oVT, 1056), 1.0), writes=[bVT])

    for tt in range(NTT):
        ln_stats(tt, lambda i, tt=tt: X[i][:, 512 * tt:512 * (tt + 1)], bX,
                 dRow, bdRow, "plain")
        for i in range(NKC):
            xn_chunk_split(oXN + 512 * i, bXN[i], i, tt)
        for mi, (oW, bW, BI, bBI, dspill, bdspill, oLM, bLM) in enumerate((
                (oWQ, bWQ, BQA, bBQA, dQ, bdQ, oQLM, bQLM),
                (oWK, bWK, BKA, bBKA, dK, bdK, oKLM, bKLM))):
            for jc in range(NKC):
                p, bp = (0, bps[0]) if jc % 2 == 0 else (1, bps[1])
                ev, bev = (oEV + 512 * (jc % 2)), bEV[jc % 2]
                fns = []
                for kc in range(NKC):
                    wv = AR(oW + 1024 * kc, 1024).bitcast(BF16)
                    xv = AR(oXN + 512 * kc, 512).bitcast(BF16)
                    fns += mm3fns(ps[p][:, :],
                                  wv[:, 128 * jc:128 * (jc + 1)],
                                  wv[:, 1024 + 128 * jc:1024 + 128 * (jc + 1)],
                                  xv[:, 0:512], xv[:, 512:1024],
                                  kc == 0, kc == NKC - 1)
                kb.opchain("tensor", fns, reads=list(bW) + list(bXN), writes=[bp])
                if tt == 0 and jc == 0:
                    op("vector", lambda e, BI=BI, mi=mi: e.tensor_scalar(
                        BB[:, 8 * mi:8 * mi + 8], BI[:, :], QSCALE, None, ALU.mult),
                       reads=[bBI], writes=[bBB])
                op("scalar", lambda e, p=p, ev=ev, jc=jc, mi=mi: e.activation(
                    AR(ev, 512), ps[p][:, :], AF.Identity,
                    bias=BB[:, 8 * mi + jc:8 * mi + jc + 1], scale=QSCALE),
                   reads=[bp, bBB], writes=[bev])
                op("vector", lambda e, ev=ev, jc=jc, tt=tt, oLM=oLM: e.tensor_reduce(
                    AR(oLM + 64 * jc + 16 * tt, 16),
                    AR(ev, 512).rearrange("p (s g) -> p s g", g=32), AX.X, ALU.add),
                   reads=[bev], writes=[bLM])
                op("gpsimd", lambda e, ev=ev, jc=jc, tt=tt, dspill=dspill: e.dma_start(
                    out=dspill.ap()[128 * jc:128 * (jc + 1), 512 * tt:512 * (tt + 1)],
                    in_=AR(ev, 512)), reads=[bev], writes=[bdspill], dma=True)
        for lc4 in range(4):
            for vt2 in range(2):
                p, bp = (2, bps[2]) if vt2 == 0 else (3, bps[3])
                fns = []
                for kc in range(NKC):
                    wv = AR(oWV + 1024 * kc, 1024).bitcast(BF16)
                    xv = AR(oXN + 512 * kc, 512).bitcast(BF16)
                    fns += mm3fns(ps[p][:, :],
                                  xv[:, 128 * lc4:128 * (lc4 + 1)],
                                  xv[:, 512 + 128 * lc4:512 + 128 * (lc4 + 1)],
                                  wv[:, 512 * vt2:512 * (vt2 + 1)],
                                  wv[:, 1024 + 512 * vt2:1024 + 512 * (vt2 + 1)],
                                  kc == 0, kc == NKC - 1)
                kb.opchain("tensor", fns, reads=list(bXN) + list(bWV), writes=[bp])
                op("vector", lambda e, p=p, vt2=vt2: e.tensor_tensor(
                    AR(oVT + VW * 8 * vt2, VW * 8).rearrange("p (h w) -> p h w", w=VW)[:, :, 0:64],
                    ps[p][:, :].rearrange("p (h d) -> p h d", d=64),
                    AR(oBVB + 512 * vt2, 512).rearrange("p (h d) -> p h d", d=64),
                    ALU.add), reads=[bp, bBVB], writes=[bVT])
            op("gpsimd", lambda e, tt=tt, lc4=lc4: e.dma_start(
                out=dV.ap()[512 * tt + 128 * lc4:512 * tt + 128 * (lc4 + 1), :],
                in_=AR(oVT, 1056)), reads=[bVT], writes=[bdV], dma=True)

    if debug:
        op("gpsimd", lambda e: e.dma_start(out=dbgV1, in_=dV.ap()), reads=[bdV],
           writes=[Buf()], dma=True)

    for oLM, bLM in ((oQLM, bQLM), (oKLM, bKLM)):
        op("vector", lambda e, oLM=oLM: e.tensor_scalar(AR(oLM, 512), AR(oLM, 512),
                                                        1.0 / 32, None, ALU.mult),
           reads=[bLM], writes=[bLM])
    for i in range(NKC):
        op("gpsimd", lambda e, i=i: e.dma_start(
            out=cli.ap()[128 * i:128 * (i + 1), :], in_=AR(oQLM + 64 * i, 64)),
           reads=[bQLM], writes=[bcl], dma=True)
        op("gpsimd", lambda e, i=i: e.dma_start(
            out=cli.ap()[D + 128 * i:D + 128 * (i + 1), :], in_=AR(oKLM + 64 * i, 64)),
           reads=[bKLM], writes=[bcl], dma=True)
    op("gpsimd", lambda e: e.collective_compute(
        "AllGather", ALU.bypass, replica_groups=[[0, 1], [2, 3], [4, 5], [6, 7]],
        ins=[cli.ap().opt()], outs=[clo.ap().opt()]),
       reads=[bcl], writes=[bcl], waits=[bcw.w])

    oQF, bQF = ar.take(0, 1024)
    oKF, bKF = ar.take(1024, 1024)
    for i in range(NKC):
        op("gpsimd", lambda e, i=i: e.dma_start(
            out=AR(oQF + 128 * i, 64), in_=clo.ap()[128 * i:128 * (i + 1), :]),
           reads=[bcl], writes=[bQF], dma=True)
        op("gpsimd", lambda e, i=i: e.dma_start(
            out=AR(oQF + 128 * i + 64, 64),
            in_=clo.ap()[2 * D + 128 * i:2 * D + 128 * (i + 1), :]),
           reads=[bcl], writes=[bQF], dma=True)
        op("gpsimd", lambda e, i=i: e.dma_start(
            out=AR(oKF + 128 * i, 64), in_=clo.ap()[D + 128 * i:D + 128 * (i + 1), :]),
           reads=[bcl], writes=[bKF], dma=True)
        op("gpsimd", lambda e, i=i: e.dma_start(
            out=AR(oKF + 128 * i + 64, 64),
            in_=clo.ap()[3 * D + 128 * i:3 * D + 128 * (i + 1), :]),
           reads=[bcl], writes=[bKF], dma=True)

    if debug:
        op("gpsimd", lambda e: e.dma_start(out=dbgLM[0:1, :].rearrange("o (a b) -> (o a) b", b=128), in_=AR(oQF, 128)[0:8, :]), reads=[bQF], writes=[Buf()], dma=True)
        op("gpsimd", lambda e: e.dma_start(out=dbgLM[1:2, :].rearrange("o (a b) -> (o a) b", b=128), in_=AR(oKF, 128)[0:8, :]), reads=[bKF], writes=[Buf()], dma=True)

    if debug:
        op("gpsimd", lambda e: e.dma_start(out=dbgV2, in_=dV.ap()), reads=[bdV],
           writes=[Buf()], dma=True)

    def lmsl(o, h):
        po = 64 * (h % 2)
        return AR(o + 128 * (h // 2), 128)[po:po + 64, :]

    oP2, bP2 = ar.take(2048, 2048)
    oP2T, bP2T = ar.take(4096, 2048)
    oVI, bVI = ar.take(6144, 2048)
    oYI, bYI = ar.take(8192, 2048)
    oT0, bT0 = ar.take(10240, 2048)
    oT0T, bT0T = ar.take(12288, 2048)
    oT1, bT1 = ar.take(14336, 2048)
    oI4, bI4 = ar.take(16384, 512)
    for q in range(4):
        op("vector", lambda e, q=q: e.tensor_copy(AR(oI4 + 128 * q, 128), IDN[:, :]),
           reads=[bIDN], writes=[bI4])

    for h4 in range(4):
        p, bp = (0, bps[0]) if h4 % 2 == 0 else (1, bps[1])
        for hh in range(4):
            h = 4 * h4 + hh
            op("tensor", lambda e, h=h, hh=hh, p=p: e.matmul(
                ps[p][:, 128 * hh:128 * (hh + 1)], lmsl(oQF, h), lmsl(oKF, h),
                start=True, stop=True), reads=[bQF, bKF], writes=[bp])
        for hh in range(4):
            h = 4 * h4 + hh
            op("scalar", lambda e, p=p, hh=hh, h=h: e.activation(
                AR(oP2 + 128 * h, 128), ps[p][:, 128 * hh:128 * (hh + 1)], AF.Exp),
               reads=[bp], writes=[bP2])
        for hh in range(4):
            h = 4 * h4 + hh
            op("vector", lambda e, h=h: e.tensor_reduce(
                BB[:, 8:9], AR(oP2 + 128 * h, 128), AX.X, ALU.add),
               reads=[bP2], writes=[bBB])
            op("vector", lambda e: e.reciprocal(BB[:, 9:10], BB[:, 8:9]),
               reads=[bBB], writes=[bBB])
            op("vector", lambda e, h=h: e.tensor_scalar(
                AR(oP2 + 128 * h, 128), AR(oP2 + 128 * h, 128), BB[:, 9:10], None,
                ALU.mult), reads=[bBB], writes=[bP2])
    op("vector", lambda e: e.tensor_reduce(
        SS[:, 0:64], AR(oP2, 2048).rearrange("p (b x) -> p b x", x=32), AX.X, ALU.add,
        apply_transpose=True), reads=[bP2], writes=[bSS])
    op("tensor", lambda e: e.transpose(ps[7][0:64, 0:128], SS[:, 0:64], IDN[:, :]),
       reads=[bSS, bIDN], writes=[bps[7]])
    op("vector", lambda e: e.tensor_copy(SS2[0:64, 0:32], ps[7][0:64, 0:32]),
       reads=[bps[7]], writes=[bSS2])
    for g in range(1, 4):
        op("vector", lambda e, g=g: e.tensor_tensor(
            SS2[0:64, 0:32], SS2[0:64, 0:32], ps[7][0:64, 32 * g:32 * (g + 1)],
            ALU.add), reads=[bps[7]], writes=[bSS2])
    op("vector", lambda e: e.reduce_max(SS2[0:64, 64:65], SS2[0:64, 0:32], AX.X), reads=[bSS2], writes=[bSS2])
    op("tensor", lambda e: e.transpose(ps[7][0:1, 128:192], SS2[0:64, 64:65],
                                       IDN[0:64, 0:64]), reads=[bSS2, bIDN],
       writes=[bps[7]])
    op("vector", lambda e: e.reduce_max(SCR[0:1, 0:1], ps[7][0:1, 128:192], AX.X), reads=[bps[7]], writes=[bSCR])
    op("gpsimd", lambda e: e.dma_start(out=c2i.ap(), in_=SCR[0:1, 0:1]),
       reads=[bSCR], writes=[bc2], dma=True)
    op("gpsimd", lambda e: e.collective_compute(
        "AllReduce", ALU.max, replica_groups=[list(range(8))],
        ins=[c2i.ap().opt()], outs=[c2o.ap().opt()]),
       reads=[bc2], writes=[bc2])
    op("gpsimd", lambda e: e.dma_start(out=SC1[:, 0:1], in_=c2o.ap().broadcast_to([128, 1])),
       reads=[bc2], writes=[bSC1], dma=True)

    for h in range(NH):
        sl = h % 4
        op("tensor", lambda e, h=h, sl=sl: e.transpose(
            ps[7][:, 128 * sl:128 * (sl + 1)], AR(oP2 + 128 * h, 128), IDN[:, :]),
           reads=[bP2, bIDN], writes=[bps[7]])
        op("vector", lambda e, h=h, sl=sl: e.tensor_copy(
            AR(oP2T + 128 * h, 128), ps[7][:, 128 * sl:128 * (sl + 1)]),
           reads=[bps[7]], writes=[bP2T])
    op("vector", lambda e: e.reciprocal(SC1[:, 15:16], SC1[:, 0:1]),
       reads=[bSC1], writes=[bSC1])
    op("vector", lambda e: e.tensor_scalar(AR(oVI, 2048), AR(oP2T, 2048), SC1[:, 15:16],
                                           None, ALU.mult),
       reads=[bP2T, bSC1], writes=[bVI])
    op("vector", lambda e: e.tensor_scalar(AR(oYI, 2048), AR(oP2, 2048), SC1[:, 15:16],
                                           None, ALU.mult),
       reads=[bP2, bSC1], writes=[bYI])

    if debug:
        op("gpsimd", lambda e: e.dma_start(out=dbgP2, in_=AR(oP2, 2048)), reads=[bP2], writes=[Buf()], dma=True)

    def inv_iter():
        for h in range(NH):
            p, bp = (4, bps[4]) if (h // 4) % 2 == 0 else (5, bps[5])
            sl = h % 4
            op("tensor", lambda e, h=h, p=p, sl=sl: e.matmul(
                ps[p][:, 128 * sl:128 * (sl + 1)], AR(oP2T + 128 * h, 128),
                AR(oVI + 128 * h, 128), start=True, stop=True),
               reads=[bP2T, bVI], writes=[bp])
            if sl == 3:
                op("vector", lambda e, h=h, p=p: e.tensor_copy(
                    AR(oT0 + 128 * (h - 3), 512), ps[p][:, :]), reads=[bp], writes=[bT0])
        for h in range(NH):
            sl = h % 4
            op("tensor", lambda e, h=h, sl=sl: e.transpose(
                ps[7][:, 128 * sl:128 * (sl + 1)], AR(oT0 + 128 * h, 128), IDN[:, :]),
               reads=[bT0, bIDN], writes=[bps[7]])
            op("vector", lambda e, h=h, sl=sl: e.tensor_copy(
                AR(oT0T + 128 * h, 128), ps[7][:, 128 * sl:128 * (sl + 1)]),
               reads=[bps[7]], writes=[bT0T])
        for q in range(4):
            op("vector", lambda e, q=q: e.scalar_tensor_tensor(
                AR(oT1 + 512 * q, 512), AR(oI4, 512), 7.0, AR(oT0 + 512 * q, 512),
                ALU.mult, ALU.subtract), reads=[bI4, bT0], writes=[bT1])
        for h in range(NH):
            p, bp = (4, bps[4]) if (h // 4) % 2 == 0 else (5, bps[5])
            sl = h % 4
            op("tensor", lambda e, h=h, p=p, sl=sl: e.matmul(
                ps[p][:, 128 * sl:128 * (sl + 1)], AR(oT0T + 128 * h, 128),
                AR(oT1 + 128 * h, 128), start=True, stop=True),
               reads=[bT0T, bT1], writes=[bp])
            if sl == 3:
                op("vector", lambda e, h=h, p=p: e.scalar_tensor_tensor(
                    AR(oT1 + 128 * (h - 3), 512), AR(oI4, 512), 15.0, ps[p][:, :],
                    ALU.mult, ALU.subtract), reads=[bp, bI4], writes=[bT1])
        for h in range(NH):
            p, bp = (4, bps[4]) if (h // 4) % 2 == 0 else (5, bps[5])
            sl = h % 4
            op("tensor", lambda e, h=h, p=p, sl=sl: e.matmul(
                ps[p][:, 128 * sl:128 * (sl + 1)], AR(oT0T + 128 * h, 128),
                AR(oT1 + 128 * h, 128), start=True, stop=True),
               reads=[bT0T, bT1], writes=[bp])
            if sl == 3:
                op("vector", lambda e, h=h, p=p: e.scalar_tensor_tensor(
                    AR(oT1 + 128 * (h - 3), 512), AR(oI4, 512), 13.0, ps[p][:, :],
                    ALU.mult, ALU.subtract), reads=[bp, bI4], writes=[bT1])
        for h in range(NH):
            p, bp = (4, bps[4]) if (h // 4) % 2 == 0 else (5, bps[5])
            p2_, bp2 = (6, bps[6]) if (h // 4) % 2 == 0 else (6, bps[6])
            sl = h % 4
            op("tensor", lambda e, h=h, p=p, sl=sl: e.matmul(
                ps[p][:, 128 * sl:128 * (sl + 1)], AR(oYI + 128 * h, 128),
                AR(oT1 + 128 * h, 128), start=True, stop=True),
               reads=[bYI, bT1], writes=[bp])
            op("tensor", lambda e, h=h, p2_=p2_, sl=sl: e.matmul(
                ps[p2_][:, 128 * sl:128 * (sl + 1)], AR(oT1 + 128 * h, 128),
                AR(oYI + 128 * h, 128), start=True, stop=True),
               reads=[bYI, bT1], writes=[bp2])
            if sl == 3:
                op("vector", lambda e, h=h, p=p: e.tensor_scalar(
                    AR(oVI + 128 * (h - 3), 512), ps[p][:, :], 0.25, None, ALU.mult),
                   reads=[bp], writes=[bVI])
                op("vector", lambda e, h=h, p2_=p2_: e.tensor_scalar(
                    AR(oYI + 128 * (h - 3), 512), ps[p2_][:, :], 0.25, None, ALU.mult),
                   reads=[bp2], writes=[bYI])

    for it_ in range(6):
        inv_iter()
        if debug:
            op("gpsimd", lambda e, it_=it_: e.dma_start(out=dbgIT[it_], in_=AR(oVI, 2048)),
               reads=[bVI], writes=[Buf()], dma=True)
            op("gpsimd", lambda e, it_=it_: e.dma_start(out=dbgYT[it_], in_=AR(oYI, 2048)),
               reads=[bYI], writes=[Buf()], dma=True)

    if debug:
        op("gpsimd", lambda e: e.dma_start(out=dbgVI, in_=AR(oVI, 2048)), reads=[bVI], writes=[Buf()], dma=True)

    oE3, bE3 = ar.take(16896, 2048)
    oKC, bKC = ar.take(18944, 2048, 2)
    oVC, bVC = ar.take(20992, 2112, 2)
    oG0, bG0 = ar.take(23104, 1056)
    oGG, bGG = ar.take(24160, 1024)

    for lc in range(NLC):
        okc, bkc = oKC + 1024 * (lc % 2), bKC[lc % 2]
        ovc, bvc = oVC + 1056 * (lc % 2), bVC[lc % 2]
        for i in range(NKC):
            op("sync", lambda e, i=i, lc=lc, okc=okc: e.dma_start(
                out=AR(okc + 128 * i, 128),
                in_=dK.ap()[128 * i:128 * (i + 1), 128 * lc:128 * (lc + 1)]),
               reads=[bdK], writes=[bkc], dma=True)
        op("sync", lambda e, lc=lc, ovc=ovc: e.dma_start(
            out=AR(ovc, 1056), in_=dV.ap()[128 * lc:128 * (lc + 1), :]),
           reads=[bdV], writes=[bvc], dma=True)
        for h4 in range(4):
            p, bp = (2, bps[2]) if h4 % 2 == 0 else (3, bps[3])
            for hh in range(4):
                h = 4 * h4 + hh
                po = 64 * (h % 2)
                op("tensor", lambda e, h=h, hh=hh, po=po, p=p, okc=okc: e.matmul(
                    ps[p][:, 128 * hh:128 * (hh + 1)],
                    AR(okc + 128 * (h // 2), 128)[po:po + 64, :], lmsl(oQF, h),
                    start=True, stop=True), reads=[bkc, bQF], writes=[bp])
            op("scalar", lambda e, p=p, h4=h4: e.activation(
                AR(oE3 + 512 * h4, 512), ps[p][:, :], AF.Exp),
               reads=[bp], writes=[bE3])
        pb0 = 4 if lc % 2 == 0 else 0
        pmap = (pb0, pb0 + 1, 7 if lc % 2 else 6)
        for h in range(NH):
            p = pmap[0] if h < 7 else (pmap[1] if h < 14 else pmap[2])
            slot = h if h < 7 else (h - 7 if h < 14 else h - 14)
            op("tensor", lambda e, h=h, p=p, slot=slot, ovc=ovc: e.matmul(
                ps[p][:, VW * slot:VW * (slot + 1)], AR(oE3 + 128 * h, 128),
                AR(ovc + VW * h, VW), start=True, stop=True),
               reads=[bE3, bvc], writes=[bps[p]])
        for seg, p in ((0, pmap[0]), (1, pmap[1]), (2, pmap[2])):
            o0 = 462 * seg
            n = 462 if seg < 2 else 132
            if lc == 0:
                op("vector", lambda e, o0=o0, n=n, p=p: e.tensor_copy(
                    AR(oG0 + o0, n), ps[p][:, 0:n]), reads=[bps[p]], writes=[bG0])
            else:
                op("vector", lambda e, o0=o0, n=n, p=p: e.tensor_tensor(
                    AR(oG0 + o0, n), AR(oG0 + o0, n), ps[p][:, 0:n], ALU.add),
                   reads=[bps[p]], writes=[bG0])

    if debug:
        op("gpsimd", lambda e: e.dma_start(out=dbgG0p, in_=AR(oG0, 1056)),
           reads=[bG0], writes=[Buf()], dma=True)
        op("gpsimd", lambda e: e.dma_start(out=dbgE3, in_=AR(oE3, 2048)),
           reads=[bE3], writes=[Buf()], dma=True)
    for h in range(NH):
        op("gpsimd", lambda e, h=h: e.dma_start(
            out=c3i.ap()[128 * h:128 * (h + 1), :], in_=AR(oG0 + VW * h, VW)),
           reads=[bG0], writes=[bc3], dma=True)
    op("gpsimd", lambda e: e.collective_compute(
        "AllReduce", ALU.add, replica_groups=[[0, 1], [2, 3], [4, 5], [6, 7]],
        ins=[c3i.ap().opt()], outs=[c3o.ap().opt()]),
       reads=[bc3], writes=[bc3])
    for h in range(NH):
        op("gpsimd", lambda e, h=h: e.dma_start(
            out=AR(oG0 + VW * h, VW), in_=c3o.ap()[128 * h:128 * (h + 1), :]),
           reads=[bc3], writes=[bG0], dma=True)
    for h in range(NH):
        op("vector", lambda e, h=h: e.reciprocal(BB[:, 10:11], AR(oG0 + VW * h + 64, 1)),
           reads=[bG0], writes=[bBB])
        op("vector", lambda e, h=h: e.tensor_scalar(
            AR(oG0 + VW * h, 64), AR(oG0 + VW * h, 64), BB[:, 10:11], None, ALU.mult),
           reads=[bBB], writes=[bG0])
    for h in range(NH):
        p, bp = (4, bps[4]) if (h // 8) % 2 == 0 else (5, bps[5])
        sl = h % 8
        op("tensor", lambda e, h=h, p=p, sl=sl: e.matmul(
            ps[p][:, 64 * sl:64 * (sl + 1)], AR(oYI + 128 * h, 128),
            AR(oG0 + VW * h, 64), start=True, stop=True),
           reads=[bYI, bG0], writes=[bp])
        if sl == 7:
            op("vector", lambda e, h=h, p=p: e.tensor_copy(
                AR(oGG + 64 * (h - 7), 512), ps[p][:, :]), reads=[bp], writes=[bGG])

    if debug:
        op("gpsimd", lambda e: e.dma_start(out=dbgG0, in_=AR(oG0, 1056)), reads=[bG0], writes=[Buf()], dma=True)
        op("gpsimd", lambda e: e.dma_start(out=dbgGG, in_=AR(oGG, 1024)), reads=[bGG], writes=[Buf()], dma=True)

    oXU, bXU = ar.take(2048, 16384, NKC)
    oQC, bQC = ar.take(18944, 2048, 2)
    oE1, bE1 = ar.take(20992, 2048)
    oDB, bDB = ar.take(25184, 2048)
    oDS, bDS = ar.take(27232, 2048)
    oDR, bDR = ar.take(29280, 2048)

    for lc in range(NLC):
        oqc, bqc = oQC + 1024 * (lc % 2), bQC[lc % 2]
        for i in range(NKC):
            op("sync", lambda e, i=i, lc=lc, oqc=oqc: e.dma_start(
                out=AR(oqc + 128 * i, 128),
                in_=dQ.ap()[128 * i:128 * (i + 1), 128 * lc:128 * (lc + 1)]),
               reads=[bdQ], writes=[bqc], dma=True)
        for h4 in range(4):
            p, bp = (2, bps[2]) if h4 % 2 == 0 else (3, bps[3])
            for hh in range(4):
                h = 4 * h4 + hh
                po = 64 * (h % 2)
                op("tensor", lambda e, h=h, hh=hh, po=po, p=p, oqc=oqc: e.matmul(
                    ps[p][:, 128 * hh:128 * (hh + 1)], lmsl(oKF, h),
                    AR(oqc + 128 * (h // 2), 128)[po:po + 64, :],
                    start=True, stop=True), reads=[bqc, bKF], writes=[bp])
            op("scalar", lambda e, p=p, h4=h4: e.activation(
                AR(oE1 + 512 * h4, 512), ps[p][:, :], AF.Exp),
               reads=[bp], writes=[bE1])
        op("vector", lambda e: e.tensor_reduce(
            SS[:, 0:64], AR(oE1, 2048).rearrange("p (b x) -> p b x", x=32), AX.X,
            ALU.add, apply_transpose=True), reads=[bE1], writes=[bSS])
        op("tensor", lambda e: e.transpose(ps[7][0:64, 0:128], SS[:, 0:64], IDN[:, :]),
           reads=[bSS, bIDN], writes=[bps[7]])
        op("vector", lambda e: e.tensor_copy(ST[0:64, 0:32], ps[7][0:64, 0:32]),
           reads=[bps[7]], writes=[bST])
        for g in range(1, 4):
            op("vector", lambda e, g=g: e.tensor_tensor(
                ST[0:64, 0:32], ST[0:64, 0:32], ps[7][0:64, 32 * g:32 * (g + 1)],
                ALU.add), reads=[bps[7]], writes=[bST])
        op("gpsimd", lambda e: e.dma_start(out=dTmp.ap(), in_=ST[0:64, 0:32]),
           reads=[bST], writes=[bdTmp], dma=True)
        op("gpsimd", lambda e, lc=lc: e.dma_start(
            out=dD1.ap().rearrange("h (l b4 i) -> h l b4 i", b4=4, i=32)[:, lc, :, :],
            in_=dTmp.ap().rearrange("(h b4) i -> h b4 i", b4=4)),
           reads=[bdTmp], writes=[bdD1], dma=True)
        for jc in range(NKC):
            p, bp = (0, bps[0]) if jc % 2 == 0 else (1, bps[1])
            for par in range(2):
                h = 2 * jc + par
                op("tensor", lambda e, h=h, p=p, par=par: e.matmul(
                    ps[p][64 * par:64 * (par + 1), 0:128], AR(oGG + 64 * h, 64),
                    AR(oE1 + 128 * h, 128), start=True, stop=True),
                   reads=[bGG, bE1], writes=[bp])
            op("vector", lambda e, jc=jc, lc=lc, p=p: e.tensor_copy(
                AR(oXU + 2048 * jc + 128 * lc, 128), ps[p][:, 0:128]),
               reads=[bp], writes=[bXU[jc]])

    op("sync", lambda e: e.dma_start(out=AR(oDS, 2048)[0:16, :], in_=dD1.ap()),
       reads=[bdD1], writes=[bDS], dma=True)
    op("vector", lambda e: e.reciprocal(AR(oDR, 2048)[0:16, :], AR(oDS, 2048)[0:16, :]),
       reads=[bDS], writes=[bDR])
    op("gpsimd", lambda e: e.dma_start(out=dD1.ap(), in_=AR(oDR, 2048)[0:16, :]),
       reads=[bDR], writes=[bdD1], dma=True)
    for jc in range(NKC):
        op("gpsimd", lambda e, jc=jc: e.dma_start(
            out=AR(oDB, 2048)[0:64, :],
            in_=dD1.ap()[2 * jc:2 * jc + 1, :].broadcast_to([64, T])),
           reads=[bdD1], writes=[bDB], dma=True)
        op("gpsimd", lambda e, jc=jc: e.dma_start(
            out=AR(oDB, 2048)[64:128, :],
            in_=dD1.ap()[2 * jc + 1:2 * jc + 2, :].broadcast_to([64, T])),
           reads=[bdD1], writes=[bDB], dma=True)
        op("vector", lambda e, jc=jc: e.tensor_tensor(
            AR(oXU + 2048 * jc, 2048), AR(oXU + 2048 * jc, 2048), AR(oDB, 2048),
            ALU.mult), reads=[bDB], writes=[bXU[jc]])
        op("vector", lambda e, jc=jc: e.tensor_tensor(
            X[jc][:, :], X[jc][:, :], AR(oXU + 2048 * jc, 2048), ALU.add),
           reads=[bXU[jc]], writes=[bX[jc]])

    if debug:
        op("gpsimd", lambda e: e.dma_start(out=dbgD1, in_=dD1.ap()), reads=[bdD1], writes=[Buf()], dma=True)
        op("gpsimd", lambda e: e.dma_start(out=dbgQ, in_=dQ.ap()), reads=[bdQ], writes=[Buf()], dma=True)
        op("gpsimd", lambda e: e.dma_start(out=dbgK, in_=dK.ap()), reads=[bdK], writes=[Buf()], dma=True)
        op("gpsimd", lambda e: e.dma_start(out=dbgV, in_=dV.ap()), reads=[bdV], writes=[Buf()], dma=True)
        for jc in range(NKC):
            op("gpsimd", lambda e, jc=jc: e.dma_start(out=dbgX1[128 * jc:128 * (jc + 1), :], in_=X[jc][:, :]), reads=[bX[jc]], writes=[Buf()], dma=True)

    def ffn_sublayer2(first, rowd, browd, cgi_, cgo_, bcg_):
        oWF, bWF = ar.take(0, 8192, NKC)
        oQZ, bQZ = ar.take(8192, 8192, NKC)
        oXF, bXF = ar.take(16384, 16384, NKC)
        if first:
            for i in range(NKC):
                op("sync", lambda e, i=i: e.dma_start(
                    out=AR(oWF + 1024 * i, 1024), in_=wffn[128 * i:128 * (i + 1), :]),
                   writes=[bWF[i]], dma=True)
            for i in range(NKC):
                if i == 0:
                    op("vector", lambda e, i=i: e.tensor_reduce(
                        BB[:, 11:12], AR(oWF + 1024 * i, 1024), AX.X, ALU.add,
                        apply_absolute_value=True), reads=[bWF[i]], writes=[bBB])
                else:
                    op("vector", lambda e, i=i: e.tensor_reduce(
                        BB[:, 12:13], AR(oWF + 1024 * i, 1024), AX.X, ALU.add,
                        apply_absolute_value=True), reads=[bWF[i]], writes=[bBB])
                    op("vector", lambda e: e.tensor_tensor(
                        BB[:, 11:12], BB[:, 11:12], BB[:, 12:13], ALU.add),
                       reads=[bBB], writes=[bBB])
            op("tensor", lambda e: e.transpose(ps[7][0:1, 0:128], BB[:, 11:12],
                                               IDN[:, :]), reads=[bBB, bIDN],
               writes=[bps[7]])
            op("vector", lambda e: e.tensor_reduce(SCR[0:1, 1:2], ps[7][0:1, 0:128],
                                                   AX.X, ALU.add),
               reads=[bps[7]], writes=[bSCR])
            op("vector", lambda e: e.tensor_scalar(SCR[0:1, 2:3], SCR[0:1, 1:2],
                                                   1.0 / (D * D), None, ALU.mult),
               reads=[bSCR], writes=[bSCR])
            op("gpsimd", lambda e: e.dma_start(out=dTmp.ap()[0:1, 0:1],
                                               in_=SCR[0:1, 2:3]),
               reads=[bSCR], writes=[bdTmp], dma=True)
            op("sync", lambda e: e.dma_start(
                out=SC1[:, 1:2], in_=dTmp.ap()[0:1, 0:1].broadcast_to([128, 1])),
               reads=[bdTmp], writes=[bSC1], dma=True)
            op("vector", lambda e: e.tensor_scalar(SC1[:, 2:3], SC1[:, 1:2], 1.0, 1e-5,
                                                   ALU.mult, ALU.add),
               reads=[bSC1], writes=[bSC1])
            op("vector", lambda e: e.tensor_scalar(SC1[:, 3:4], SC1[:, 2:3], 0.5, None,
                                                   ALU.mult), reads=[bSC1], writes=[bSC1])
            op("vector", lambda e: e.tensor_scalar(SC1[:, 4:5], SC1[:, 3:4], -1.0, None,
                                                   ALU.mult), reads=[bSC1], writes=[bSC1])
            for i in range(NKC):
                op("vector", lambda e, i=i: e.tensor_scalar(
                    AR(oQZ + 1024 * i, 1024), AR(oWF + 1024 * i, 1024), SC1[:, 3:4],
                    None, ALU.is_gt), reads=[bWF[i], bSC1], writes=[bQZ[i]])
                op("vector", lambda e, i=i: e.tensor_scalar(
                    AR(oWF + 1024 * i, 1024), AR(oWF + 1024 * i, 1024), SC1[:, 4:5],
                    None, ALU.is_lt), reads=[bSC1], writes=[bWF[i]])
                qv = AR(oQZ + 1024 * i, 1024).bitcast(BF16)
                op("vector", lambda e, i=i, qv=qv: e.tensor_tensor(
                    qv[:, 0:1024], AR(oQZ + 1024 * i, 1024),
                    AR(oWF + 1024 * i, 1024), ALU.subtract),
                   reads=[bWF[i]], writes=[bQZ[i]])
                op("gpsimd", lambda e, i=i, qv=qv: e.dma_start(
                    out=dWq.ap()[128 * i:128 * (i + 1), :], in_=qv[:, 0:1024]),
                   reads=[bQZ[i]], writes=[bdWq], dma=True)
        else:
            for i in range(NKC):
                qv = AR(oQZ + 1024 * i, 1024).bitcast(BF16)
                op("sync", lambda e, i=i, qv=qv: e.dma_start(
                    out=qv[:, 0:1024], in_=dWq.ap()[128 * i:128 * (i + 1), :]),
                   reads=[bdWq], writes=[bQZ[i]], dma=True)
        oT2, bT2 = ar.take(32768, 512)
        oMX, bMX = ar.take(33280, 2)
        op("vector", lambda e: e.memset(BB[:, 13:14], 0.0), writes=[bBB])
        op("gpsimd", lambda e: e.memset(AR(oMX, 1), 0.0), writes=[bMX])
        for tt in range(NTT):
            ln_stats(tt, lambda i, tt=tt: X[i][:, 512 * tt:512 * (tt + 1)], bX,
                     rowd, browd, "ffn")
            for i in range(NKC):
                dst = AR(oXF + 2048 * i + 512 * tt, 512)
                if i % 2 == 0:
                    xn_chunk(dst, bXF[i], i, tt, True)
                    op("vector", lambda e, dst=dst: e.reduce_max(
                        BB[:, 14:15], dst, AX.X,
                        apply_absolute_value=True), reads=[bXF[i]], writes=[bBB])
                    op("vector", lambda e: e.tensor_tensor(BB[:, 13:14], BB[:, 13:14],
                                                           BB[:, 14:15], ALU.max),
                       reads=[bBB], writes=[bBB])
                else:
                    op("gpsimd", lambda e, i=i, tt=tt: e.tensor_tensor(
                        AR(oT2, 512), X[i][:, 512 * tt:512 * (tt + 1)], BCM[:, :],
                        ALU.subtract), reads=[bX[i], bBCM], writes=[bT2])
                    op("gpsimd", lambda e: e.tensor_tensor(
                        AR(oT2, 512), AR(oT2, 512), BCR[:, :], ALU.mult),
                       reads=[bT2, bBCR], writes=[bT2])
                    op("gpsimd", lambda e, i=i, dst=dst: e.tensor_scalar(
                        dst, AR(oT2, 512), GFF[:, i:i + 1], BFF[:, i:i + 1],
                        ALU.mult, ALU.add),
                       reads=[bT2, bGFF, bBFF], writes=[bXF[i]])
                    op("vector", lambda e, dst=dst: e.reduce_max(
                        BB[:, 14:15], dst, AX.X,
                        apply_absolute_value=True), reads=[bXF[i]], writes=[bBB])
                    op("vector", lambda e: e.tensor_tensor(
                        BB[:, 13:14], BB[:, 13:14], BB[:, 14:15], ALU.max),
                       reads=[bBB], writes=[bBB])
        partition_max_to_scalar(BB[:, 13:14], bBB, cgi_, bcg_)
        op("gpsimd", lambda e: e.collective_compute(
            "AllReduce", ALU.max, replica_groups=[list(range(8))],
            ins=[cgi_.ap().opt()], outs=[cgo_.ap().opt()]),
           reads=[bcg_], writes=[bcg_])
        op("gpsimd", lambda e: e.dma_start(out=SC1[:, 5:6],
                                           in_=cgo_.ap().broadcast_to([128, 1])),
           reads=[bcg_], writes=[bSC1], dma=True)
        op("vector", lambda e: e.reciprocal(SC1[:, 6:7], SC1[:, 5:6]),
           reads=[bSC1], writes=[bSC1])
        op("vector", lambda e: e.tensor_scalar(SC1[:, 6:7], SC1[:, 6:7], 128.0, None,
                                               ALU.mult), reads=[bSC1], writes=[bSC1])
        op("vector", lambda e: e.tensor_scalar(SC1[:, 7:8], SC1[:, 5:6], SC1[:, 1:2],
                                               None, ALU.mult), reads=[bSC1], writes=[bSC1])
        op("vector", lambda e: e.tensor_scalar(SC1[:, 7:8], SC1[:, 7:8], 1.0 / 128.0,
                                               None, ALU.mult), reads=[bSC1], writes=[bSC1])
        CL = 128.0 - 1e-5
        for tt in range(NTT):
            for i in range(NKC):
                xv = AR(oXF + 2048 * i + 512 * tt, 512).bitcast(BF16)
                op("vector", lambda e, i=i, tt=tt: e.tensor_scalar(
                    TMP[:, :], AR(oXF + 2048 * i + 512 * tt, 512), SC1[:, 6:7], None,
                    ALU.mult), reads=[bSC1, bXF[i]], writes=[bTMP])
                op("vector", lambda e: e.tensor_scalar(
                    TMP[:, :], TMP[:, :], CL, -CL, ALU.min, ALU.max),
                   reads=[], writes=[bTMP])
                op("scalar", lambda e, xv=xv: e.activation(
                    xv[:, 0:512], TMP[:, :], AF.Identity, bias=0.0, scale=1.0),
                   reads=[bTMP], writes=[bXF[i]])
                op("vector", lambda e, xv=xv: e.scalar_tensor_tensor(
                    xv[:, 512:1024], TMP[:, :], 1.0, xv[:, 0:512], ALU.mult,
                    ALU.subtract), reads=[bTMP], writes=[bXF[i]])
            for jc in range(NKC):
                p, bp = (0, bps[0]) if jc % 2 == 0 else (1, bps[1])
                fns = []
                for kc in range(NKC):
                    qv = AR(oQZ + 1024 * kc, 1024).bitcast(BF16)
                    xv = AR(oXF + 2048 * kc + 512 * tt, 512).bitcast(BF16)
                    for i2 in range(2):
                        fns.append(lambda e, kc=kc, jc=jc, p=p, qv=qv, xv=xv,
                                   i2=i2: e.matmul(
                            ps[p][:, :], qv[:, 128 * jc:128 * (jc + 1)],
                            xv[:, 512 * i2:512 * (i2 + 1)],
                            start=(kc == 0 and i2 == 0),
                            stop=(kc == NKC - 1 and i2 == 1)))
                kb.opchain("tensor", fns, reads=list(bQZ) + list(bXF), writes=[bp])
                op("vector", lambda e, jc=jc, tt=tt, p=p: e.scalar_tensor_tensor(
                    X[jc][:, 512 * tt:512 * (tt + 1)], ps[p][:, :], SC1[:, 7:8],
                    X[jc][:, 512 * tt:512 * (tt + 1)], ALU.mult, ALU.add),
                   reads=[bp, bSC1], writes=[bX[jc]])

    ffn_sublayer2(True, dRow2, bdRow2, cgi, cgo, bcg)

    if debug:
        for jc in range(NKC):
            op("gpsimd", lambda e, jc=jc: e.dma_start(out=dbgX2[128 * jc:128 * (jc + 1), :], in_=X[jc][:, :]), reads=[bX[jc]], writes=[Buf()], dma=True)
        op("gpsimd", lambda e: e.dma_start(out=dbgSC, in_=SC1[:, :]), reads=[bSC1], writes=[Buf()], dma=True)

    oWQb, bWQb = ar.take(0, 8192, NKC)
    oWKb, bWKb = ar.take(8192, 8192, NKC)
    oWVb, bWVb = ar.take(16384, 8192, NKC)
    oXN3, bXN3 = ar.take(24576, 4096, NKC)
    oEV3, bEV3 = ar.take(28672, 1024, 2)
    oVT3, bVT3 = ar.take(29696, 1056)
    wsplit_load(oWQb, bWQb, wqbT, oXN3, (bXN3[0], bXN3[2]))
    wsplit_load(oWKb, bWKb, wkbT, oXN3, (bXN3[0], bXN3[2]))
    wsplit_load(oWVb, bWVb, wvbT, oXN3, (bXN3[0], bXN3[2]))
    op("vector", lambda e: e.memset(AR(oVT3, 1056), 1.0), writes=[bVT3])
    for tt in range(NTT):
        ln_stats(tt, lambda i, tt=tt: X[i][:, 512 * tt:512 * (tt + 1)], bX,
                 dRow, bdRow, "plain")
        for i in range(NKC):
            xn_chunk_split(oXN3 + 512 * i, bXN3[i], i, tt)
        for (oW, bW, dspill, bdspill) in ((oWQb, bWQb, dQb, bdQb),
                                          (oWKb, bWKb, dKb, bdKb)):
            for jc in range(NKC):
                p, bp = (0, bps[0]) if jc % 2 == 0 else (1, bps[1])
                ev, bev = (oEV3 + 512 * (jc % 2)), bEV3[jc % 2]
                fns = []
                for kc in range(NKC):
                    wv = AR(oW + 1024 * kc, 1024).bitcast(BF16)
                    xv = AR(oXN3 + 512 * kc, 512).bitcast(BF16)
                    fns += mm3fns(ps[p][:, :],
                                  wv[:, 128 * jc:128 * (jc + 1)],
                                  wv[:, 1024 + 128 * jc:1024 + 128 * (jc + 1)],
                                  xv[:, 0:512], xv[:, 512:1024],
                                  kc == 0, kc == NKC - 1)
                kb.opchain("tensor", fns, reads=list(bW) + list(bXN3), writes=[bp])
                op("vector", lambda e, p=p, ev=ev: e.tensor_copy(AR(ev, 512), ps[p][:, :]),
                   reads=[bp], writes=[bev])
                op("gpsimd", lambda e, ev=ev, jc=jc, tt=tt, dspill=dspill: e.dma_start(
                    out=dspill.ap()[128 * jc:128 * (jc + 1), 512 * tt:512 * (tt + 1)],
                    in_=AR(ev, 512)), reads=[bev], writes=[bdspill], dma=True)
        for lc4 in range(4):
            for vt2 in range(2):
                p, bp = (2, bps[2]) if vt2 == 0 else (3, bps[3])
                fns = []
                for kc in range(NKC):
                    wv = AR(oWVb + 1024 * kc, 1024).bitcast(BF16)
                    xv = AR(oXN3 + 512 * kc, 512).bitcast(BF16)
                    fns += mm3fns(ps[p][:, :],
                                  xv[:, 128 * lc4:128 * (lc4 + 1)],
                                  xv[:, 512 + 128 * lc4:512 + 128 * (lc4 + 1)],
                                  wv[:, 512 * vt2:512 * (vt2 + 1)],
                                  wv[:, 1024 + 512 * vt2:1024 + 512 * (vt2 + 1)],
                                  kc == 0, kc == NKC - 1)
                kb.opchain("tensor", fns, reads=list(bXN3) + list(bWVb), writes=[bp])
                op("vector", lambda e, p=p, vt2=vt2: e.tensor_copy(
                    AR(oVT3 + VW * 8 * vt2, VW * 8).rearrange(
                        "p (h w) -> p h w", w=VW)[:, :, 0:64],
                    ps[p][:, :].rearrange("p (h d) -> p h d", d=64)),
                   reads=[bp], writes=[bVT3])
            op("gpsimd", lambda e, tt=tt, lc4=lc4: e.dma_start(
                out=dVb.ap()[512 * tt + 128 * lc4:512 * tt + 128 * (lc4 + 1), :],
                in_=AR(oVT3, 1056)), reads=[bVT3], writes=[bdVb], dma=True)

    op("gpsimd", lambda e: e.dma_start(
        out=cbi.ap()[0:1, 0:D * 128].rearrange("o (r c) -> (o r) c", c=128),
        in_=dKb.ap()[:, T - 128:T]), reads=[bdKb], writes=[bcb], dma=True)
    op("gpsimd", lambda e: e.dma_start(
        out=cbi.ap()[0:1, D * 128:BND].rearrange("o (r c) -> (o r) c", c=NH * VW),
        in_=dVb.ap()[T - 128:T, :]), reads=[bdVb], writes=[bcb], dma=True)
    op("gpsimd", lambda e: e.collective_compute(
        "AllGather", ALU.bypass, replica_groups=[[0, 1], [2, 3], [4, 5], [6, 7]],
        ins=[cbi.ap().opt()], outs=[cbo.ap().opt()]),
       reads=[bcb], writes=[bcb])

    oWO, bWO = ar.take(0, 8192, NKC)
    oQW, bQW = ar.take(8192, 4096, 2)
    oKCb, bKCb = ar.take(12288, 2048, 2)
    oVCb, bVCb = ar.take(14336, 2112, 2)
    oEC, bEC = ar.take(16448, 512)
    oEP, bEP = ar.take(16960, 2048, 4)
    oOU, bOU = ar.take(19008, 4096, NKC)
    oDSb, bDSb = ar.take(23104, 512)
    oDRb, bDRb = ar.take(23616, 512)
    oDBb, bDBb = ar.take(24128, 512)
    oKBD, bKBD = ar.take(24640, 1024)
    oVBD, bVBD = ar.take(25664, 1056)
    oM4, bM4 = ar.take(26720, 1536)
    oDT, bDT = ar.take(28256, 256, 4)

    for q in range(4):
        op("vector", lambda e, q=q: e.tensor_copy(AR(oM4 + 128 * q, 128), MSKC[:, :]),
           reads=[bMSKC], writes=[bM4])
        op("vector", lambda e, q=q: e.tensor_copy(AR(oM4 + 512 + 128 * q, 128),
                                                  MSKP[:, :]),
           reads=[bMSKP], writes=[bM4])
        op("vector", lambda e, q=q: e.tensor_copy(AR(oM4 + 1024 + 128 * q, 128),
                                                  MSKB[:, :]),
           reads=[bMSKB], writes=[bM4])
    wsplit_load(oWO, bWO, wobT, oQW, (bQW[0], bQW[0]))
    for i in range(NKC):
        op("gpsimd", lambda e, i=i: e.dma_start(
            out=AR(oKBD + 128 * i, 128),
            in_=cbo.ap()[0:1, 128 * 128 * i:128 * 128 * (i + 1)].rearrange(
                "o (r c) -> (o r) c", c=128)), reads=[bcb], writes=[bKBD], dma=True)
    op("gpsimd", lambda e: e.dma_start(
        out=AR(oVBD, 1056),
        in_=cbo.ap()[0:1, D * 128:BND].rearrange("o (r c) -> (o r) c", c=NH * VW)),
       reads=[bcb], writes=[bVBD], dma=True)

    def b2_iter2(kc):
        last = (kc == NLC - 1)
        first = (kc == -1)
        if first:
            okk, bkk = oKBD, bKBD
            ovv, bvv = oVBD, bVBD
        else:
            okk, bkk = oKCb + 1024 * (kc % 2), bKCb[kc % 2]
            ovv, bvv = oVCb + 1056 * (kc % 2), bVCb[kc % 2]
            for i in range(NKC):
                op("sync", lambda e, i=i, kc=kc, okk=okk: e.dma_start(
                    out=AR(okk + 128 * i, 128),
                    in_=dKb.ap()[128 * i:128 * (i + 1), 128 * kc:128 * (kc + 1)]),
                   reads=[bdKb], writes=[bkk], dma=True)
            op("sync", lambda e, kc=kc, ovv=ovv: e.dma_start(
                out=AR(ovv, 1056), in_=dVb.ap()[128 * kc:128 * (kc + 1), :]),
               reads=[bdVb], writes=[bvv], dma=True)
        oqw, bqw = oQW + 2048 * ((kc + 1) % 2), bQW[(kc + 1) % 2]
        q0 = max(kc, 0) * 128
        qn = 128 if (first or last) else 256
        qoff = 128 if first else 0
        for i in range(NKC):
            op("sync", lambda e, i=i, oqw=oqw, q0=q0, qn=qn, qoff=qoff: e.dma_start(
                out=AR(oqw + 256 * i + qoff, qn),
                in_=dQb.ap()[128 * i:128 * (i + 1), q0:q0 + qn]),
               reads=[bdQb], writes=[bqw], dma=True)
        for w in range(4):
            for hh in range(4):
                h = 4 * w + hh
                p = 0 if hh < 2 else 1
                so = 256 * (hh % 2)
                po = 64 * (h % 2)
                op("tensor", lambda e, h=h, p=p, so=so, po=po, okk=okk, oqw=oqw,
                   qoff=qoff, qn=qn: e.matmul(
                    ps[p][:, so + qoff:so + qoff + qn],
                    AR(okk + 128 * (h // 2), 128)[po:po + 64, :],
                    AR(oqw + 256 * (h // 2) + qoff, qn)[po:po + 64, :],
                    start=True, stop=True), reads=[bkk, bqw], writes=[bps[p]])
            for part in range(2):
                if (part == 0 and first) or (part == 1 and last):
                    continue
                oE = oEC if part == 0 else oEP + 512 * w
                bE = bEC if part == 0 else bEP[w]
                for hh in range(4):
                    p = 0 if hh < 2 else 1
                    so = 256 * (hh % 2) + 128 * part
                    op("scalar", lambda e, p=p, so=so, hh=hh, oE=oE: e.activation(
                        AR(oE + 128 * hh, 128), ps[p][:, so:so + 128], AF.Exp,
                        scale=0.125), reads=[bps[p]], writes=[bE])
                mk = oM4 if part == 0 else (oM4 + 1024 if first else oM4 + 512)
                op("vector", lambda e, oE=oE, mk=mk: e.tensor_tensor(
                    AR(oE, 512), AR(oE, 512), AR(mk, 512), ALU.mult),
                   reads=[bM4], writes=[bE])
            if not first:
                pb = 3 + (w // 2)
                co = 256 * (w % 2)
                for hh in range(4):
                    h = 4 * w + hh
                    op("tensor", lambda e, h=h, hh=hh, pb=pb, co=co, ovv=ovv: e.matmul(
                        ps[pb][64 * (h % 2):64 * (h % 2) + 64,
                               co + 128 * (hh // 2):co + 128 * (hh // 2) + 128],
                        AR(ovv + VW * h, 64), AR(oEC + 128 * hh, 128),
                        start=True, stop=True), reads=[bvv, bEC], writes=[bps[pb]])
                for pr in range(2):
                    jc = 2 * w + pr
                    op("vector", lambda e, jc=jc, pb=pb, co=co, pr=pr, kc=kc: e.tensor_tensor(
                        AR(oOU + 512 * jc + 128 * (kc % 4), 128),
                        AR(oOU + 512 * jc + 128 * (kc % 4), 128),
                        ps[pb][:, co + 128 * pr:co + 128 * (pr + 1)], ALU.add),
                       reads=[bps[pb]], writes=[bOU[jc]])
                op("vector", lambda e: e.tensor_reduce(
                    AR(oDT, 16), AR(oEC, 512).rearrange("p (b x) -> p b x", x=32),
                    AX.X, ALU.add, apply_transpose=True), reads=[bEC], writes=[bDT[0]])
                op("vector", lambda e, w=w: e.tensor_tensor(
                    AR(oDT, 16), AR(oDT, 16), AR(oDT + 64 + 16 * w, 16), ALU.add),
                   reads=[bDT[1]], writes=[bDT[0]])
                op("tensor", lambda e: e.transpose(ps[2][0:16, 0:128], AR(oDT, 16),
                                                   IDN[:, :]),
                   reads=[bDT[0], bIDN], writes=[bps[2]])
                op("vector", lambda e: e.tensor_copy(SS2[0:16, 0:32],
                                                     ps[2][0:16, 0:32]),
                   reads=[bps[2]], writes=[bSS2])
                for g in range(1, 4):
                    op("vector", lambda e, g=g: e.tensor_tensor(
                        SS2[0:16, 0:32], SS2[0:16, 0:32],
                        ps[2][0:16, 32 * g:32 * (g + 1)], ALU.add),
                       reads=[bps[2]], writes=[bSS2])
                op("gpsimd", lambda e: e.dma_start(out=dTmp.ap()[0:16, :],
                                                   in_=SS2[0:16, 0:32]),
                   reads=[bSS2], writes=[bdTmp], dma=True)
                op("gpsimd", lambda e, w=w, kc=kc: e.dma_start(
                    out=dDb.ap().rearrange("h (l b4 i) -> h l b4 i", b4=4, i=32)[
                        4 * w:4 * (w + 1), kc, :, :],
                    in_=dTmp.ap()[0:16, :].rearrange("(h b4) i -> h b4 i", b4=4)),
                   reads=[bdTmp], writes=[bdDb], dma=True)
        if (not first) and kc % 4 == 3:
            tt = kc // 4
            op("sync", lambda e, tt=tt: e.dma_start(
                out=AR(oDSb, 512)[0:16, :], in_=dDb.ap()[:, 512 * tt:512 * (tt + 1)]),
               reads=[bdDb], writes=[bDSb], dma=True)
            op("vector", lambda e: e.reciprocal(AR(oDRb, 512)[0:16, :],
                                                AR(oDSb, 512)[0:16, :]),
               reads=[bDSb], writes=[bDRb])
            op("gpsimd", lambda e, tt=tt: e.dma_start(
                out=dDb.ap()[:, 512 * tt:512 * (tt + 1)], in_=AR(oDRb, 512)[0:16, :]),
               reads=[bDRb], writes=[bdDb], dma=True)
            for jc in range(NKC):
                op("gpsimd", lambda e, jc=jc, tt=tt: e.dma_start(
                    out=AR(oDBb, 512)[0:64, :],
                    in_=dDb.ap()[2 * jc:2 * jc + 1, 512 * tt:512 * (tt + 1)].broadcast_to([64, 512])),
                   reads=[bdDb], writes=[bDBb], dma=True)
                op("gpsimd", lambda e, jc=jc, tt=tt: e.dma_start(
                    out=AR(oDBb, 512)[64:128, :],
                    in_=dDb.ap()[2 * jc + 1:2 * jc + 2, 512 * tt:512 * (tt + 1)].broadcast_to([64, 512])),
                   reads=[bdDb], writes=[bDBb], dma=True)
                op("vector", lambda e, jc=jc: e.tensor_tensor(
                    TMP[:, :], AR(oOU + 512 * jc, 512), AR(oDBb, 512),
                    ALU.mult), reads=[bDBb, bOU[jc]], writes=[bTMP])
                ov = AR(oOU + 512 * jc, 512).bitcast(BF16)
                op("scalar", lambda e, ov=ov: e.activation(
                    ov[:, 0:512], TMP[:, :], AF.Identity, bias=0.0, scale=1.0),
                   reads=[bTMP], writes=[bOU[jc]])
                op("vector", lambda e, ov=ov: e.scalar_tensor_tensor(
                    ov[:, 512:1024], TMP[:, :], 1.0, ov[:, 0:512], ALU.mult,
                    ALU.subtract), reads=[bTMP], writes=[bOU[jc]])
            for jc in range(NKC):
                p = 0 if jc % 2 == 0 else 1
                fns = []
                for kk in range(NKC):
                    wv = AR(oWO + 1024 * kk, 1024).bitcast(BF16)
                    ov = AR(oOU + 512 * kk, 512).bitcast(BF16)
                    fns += mm3fns(ps[p][:, :],
                                  wv[:, 128 * jc:128 * (jc + 1)],
                                  wv[:, 1024 + 128 * jc:1024 + 128 * (jc + 1)],
                                  ov[:, 0:512], ov[:, 512:1024],
                                  kk == 0, kk == NKC - 1)
                kb.opchain("tensor", fns, reads=list(bWO) + list(bOU), writes=[bps[p]])
                op("vector", lambda e, jc=jc, tt=tt, p=p: e.tensor_tensor(
                    X[jc][:, 512 * tt:512 * (tt + 1)], X[jc][:, 512 * tt:512 * (tt + 1)],
                    ps[p][:, :], ALU.add), reads=[bps[p]], writes=[bX[jc]])
        if not last:
            for w in range(4):
                pb = 5 + (w // 2)
                co = 256 * (w % 2)
                for hh in range(4):
                    h = 4 * w + hh
                    op("tensor", lambda e, h=h, hh=hh, pb=pb, co=co, ovv=ovv, w=w: e.matmul(
                        ps[pb][64 * (h % 2):64 * (h % 2) + 64,
                               co + 128 * (hh // 2):co + 128 * (hh // 2) + 128],
                        AR(ovv + VW * h, 64), AR(oEP + 512 * w + 128 * hh, 128),
                        start=True, stop=True), reads=[bvv, bEP[w]], writes=[bps[pb]])
                for pr in range(2):
                    jc = 2 * w + pr
                    op("vector", lambda e, jc=jc, pb=pb, co=co, pr=pr, kc=kc: e.tensor_copy(
                        AR(oOU + 512 * jc + 128 * ((kc + 1) % 4), 128),
                        ps[pb][:, co + 128 * pr:co + 128 * (pr + 1)]),
                       reads=[bps[pb]], writes=[bOU[jc]])
                op("vector", lambda e, w=w: e.tensor_reduce(
                    AR(oDT + 64 + 16 * w, 16),
                    AR(oEP + 512 * w, 512).rearrange("p (b x) -> p b x", x=32),
                    AX.X, ALU.add, apply_transpose=True),
                   reads=[bEP[w]], writes=[bDT[1]])

    for kc in range(-1, NLC):
        b2_iter2(kc)

    if debug:
        for jc in range(NKC):
            op("gpsimd", lambda e, jc=jc: e.dma_start(out=dbgX3[128 * jc:128 * (jc + 1), :], in_=X[jc][:, :]), reads=[bX[jc]], writes=[Buf()], dma=True)
        op("gpsimd", lambda e: e.dma_start(out=dbgQb, in_=dQb.ap()), reads=[bdQb], writes=[Buf()], dma=True)
        op("gpsimd", lambda e: e.dma_start(out=dbgKb, in_=dKb.ap()), reads=[bdKb], writes=[Buf()], dma=True)
        op("gpsimd", lambda e: e.dma_start(out=dbgVb, in_=dVb.ap()), reads=[bdVb], writes=[Buf()], dma=True)
        op("gpsimd", lambda e: e.dma_start(out=dbgDb, in_=dDb.ap()), reads=[bdDb], writes=[Buf()], dma=True)
        op("gpsimd", lambda e: e.dma_start(out=dbgBND, in_=cbo.ap()), reads=[bcb], writes=[Buf()], dma=True)

    ffn_sublayer2(False, dRow2, bdRow2, cg2i, cg2o, bcg2)

    otoks = []
    for jc in range(NKC):
        tok = op("sync", lambda e, jc=jc: e.dma_start(
            out=outT[128 * jc:128 * (jc + 1), :], in_=X[jc][:, :]),
            reads=[bX[jc]], dma=True)
        otoks.append(tok)
    op("vector", lambda e: e.memset(SCR[0:1, 64:65], 0.0), writes=[bSCR], waits=otoks)

    with (nc.semaphore("t") as st_, nc.semaphore("v") as sv, nc.semaphore("s") as ss_,
          nc.semaphore("g") as sg, nc.semaphore("ld") as sld, nc.semaphore("sd") as ssd,
          nc.semaphore("gd") as sgd, nc.Block() as block):
        sems = {"t": st_, "v": sv, "s": ss_, "g": sg, "ld": sld, "sd": ssd, "gd": sgd}
        kb.realize(block, sems)
    return nc


_NC_CACHE = None


def make_inmaps(**inputs):
    x = np.asarray(inputs["L0_attn_a_x"], dtype=np.float32)

    def t(w):
        return np.ascontiguousarray(np.asarray(w, dtype=np.float32).T)

    def pp(v):
        return np.ascontiguousarray(np.asarray(v, dtype=np.float32).reshape(8, 128).T)

    shared = {
        "wqaT": t(inputs["L0_attn_a_Wq"]), "wkaT": t(inputs["L0_attn_a_Wk"]),
        "wvaT": t(inputs["L0_attn_a_Wv"]),
        "bqa": pp(inputs["L0_attn_a_bq"]), "bka": pp(inputs["L0_attn_a_bk"]),
        "bvaf": np.ascontiguousarray(np.asarray(inputs["L0_attn_a_bv"], np.float32).reshape(1, D)),
        "wffn": np.ascontiguousarray(np.asarray(inputs["L0_ffn_weight"], np.float32)),
        "gffn": pp(inputs["L0_ffn_ln_gamma"]), "bffn": pp(inputs["L0_ffn_ln_beta"]),
        "wqbT": t(inputs["L0_attn_b_Wq"]), "wkbT": t(inputs["L0_attn_b_Wk"]),
        "wvbT": t(inputs["L0_attn_b_Wv"]), "wobT": t(inputs["L0_attn_b_Wo"]),
    }
    in_maps = []
    for c in range(8):
        b, hf = c // 2, c % 2
        m = dict(shared)
        m["xT"] = np.ascontiguousarray(x[b, 2048 * hf:2048 * (hf + 1), :].T)
        m["flagT"] = np.full((128, 1), float(hf), np.float32)
        in_maps.append(m)
    return in_maps


def kernel(L0_attn_a_x, L0_attn_a_Wq, L0_attn_a_bq, L0_attn_a_Wk, L0_attn_a_bk,
           L0_attn_a_Wv, L0_attn_a_bv, L0_attn_b_Wq, L0_attn_b_Wk, L0_attn_b_Wv,
           L0_attn_b_Wo, L0_ffn_weight, L0_ffn_ln_gamma, L0_ffn_ln_beta):
    global _NC_CACHE
    if _NC_CACHE is None:
        _NC_CACHE = build_program()
    nc = _NC_CACHE

    x = np.asarray(L0_attn_a_x, dtype=np.float32)

    def t(w):
        return np.ascontiguousarray(np.asarray(w, dtype=np.float32).T)

    def pp(v):
        return np.ascontiguousarray(np.asarray(v, dtype=np.float32).reshape(8, 128).T)

    shared = {
        "wqaT": t(L0_attn_a_Wq), "wkaT": t(L0_attn_a_Wk), "wvaT": t(L0_attn_a_Wv),
        "bqa": pp(L0_attn_a_bq), "bka": pp(L0_attn_a_bk),
        "bvaf": np.ascontiguousarray(np.asarray(L0_attn_a_bv, np.float32).reshape(1, D)),
        "wffn": np.ascontiguousarray(np.asarray(L0_ffn_weight, np.float32)),
        "gffn": pp(L0_ffn_ln_gamma), "bffn": pp(L0_ffn_ln_beta),
        "wqbT": t(L0_attn_b_Wq), "wkbT": t(L0_attn_b_Wk), "wvbT": t(L0_attn_b_Wv),
        "wobT": t(L0_attn_b_Wo),
    }
    in_maps = []
    for c in range(8):
        b, hf = c // 2, c % 2
        m = dict(shared)
        m["xT"] = np.ascontiguousarray(x[b, 2048 * hf:2048 * (hf + 1), :].T)
        m["flagT"] = np.full((128, 1), float(hf), np.float32)
        in_maps.append(m)
    res = run_bass_kernel_spmd(nc, in_maps, core_ids=list(range(8)))
    out = np.empty((4, 4096, 1024), np.float32)
    for c in range(8):
        b, hf = c // 2, c % 2
        out[b, 2048 * hf:2048 * (hf + 1), :] = res.results[c]["outT"].T
    return out
